# version 17
# baseline (speedup 1.0000x reference)
import math
from contextlib import ExitStack

import numpy as np
import ml_dtypes

import concourse.bass as bass
import concourse.mybir as mybir
from concourse.bass_utils import run_bass_kernel_spmd

F32 = mybir.dt.float32
BF16 = mybir.dt.bfloat16
F16 = mybir.dt.float16
I8 = mybir.dt.int8
I16 = mybir.dt.int16
I32 = mybir.dt.int32
AF = mybir.ActivationFunctionType
ALU = mybir.AluOpType
BF = ml_dtypes.bfloat16

D = 2048
H = 16
HD = 128
B = 4
S = 2048
NT = 1024
NC = 16
SCALE = 1.0 / math.sqrt(HD)
LN_EPS = 1e-5
RMS_EPS = 1e-6
I0 = 8192
I1 = 5632
NEG = -1e9


def ts(i, n):
    return slice(i * n, (i + 1) * n)


def d3(dram_ap):
    return dram_ap.rearrange("(c p) t -> p c t", p=128)


def s3(sbuf_ap, w):
    return sbuf_ap.rearrange("p (c t) -> p c t", t=w)



class P:
    SELF_WAIT_ALL = False
    ENG = ("sp", "pe", "act", "dve", "pool")
    BLOCK_ATTR = {"sp": "sync", "pe": "tensor", "act": "scalar",
                  "dve": "vector", "pool": "gpsimd"}

    NDMA_SLOTS = 6

    def __init__(self):
        self.q = {e: [] for e in self.ENG}
        self.cnt = {e: 0 for e in self.ENG}
        self.track = {}
        self.dma_idx = {}
        self.slot_uses = {}
        self.slot_sems = set()

    def add(self, eng, fn, deps=(), dma=False, self_wait=False):
        waits = {}

        def _w(t):
            if t is None:
                return
            e2, v = t
            if e2 != eng or self_wait or P.SELF_WAIT_ALL:
                waits[e2] = max(waits.get(e2, 0), v)

        for t in deps:
            if isinstance(t, list):
                for u in t:
                    _w(u)
            else:
                _w(t)
        if dma:
            i = self.dma_idx.get(eng, 0)
            self.dma_idx[eng] = i + 1
            sname = f"{eng}d{i % self.NDMA_SLOTS}"
            self.slot_sems.add(sname)
            prev = self.slot_uses.get(sname, 0)
            waits[sname] = max(waits.get(sname, 0), 16 * prev)
            self.slot_uses[sname] = prev + 1
            self.q[eng].append((waits, fn, 16, sname))
            return (sname, 16 * (prev + 1))
        self.cnt[eng] += 1
        self.q[eng].append((waits, fn, 1, eng))
        return (eng, self.cnt[eng])

    def frontier(self):
        toks = [(e, self.cnt[e]) for e in self.ENG if self.cnt[e] > 0]
        toks += [(s, 16 * self.slot_uses[s]) for s in sorted(self.slot_sems)]
        return toks

    def use(self, *keys):
        out = []
        for k in keys:
            out.extend(self.track.get(k, []))
        return out

    def set(self, key, *toks):
        self.track[key] = [t for t in toks if t is not None]

    def emit(self, nc, stack, final_deps):
        sems = {e: stack.enter_context(nc.semaphore(f"sem_{e}")) for e in self.ENG}
        for sname in sorted(self.slot_sems):
            sems[sname] = stack.enter_context(nc.semaphore(f"sem_{sname}"))
        fw = {}
        for t in final_deps:
            if t is None:
                continue
            e2, v = t
            fw[e2] = max(fw.get(e2, 0), v)
        block = stack.enter_context(nc.Block())
        for e in self.ENG:
            ops = self.q[e]
            if not ops and e != "pool":
                continue

            def body(eng, e=e, ops=ops):
                waited = {}
                for waits, fn, amt, tgt in ops:
                    for sname, v in waits.items():
                        if waited.get(sname, 0) < v:
                            eng.wait_ge(sems[sname], v)
                            waited[sname] = v
                    ins = fn(eng)
                    ins.then_inc(sems[tgt], amt)
                if e == "pool":
                    for sname, v in fw.items():
                        eng.wait_ge(sems[sname], v)

            getattr(block, self.BLOCK_ATTR[e])(body)



def mk_ln_bufs(sb, pst, with_xb=True):
    return {
        **({"ln_xb": sb("ln_xb", [128, NC * 512], BF16)} if with_xb else {}),
        "ln_sq": sb("ln_sq", [128, NC * 512], BF16),
        "ln_mrow": sb("ln_mrow", [1, 512], F32),
        "ln_vrow": sb("ln_vrow", [1, 512], F32),
        "ln_srow": sb("ln_srow", [1, 512], F32),
        "ln_rrow": sb("ln_rrow", [1, 512], F32),
        "ln_msq": sb("ln_msq", [1, 512], F32),
        "ln_tmp": sb("ln_tmp", [128, 512], F32),
        "ps_row0": pst("ps_row0", [128, 512]),
        "ps_row1": pst("ps_row1", [128, 512]),
        "ps_bcm": pst("ps_bcm", [128, 512]),
        "ps_bcr": pst("ps_bcr", [128, 512]),
    }


def emit_ln(p, bufs, x_s, xn_s, ntok, deps_x):
    xb, sq = bufs["ln_xb"], bufs["ln_sq"]
    mrow, vrow, srow, rrow, msq = (bufs["ln_mrow"], bufs["ln_vrow"],
                                   bufs["ln_srow"], bufs["ln_rrow"], bufs["ln_msq"])
    ps_r0, ps_r1 = bufs["ps_row0"], bufs["ps_row1"]
    ps_bcm, ps_bcr = bufs["ps_bcm"], bufs["ps_bcr"]
    ones_bf, ones_f32 = bufs["ones_bf"], bufs["ones_f32"]
    tmp = bufs["ln_tmp"]
    ngrp = ntok // 512
    toks = []
    for tg in range(ngrp):
        c0 = tg * 512

        def f_xb(e, c0=c0):
            for dc in range(NC):
                e.copy(out=xb[:, ts(dc, 512)], in_=x_s[:, dc * ntok + c0: dc * ntok + c0 + 512])
            for dc in range(NC):
                ins = e.activation(out=sq[:, ts(dc, 512)],
                                   in_=x_s[:, dc * ntok + c0: dc * ntok + c0 + 512],
                                   func=AF.Square)
            return ins

        t_xb = p.add("act", f_xb, deps=deps_x + p.use(tag + "ln_red"))

        def f_sum(e):
            for dc in range(NC):
                e.matmul(ps_r0[0:1, :], ones_bf[:, 0:1], xb[:, ts(dc, 512)],
                         start=(dc == 0), stop=(dc == NC - 1))
            for dc in range(NC):
                ins = e.matmul(ps_r1[0:1, :], ones_bf[:, 0:1], sq[:, ts(dc, 512)],
                               start=(dc == 0), stop=(dc == NC - 1))
            return ins

        t_red = p.add("pe", f_sum, deps=[t_xb] + p.use(tag + "ln_rows"))
        p.set(tag + "ln_red", t_red)

        def f_rows(e):
            e.tensor_scalar(out=mrow[:], in0=ps_r0[0:1, :], scalar1=1.0 / D,
                            scalar2=None, op0=ALU.mult)
            e.tensor_tensor(out=msq[:], in0=mrow[:], in1=mrow[:], op=ALU.mult)
            e.tensor_scalar(out=vrow[:], in0=ps_r1[0:1, :], scalar1=1.0 / D,
                            scalar2=LN_EPS, op0=ALU.mult, op1=ALU.add)
            return e.tensor_tensor(out=vrow[:], in0=vrow[:], in1=msq[:], op=ALU.subtract)

        t_rows = p.add("dve", f_rows, deps=[t_red])
        p.set(tag + "ln_rows", t_rows)
        t_sqrt = p.add("act", lambda e: e.activation(out=srow[:], in_=vrow[:], func=AF.Sqrt),
                       deps=[t_rows])
        t_rcp = p.add("dve", lambda e: e.reciprocal(out=rrow[:], in_=srow[:]), deps=[t_sqrt])

        def f_bc(e):
            e.matmul(ps_bcm[:], ones_f32[0:1, :], mrow[:], start=True, stop=True)
            return e.matmul(ps_bcr[:], ones_f32[0:1, :], rrow[:], start=True, stop=True)

        t_bc = p.add("pe", f_bc, deps=[t_rcp, t_rows] + p.use(tag + "ln_xn"))

        def f_xn(e, c0=c0):
            for dc in range(NC):
                sl = slice(dc * ntok + c0, dc * ntok + c0 + 512)
                e.tensor_tensor(out=tmp[:], in0=x_s[:, sl], in1=ps_bcm[:], op=ALU.subtract)
                ins = e.tensor_tensor(out=xn_s[:, sl], in0=tmp[:], in1=ps_bcr[:], op=ALU.mult)
            return ins

        t_xn = p.add("dve", f_xn, deps=[t_bc] + deps_x)
        p.set(tag + "ln_xn", t_xn)
        toks.append(t_xn)
    return toks



def emit_proj(p, wdram, n_out, src_s, ntok, src_toks, wb, wkeys, psbufs, pskeys,
              evac, extra_deps_w=()):
    ngrp = ntok // 512
    nog = n_out // 512
    ev_toks = []
    t_w = [None] * nog
    for og in range(nog):
        wi = og % 2

        def f_w(e, og=og, b=wb[wi]):
            return e.dma_start(out=b[:], in_=wdram[og])

        t_w[og] = p.add("pool", f_w, deps=p.use(wkeys[wi]) + list(extra_deps_w), dma=True)
        mm_toks = []
        for tg in range(ngrp):
            for oc4 in range(4):
                pi = (og * ngrp * 4 + tg * 4 + oc4) % len(psbufs)
                ps = psbufs[pi]
                orow = og * 512 + oc4 * 128

                def f_mm(e, tg=tg, oc4=oc4, ps=ps, b=wb[wi]):
                    for dc in range(NC):
                        ins = e.matmul(ps[:], b[:, dc * 512 + oc4 * 128: dc * 512 + (oc4 + 1) * 128],
                                       src_s[:, dc * ntok + tg * 512: dc * ntok + (tg + 1) * 512],
                                       start=(dc == 0), stop=(dc == NC - 1))
                    return ins

                t_mm = p.add("pe", f_mm, deps=[t_w[og]] + p.use(pskeys[pi]) + src_toks)
                mm_toks.append(t_mm)
                t_ev = evac(p, [t_mm], ps, pskeys[pi], orow, tg)
                p.set(pskeys[pi], t_ev)
                ev_toks.append(t_ev)
        p.set(wkeys[wi], mm_toks[-1])
    return ev_toks


def emit_proj_tm(p, wdram, n_out, o_base, src_s, ntok, src_toks, wb, wkeys,
                 psbufs, pskeys, evac, bias_rhs=None, ones16=None):
    ntt = ntok // 128
    nog = n_out // 512
    ev_toks = []
    for og in range(nog):
        wi = og % 2

        def f_w(e, og=og, b=wb[wi]):
            return e.dma_start(out=b[:], in_=wdram[o_base // 512 + og])

        t_w = p.add("pool", f_w, deps=p.use(wkeys[wi]), dma=True)
        mm_toks = []
        for tt in range(ntt):
            pi = (og * ntt + tt) % len(psbufs)
            ps = psbufs[pi]

            def f_mm(e, tt=tt, ps=ps, b=wb[wi], og=og):
                for dc in range(NC):
                    ins = e.matmul(ps[:], src_s[:, dc * ntok + tt * 128: dc * ntok + (tt + 1) * 128],
                                   b[:, ts(dc, 512)],
                                   start=(dc == 0), stop=(bias_rhs is None and dc == NC - 1))
                if bias_rhs is not None:
                    ins = e.matmul(ps[:], ones16[0:1, :], bias_rhs[0:1, ts(og, 512)],
                                   start=False, stop=True)
                return ins

            t_mm = p.add("pe", f_mm, deps=[t_w] + p.use(pskeys[pi]) + src_toks)
            mm_toks.append(t_mm)
            t_ev = evac(p, [t_mm], ps, pskeys[pi], og, tt)
            p.set(pskeys[pi], t_ev)
            ev_toks.append(t_ev)
        p.set(wkeys[wi], mm_toks[-1])
    return ev_toks


def mk_store_evac(p, evbufs, evkeys, route):
    state = {"i": 0}

    def evac(p_, deps, ps, pskey, orow, tg):
        i = state["i"] % 2
        state["i"] += 1
        ev = evbufs[i]
        emitfn, dst = route(orow)
        t_ev = p_.add("act", lambda e, ps=ps, ev=ev, orow=orow: emitfn(e, ps, ev, orow),
                      deps=deps + p_.use(evkeys[i]))

        def f_st(e, ev=ev, orow=orow, tg=tg):
            return e.dma_start(out=dst[(orow % D):(orow % D) + 128, ts(tg, 512)], in_=ev[:])

        t_st = p_.add("sp", f_st, deps=[t_ev], dma=True)
        p_.set(evkeys[i], t_st)
        return t_st

    return evac


def mk_store_evac_tm(p, evbufs, evkeys, dst):
    state = {"i": 0}

    def evac(p_, deps, ps, pskey, og, tt):
        i = state["i"] % 2
        state["i"] += 1
        ev = evbufs[i]
        t_ev = p_.add("act", lambda e, ps=ps, ev=ev: e.copy(out=ev[:], in_=ps[:, 0:512]),
                      deps=deps + p_.use(evkeys[i]))
        t_st = p_.add("sp", lambda e, ev=ev, og=og, tt=tt: e.dma_start(
            out=dst[ts(tt, 128), ts(og, 512)], in_=ev[:]), deps=[t_ev], dma=True)
        p_.set(evkeys[i], t_st)
        return t_st

    return evac



def build_t1():
    nc = bass.Bass("TRN2", target_bir_lowering=False)
    x_fm = nc.dram_tensor("x_fm", [D, NT], F32, kind="ExternalInput").ap()
    wqkvT = nc.dram_tensor("wqkvT", [D, 3 * D], F32, kind="ExternalInput").ap()
    bq_t = nc.dram_tensor("bq_t", [128, 48], F32, kind="ExternalInput").ap()
    bv_row = nc.dram_tensor("bv_row", [1, D], F32, kind="ExternalInput").ap()
    ones_bf_in = nc.dram_tensor("ones_bf", [128, 1], BF16, kind="ExternalInput").ap()
    ones_f32_in = nc.dram_tensor("ones_f32", [1, 128], F32, kind="ExternalInput").ap()
    ones_f16_in = nc.dram_tensor("ones_f16", [1, 128], F16, kind="ExternalInput").ap()

    xn0 = nc.dram_tensor("xn0", [D, NT], BF16, kind="ExternalOutput").ap()
    q0 = nc.dram_tensor("q0", [D, NT], BF16, kind="ExternalOutput").ap()
    k0 = nc.dram_tensor("k0", [D, NT], BF16, kind="ExternalOutput").ap()
    v0 = nc.dram_tensor("v0", [NT, D], BF16, kind="ExternalOutput").ap()

    with ExitStack() as stack:
        sb = lambda nm, shp, dt: stack.enter_context(nc.sbuf_tensor(nm, shp, dt))
        pst = lambda nm, shp, dt=F32: stack.enter_context(nc.psum_tensor(nm, shp, dt))
        bufs = mk_ln_bufs(sb, pst)
        bufs["ones_bf"] = sb("ones_bf_s", [128, 1], BF16)
        bufs["ones_f32"] = sb("ones_f32_s", [1, 128], F32)
        x_s = sb("x_s", [128, NC * NT], F32)
        xn_s = sb("xn_s", [128, NC * NT], BF16)
        wb = [sb("wb0", [128, NC * 512], BF16), sb("wb1", [128, NC * 512], BF16)]
        evp = [sb("evA", [128, 512], BF16), sb("evB", [128, 512], BF16)]
        evv = [sb("evVA", [128, 512], BF16), sb("evVB", [128, 512], BF16)]
        bt = sb("bt", [128, 48], F32)
        bscl = sb("bscl", [128, 16], F32)
        bv32 = sb("bv32", [1, D], F32)
        bv16 = sb("bv16", [1, D], F16)
        ones16 = sb("ones16", [1, 128], F16)
        psA = pst("psA", [128, 512])
        psB = pst("psB", [128, 512])
        psVA = pst("psVA", [128, 512])
        psVB = pst("psVB", [128, 512])

        p = P()
        tok_x = p.add("pool", lambda e: e.dma_start(
            out=s3(x_s[:], NT), in_=d3(x_fm)), dma=True)
        cdeps = []
        for dst, src in ((bufs["ones_bf"], ones_bf_in), (bufs["ones_f32"], ones_f32_in),
                         (ones16, ones_f16_in), (bt, bq_t), (bv32, bv_row)):
            cdeps.append(p.add("sp", lambda e, d=dst, s=src: e.dma_start(out=d[:], in_=s),
                               dma=True))
        t_bscl = p.add("dve", lambda e: e.tensor_scalar(
            out=bscl[:], in0=bt[:, 0:16], scalar1=SCALE, scalar2=None, op0=ALU.mult),
            deps=cdeps)
        t_bv16 = p.add("dve", lambda e: e.tensor_copy(out=bv16[:], in_=bv32[:]), deps=cdeps)

        xn_toks = emit_ln(p, bufs, x_s, xn_s, NT, [tok_x] + cdeps)
        st_xn = p.add("sp", lambda e: e.dma_start(
            out=d3(xn0), in_=s3(xn_s[:], NT)),
            deps=xn_toks, dma=True)

        def route_qk(orow):
            if orow < D:
                def f(e, ps, ev, orow):
                    return e.activation(out=ev[:], in_=ps[:], func=AF.Identity,
                                        bias=bscl[:, orow // 128: orow // 128 + 1],
                                        scale=SCALE)
                return f, q0
            def f(e, ps, ev, orow):
                return e.activation(out=ev[:], in_=ps[:], func=AF.Identity,
                                    bias=bt[:, orow // 128: orow // 128 + 1], scale=1.0)
            return f, k0

        ev_qk = mk_store_evac(p, evp, ["evA", "evB"], route_qk)
        st_qk = emit_proj(p, wqkvT, 2 * D, xn_s, NT, xn_toks + [t_bscl],
                          wb, ["wb0", "wb1"], [psA, psB], ["psA", "psB"], ev_qk)

        ev_v = mk_store_evac_tm(p, evv, ["evVA", "evVB"], v0)
        st_v = emit_proj_tm(p, wqkvT, D, 2 * D, xn_s, NT, xn_toks + [t_bv16],
                            wb, ["wb0", "wb1"], [psVA, psVB], ["psVA", "psVB"],
                            ev_v, bias_rhs=bv16, ones16=ones16)

        p.emit(nc, stack, [st_xn] + st_qk + st_v)
    return nc



def build_attn():
    nc = bass.Bass("TRN2", target_bir_lowering=False)
    q_in = nc.dram_tensor("q_in", [8 * HD, S], BF16, kind="ExternalInput").ap()
    k_in = nc.dram_tensor("k_in", [8 * HD, S], BF16, kind="ExternalInput").ap()
    v_in = nc.dram_tensor("v_in", [S, 8 * HD], BF16, kind="ExternalInput").ap()
    a_in = nc.dram_tensor("a_in", [1, 8 * S], F16, kind="ExternalInput").ap()
    w_in = nc.dram_tensor("w_in", [128, 8 * 16], F32, kind="ExternalInput").ap()
    c_in = nc.dram_tensor("c_in", [128, 8 * 16], F32, kind="ExternalInput").ap()
    mask_in = nc.dram_tensor("mask_in", [128, 128], F32, kind="ExternalInput").ap()
    ident_in = nc.dram_tensor("ident_in", [128, 128], BF16, kind="ExternalInput").ap()
    ones_f16_in = nc.dram_tensor("ones_f16", [1, 128], F16, kind="ExternalInput").ap()
    ctx_out = nc.dram_tensor("ctx_out", [S, 8 * HD], BF16, kind="ExternalOutput").ap()

    NKC = S // 128

    with ExitStack() as stack:
        sb = lambda nm, shp, dt: stack.enter_context(nc.sbuf_tensor(nm, shp, dt))
        pst = lambda nm, shp, dt=F32: stack.enter_context(nc.psum_tensor(nm, shp, dt))
        qh = [sb("qh0", [128, S], BF16), sb("qh1", [128, S], BF16)]
        kh = [sb("kh0", [128, S], BF16), sb("kh1", [128, S], BF16)]
        vh = [sb("vh0", [128, NKC * 129], BF16), sb("vh1", [128, NKC * 129], BF16)]
        p_s = sb("p_s", [128, S], BF16)
        pt_s = sb("pt_s", [128, S], BF16)
        a_s = sb("a_s", [1, 8 * S], F16)
        w_s = sb("w_s", [128, 8 * 16], F32)
        c_s = sb("c_s", [128, 8 * 16], F32)
        mask_s = sb("mask_s", [128, 128], F32)
        ident = sb("ident", [128, 128], BF16)
        ones16 = sb("ones16", [1, 128], F16)
        rcp = sb("rcp", [128, 1], F32)
        ctx_ev = [sb("ctxev0", [128, 128], BF16), sb("ctxev1", [128, 128], BF16)]
        ps_sc = pst("ps_sc", [128, S])
        ps_pt = pst("ps_pt", [128, 1024], BF16)
        ps_cx = [pst("ps_cx0", [128, 512]), pst("ps_cx1", [128, 512])]

        p = P()
        tc = []
        for dst, src in ((a_s, a_in), (w_s, w_in), (c_s, c_in), (mask_s, mask_in),
                         (ident, ident_in), (ones16, ones_f16_in)):
            tc.append(p.add("sp", lambda e, d=dst, s=src: e.dma_start(out=d[:], in_=s),
                            dma=True))
        t_ones_v = [p.add("dve", lambda e, b=b: e.memset(vh[b][:], 1.0)) for b in range(2)]

        for h in range(8):
            bsel = h % 2
            dep_war = p.use(f"head{h - 2}") + [t_ones_v[bsel]]
            t_lq = p.add("pool", lambda e, h=h, bsel=bsel: e.dma_start(
                out=qh[bsel][:], in_=q_in[ts(h, 128), :]), deps=dep_war, dma=True)
            t_lk = p.add("pool", lambda e, h=h, bsel=bsel: e.dma_start(
                out=kh[bsel][:], in_=k_in[ts(h, 128), :]), deps=dep_war, dma=True)
            t_lv = p.add("pool", lambda e, h=h, bsel=bsel: e.dma_start(
                out=vh[bsel][:].rearrange("p (c x) -> p c x", x=129)[:, :, 0:128],
                in_=v_in[:, ts(h, 128)].rearrange("(c p) v -> p c v", p=128)),
                deps=dep_war, dma=True)
            t_load = [t_lq, t_lk, t_lv]

            last_ctx_st = None
            for qt in range(16):
                klen = (qt + 1) * 128
                nch = (klen + 511) // 512

                def f_sc(e, h=h, qt=qt, klen=klen, nch=nch, bsel=bsel):
                    for c in range(nch):
                        w = min(512, klen - c * 512)
                        e.matmul(ps_sc[:, c * 512: c * 512 + w],
                                 qh[bsel][:, ts(qt, 128)],
                                 kh[bsel][:, c * 512: c * 512 + w],
                                 start=True, stop=False)
                        ins = e.matmul(ps_sc[:, c * 512: c * 512 + w],
                                       ones16[0:1, :],
                                       a_s[0:1, h * S + c * 512: h * S + c * 512 + w],
                                       start=False, stop=True)
                    return ins

                t_sc = p.add("pe", f_sc, deps=t_load + tc + p.use("exp"))
                t_mask = p.add("dve", lambda e, qt=qt: e.tensor_tensor(
                    out=ps_sc[:, ts(qt, 128)], in0=ps_sc[:, ts(qt, 128)],
                    in1=mask_s[:], op=ALU.add), deps=[t_sc])

                t_exp = p.add("act", lambda e, h=h, qt=qt, klen=klen: e.activation(
                    out=p_s[:, 0:klen], in_=ps_sc[:, 0:klen], func=AF.Exp,
                    bias=c_s[:, h * 16 + qt: h * 16 + qt + 1], scale=1.0),
                    deps=[t_mask] + p.use("ptev"))
                p.set("exp", t_exp)

                nkc = qt + 1
                t_ptev_list = []
                for kc4 in range((nkc + 3) // 4):
                    def f_tr(e, kc4=kc4, nkc=nkc):
                        for j in range(4):
                            kc = kc4 * 4 + j
                            if kc >= nkc:
                                break
                            ins = e.transpose(ps_pt[:, ts(j, 128)], p_s[:, ts(kc, 128)],
                                              ident[:])
                        return ins

                    t_tr = p.add("pe", f_tr, deps=[t_exp] + p.use("pt_ps"))

                    def f_ptev(e, kc4=kc4, nkc=nkc, h=h):
                        for j in range(4):
                            kc = kc4 * 4 + j
                            if kc >= nkc:
                                break
                            ins = e.tensor_scalar(
                                out=pt_s[:, ts(kc, 128)], in0=ps_pt[:, ts(j, 128)],
                                scalar1=w_s[:, h * 16 + kc: h * 16 + kc + 1],
                                scalar2=None, op0=ALU.mult)
                        return ins

                    t_ptev = p.add("dve", f_ptev, deps=[t_tr])
                    p.set("pt_ps", t_ptev)
                    t_ptev_list.append(t_ptev)
                p.set("ptev", t_ptev_list[-1])

                pcx = ps_cx[qt % 2]
                cxkey = f"cx{qt % 2}"

                def f_cx(e, nkc=nkc, bsel=bsel, pcx=pcx):
                    for kc in range(nkc):
                        ins = e.matmul(pcx[:, 0:129], pt_s[:, ts(kc, 128)],
                                       vh[bsel][:, kc * 129:(kc + 1) * 129],
                                       start=(kc == 0), stop=(kc == nkc - 1))
                    return ins

                t_cx = p.add("pe", f_cx, deps=t_ptev_list + t_load + p.use(cxkey))
                t_rcp = p.add("dve", lambda e, pcx=pcx: e.reciprocal(
                    out=rcp[:], in_=pcx[:, 128:129]), deps=[t_cx])
                cev = ctx_ev[qt % 2]
                t_cev = p.add("dve", lambda e, pcx=pcx, cev=cev: e.tensor_scalar(
                    out=cev[:], in0=pcx[:, 0:128], scalar1=rcp[:], scalar2=None,
                    op0=ALU.mult), deps=[t_rcp] + p.use(f"cev{qt % 2}"), self_wait=True)
                p.set(cxkey, t_cev)
                last_ctx_st = p.add("sp", lambda e, cev=cev, qt=qt, h=h: e.dma_start(
                    out=ctx_out[ts(qt, 128), ts(h, 128)], in_=cev[:]), deps=[t_cev], dma=True)
                p.set(f"cev{qt % 2}", last_ctx_st)
            p.set(f"head{h}", last_ctx_st)

        p.emit(nc, stack, p.use("head6") + p.use("head7"))
    return nc



def build_t3():
    nc = bass.Bass("TRN2", target_bir_lowering=False)
    x_fm = nc.dram_tensor("x_fm", [D, NT], F32, kind="ExternalInput").ap()
    ctx_fm = nc.dram_tensor("ctx_fm", [D, NT], BF16, kind="ExternalInput").ap()
    xn0_fm = nc.dram_tensor("xn0_fm", [D, NT], BF16, kind="ExternalInput").ap()
    woutT = nc.dram_tensor("woutT", [D, D], F32, kind="ExternalInput").ap()
    bout_t = nc.dram_tensor("bout_t", [128, 16], F32, kind="ExternalInput").ap()
    wg8 = nc.dram_tensor("wg8", [D, I0], I8, kind="ExternalInput").ap()
    wu8 = nc.dram_tensor("wu8", [D, I0], I8, kind="ExternalInput").ap()
    wd8 = nc.dram_tensor("wd8", [I0, D], I8, kind="ExternalInput").ap()
    sg_t = nc.dram_tensor("sg_t", [128, 64], F32, kind="ExternalInput").ap()
    su_t = nc.dram_tensor("su_t", [128, 64], F32, kind="ExternalInput").ap()
    sd_t = nc.dram_tensor("sd_t", [128, 16], F32, kind="ExternalInput").ap()
    wq1 = nc.dram_tensor("wq1", [D, D], F32, kind="ExternalInput").ap()
    wk1 = nc.dram_tensor("wk1", [D, D], F32, kind="ExternalInput").ap()
    wv1 = nc.dram_tensor("wv1", [D, D], F32, kind="ExternalInput").ap()
    ones_bf_in = nc.dram_tensor("ones_bf", [128, 1], BF16, kind="ExternalInput").ap()
    ones_f32_in = nc.dram_tensor("ones_f32", [1, 128], F32, kind="ExternalInput").ap()

    x1_fm = nc.dram_tensor("x1_fm", [D, NT], F32, kind="ExternalOutput").ap()
    xn1 = nc.dram_tensor("xn1", [D, NT], BF16, kind="ExternalOutput").ap()
    q1 = nc.dram_tensor("q1", [D, NT], BF16, kind="ExternalOutput").ap()
    k1 = nc.dram_tensor("k1", [D, NT], BF16, kind="ExternalOutput").ap()
    v1 = nc.dram_tensor("v1", [NT, D], BF16, kind="ExternalOutput").ap()

    with ExitStack() as stack:
        sb = lambda nm, shp, dt: stack.enter_context(nc.sbuf_tensor(nm, shp, dt))
        pst = lambda nm, shp, dt=F32: stack.enter_context(nc.psum_tensor(nm, shp, dt))
        x_s = sb("x_s", [128, NC * NT], F32)
        wb = [sb("wb0", [128, NC * 512], BF16), sb("wb1", [128, NC * 512], BF16)]
        bo_s = sb("bo_s", [128, 16], F32)
        sg_s = sb("sg_s", [128, 64], F32)
        su_s = sb("su_s", [128, 64], F32)
        sd_s = sb("sd_s", [128, 16], F32)
        ones_bf = sb("ones_bf_s", [128, 1], BF16)
        ones_f32 = sb("ones_f32_s", [1, 128], F32)
        psA = pst("psA", [128, 512])
        psB = pst("psB", [128, 512])
        psG = pst("psG", [128, 512])
        psU = pst("psU", [128, 512])

        p = P()
        tok_x = p.add("pool", lambda e: e.dma_start(
            out=s3(x_s[:], NT), in_=d3(x_fm)), dma=True)
        cdeps = []
        for dst, src in ((bo_s, bout_t), (sg_s, sg_t), (su_s, su_t), (sd_s, sd_t),
                         (ones_bf, ones_bf_in), (ones_f32, ones_f32_in)):
            cdeps.append(p.add("sp", lambda e, d=dst, s=src: e.dma_start(out=d[:], in_=s),
                               dma=True))

        with nc.sbuf_tensor("ctx_s", [128, NC * NT], BF16) as ctx_s:
            tok_ctx = p.add("pool", lambda e: e.dma_start(
                out=s3(ctx_s[:], NT), in_=d3(ctx_fm)), dma=True)

            def ev_oproj(p_, deps, ps, pskey, orow, tg):
                oc = orow // 128
                sl = slice(oc * NT + tg * 512, oc * NT + (tg + 1) * 512)
                return p_.add("dve", lambda e, ps=ps, sl=sl, oc=oc: e.scalar_tensor_tensor(
                    out=x_s[:, sl], in0=ps[:], scalar=bo_s[:, oc:oc + 1], in1=x_s[:, sl],
                    op0=ALU.add, op1=ALU.add), deps=deps + [tok_x] + cdeps)

            oproj_toks = emit_proj(p, woutT, D, ctx_s, NT, [tok_ctx],
                                   wb, ["wb0", "wb1"], [psA, psB], ["psA", "psB"], ev_oproj)
        p.set("oproj", oproj_toks[-1])

        ffn_toks = []
        with nc.sbuf_tensor("xn0_s", [128, NC * NT], BF16) as xn0_s, \
             nc.sbuf_tensor("hf", [128, 64 * 512], BF16) as hf, \
             nc.sbuf_tensor("g_s", [128, 512], BF16) as g_s, \
             nc.sbuf_tensor("u_s", [128, 512], BF16) as u_s:
            tok_xn0 = p.add("pool", lambda e: e.dma_start(
                out=s3(xn0_s[:], NT), in_=d3(xn0_fm)), deps=p.use("oproj"), dma=True)
            for tb in range(2):
                fuse_toks = []
                for icg in range(16):
                    t_wg = p.add("pool", lambda e, icg=icg: e.dma_start(
                        out=s3(wb[0][:], 512), in_=d3(wg8[:, ts(icg, 512)])),
                        deps=p.use("wb0", "oproj"), dma=True)
                    t_wu = p.add("pool", lambda e, icg=icg: e.dma_start(
                        out=s3(wb[1][:], 512), in_=d3(wu8[:, ts(icg, 512)])),
                        deps=p.use("wb1", "oproj"), dma=True)
                    mmg = []
                    mmu = []
                    for j4 in range(4):
                        ic = icg * 4 + j4

                        def f_g(e, j4=j4, tb=tb):
                            for dc in range(NC):
                                ins = e.matmul(psG[:], wb[0][:, dc * 512 + j4 * 128: dc * 512 + (j4 + 1) * 128],
                                               xn0_s[:, dc * NT + tb * 512: dc * NT + (tb + 1) * 512],
                                               start=(dc == 0), stop=(dc == NC - 1))
                            return ins

                        def f_u(e, j4=j4, tb=tb):
                            for dc in range(NC):
                                ins = e.matmul(psU[:], wb[1][:, dc * 512 + j4 * 128: dc * 512 + (j4 + 1) * 128],
                                               xn0_s[:, dc * NT + tb * 512: dc * NT + (tb + 1) * 512],
                                               start=(dc == 0), stop=(dc == NC - 1))
                            return ins

                        t_g = p.add("pe", f_g, deps=[t_wg, tok_xn0] + p.use("psG"))
                        t_u = p.add("pe", f_u, deps=[t_wu, tok_xn0] + p.use("psU"))
                        mmg.append(t_g)
                        mmu.append(t_u)
                        t_sil = p.add("act", lambda e, ic=ic: e.activation(
                            out=g_s[:], in_=psG[:], func=AF.Silu, bias=0.0,
                            scale=sg_s[:, ic:ic + 1]), deps=[t_g] + p.use("gsbuf"))
                        t_ue = p.add("dve", lambda e, ic=ic: e.tensor_scalar(
                            out=u_s[:], in0=psU[:], scalar1=su_s[:, ic:ic + 1],
                            scalar2=None, op0=ALU.mult), deps=[t_u] + p.use("usbuf"))
                        p.set("psG", t_sil)
                        p.set("psU", t_ue)
                        t_f = p.add("dve", lambda e, ic=ic: e.tensor_tensor(
                            out=hf[:, ts(ic, 512)], in0=g_s[:], in1=u_s[:], op=ALU.mult),
                            deps=[t_sil, t_ue] + p.use("hfrd"))
                        p.set("gsbuf", t_f)
                        p.set("usbuf", t_f)
                        fuse_toks.append(t_f)
                    p.set("wb0", mmg[-1])
                    p.set("wb1", mmu[-1])
                dn_toks = []
                for oc in range(16):
                    wbi = oc % 2
                    t_wd = p.add("pool", lambda e, oc=oc, wbi=wbi: e.dma_start(
                        out=s3(wb[wbi][:, 0:64 * 128], 128), in_=wd8[:, ts(oc, 128)].rearrange("(c p) o -> p c o", p=128)),
                        deps=p.use(f"wb{wbi}") + fuse_toks[-1:], dma=True)
                    ps = psG if wbi == 0 else psU
                    pskey = "psG" if wbi == 0 else "psU"

                    def f_d(e, ps=ps, wbi=wbi):
                        for ic in range(64):
                            ins = e.matmul(ps[:], wb[wbi][:, ts(ic, 128)], hf[:, ts(ic, 512)],
                                           start=(ic == 0), stop=(ic == 63))
                        return ins

                    t_d = p.add("pe", f_d, deps=[t_wd] + p.use(pskey) + fuse_toks)
                    p.set(f"wb{wbi}", t_d)

                    def f_dev(e, oc=oc, ps=ps, tb=tb):
                        sl = slice(oc * NT + tb * 512, oc * NT + (tb + 1) * 512)
                        return e.scalar_tensor_tensor(out=x_s[:, sl], in0=ps[:],
                                                      scalar=sd_s[:, oc:oc + 1],
                                                      in1=x_s[:, sl], op0=ALU.mult,
                                                      op1=ALU.add)

                    t_dev = p.add("dve", f_dev, deps=[t_d])
                    p.set(pskey, t_dev)
                    dn_toks.append(t_dev)
                    ffn_toks.append(t_dev)
                p.set("hfrd", dn_toks[-1])

        st_x1 = p.add("sp", lambda e: e.dma_start(
            out=d3(x1_fm), in_=s3(x_s[:], NT)),
            deps=ffn_toks, dma=True)

        with ExitStack() as st2:
            sb2 = lambda nm, shp, dt: st2.enter_context(nc.sbuf_tensor(nm, shp, dt))
            bufs = mk_ln_bufs(sb2, pst)
            bufs["ones_bf"] = ones_bf
            bufs["ones_f32"] = ones_f32
            xn1_s = sb2("xn1_s", [128, NC * NT], BF16)
            evp = [sb2("evA", [128, 512], BF16), sb2("evB", [128, 512], BF16)]
            evv = [sb2("evVA", [128, 512], BF16), sb2("evVB", [128, 512], BF16)]
            ones16 = sb2("ones16_s", [1, 128], F16)

            xn_toks = emit_ln(p, bufs, x_s, xn1_s, NT, ffn_toks + cdeps)
            st_xn1 = p.add("sp", lambda e: e.dma_start(
                out=d3(xn1), in_=s3(xn1_s[:], NT)),
                deps=xn_toks, dma=True)

            def route_q(orow):
                def f(e, ps, ev, orow):
                    return e.activation(out=ev[:], in_=ps[:], func=AF.Copy, bias=0.0,
                                        scale=SCALE)
                return f, q1

            def route_k(orow):
                def f(e, ps, ev, orow):
                    return e.copy(out=ev[:], in_=ps[:])
                return f, k1

            ev_q = mk_store_evac(p, evp, ["evA", "evB"], route_q)
            st_q1 = emit_proj(p, wq1, D, xn1_s, NT, xn_toks, wb, ["wb0", "wb1"],
                              [psA, psB], ["psA", "psB"], ev_q)
            ev_k = mk_store_evac(p, evp, ["evA", "evB"], route_k)
            st_k1 = emit_proj(p, wk1, D, xn1_s, NT, xn_toks, wb, ["wb0", "wb1"],
                              [psA, psB], ["psA", "psB"], ev_k)
            ev_v = mk_store_evac_tm(p, evv, ["evVA", "evVB"], v1)
            st_v1 = emit_proj_tm(p, wv1, D, 0, xn1_s, NT, xn_toks, wb, ["wb0", "wb1"],
                                 [psG, psU], ["psG", "psU"], ev_v)

            p.emit(nc, stack, [st_x1, st_xn1] + st_q1 + st_k1 + st_v1)
    return nc



def build_td():
    nc = bass.Bass("TRN2", target_bir_lowering=False)
    NO = 11264 // 8
    NH = 2048 // 8
    gup = nc.dram_tensor("gup", [NO, 256], I32, kind="ExternalInput").ap()
    gsc = nc.dram_tensor("gsc", [NO, 16], F16, kind="ExternalInput").ap()
    dp = nc.dram_tensor("dp", [NH, 704], I32, kind="ExternalInput").ap()
    dsc = nc.dram_tensor("dsc", [NH, 44], F16, kind="ExternalInput").ap()
    wgu_d = nc.dram_tensor("wgu_d", [NO, D], BF16, kind="ExternalOutput").ap()
    wd_d = nc.dram_tensor("wd_d", [NH, I1], BF16, kind="ExternalOutput").ap()

    with ExitStack() as stack:
        sb = lambda nm, shp, dt: stack.enter_context(nc.sbuf_tensor(nm, shp, dt))
        pk = sb("pk", [128, 704], I32)
        sh = sb("sh", [128, 1408], I16)
        bits = sb("bits", [128, 1408], I16)
        vals = sb("vals", [128, I1], F32)
        w_s = sb("w_s", [128, I1], BF16)
        sc_s = sb("sc_s", [128, 44], F32)

        p = P()
        t_prev = None

        def chunk(src_pk, src_sc, dst, nwords, ngs, t_prev):
            nv = nwords * 2
            nn = nwords * 8
            t_lp = p.add("pool", lambda e: e.dma_start(out=pk[:, 0:nwords], in_=src_pk),
                         deps=[t_prev], dma=True)
            t_ls = p.add("pool", lambda e: e.dma_start(out=sc_s[:, 0:ngs], in_=src_sc),
                         deps=[t_prev], dma=True)
            v16 = pk[:, 0:nwords].bitcast(I16)
            vals4 = vals[:, 0:nn].rearrange("p (n four) -> p n four", four=4)

            def f_deq(e):
                for j in range(4):
                    if j < 3:
                        e.tensor_scalar(out=sh[:, 0:nv], in0=v16, scalar1=12 - 4 * j,
                                        scalar2=None, op0=ALU.logical_shift_left)
                        srcb = sh[:, 0:nv]
                    else:
                        srcb = v16
                    e.tensor_scalar(out=bits[:, 0:nv], in0=srcb, scalar1=3, scalar2=-29184,
                                    op0=ALU.logical_shift_right, op1=ALU.bitwise_and)
                    e.tensor_copy(out=vals4[:, :, j], in_=bits[:, 0:nv].bitcast(F16))
                for g in range(ngs):
                    ins = e.tensor_scalar(out=w_s[:, ts(g, 128)], in0=vals[:, ts(g, 128)],
                                          scalar1=sc_s[:, g:g + 1], scalar2=None,
                                          op0=ALU.mult)
                return ins

            t_dq = p.add("dve", f_deq, deps=[t_lp, t_ls])
            return p.add("sp", lambda e: e.dma_start(out=dst, in_=w_s[:, 0:nn]),
                         deps=[t_dq], dma=True)

        for c in range(11):
            t_prev = chunk(gup[ts(c, 128), :], gsc[ts(c, 128), :], wgu_d[ts(c, 128), :],
                           256, 16, t_prev)
        for c in range(2):
            t_prev = chunk(dp[ts(c, 128), :], dsc[ts(c, 128), :], wd_d[ts(c, 128), :],
                           704, 44, t_prev)

        p.emit(nc, stack, [t_prev])
    return nc



def build_t5(stage="all"):
    nc = bass.Bass("TRN2", target_bir_lowering=False)
    x_fm = nc.dram_tensor("x_fm", [D, NT], F32, kind="ExternalInput").ap()
    ctx_fm = nc.dram_tensor("ctx_fm", [D, NT], BF16, kind="ExternalInput").ap()
    xn1_fm = nc.dram_tensor("xn1_fm", [D, NT], BF16, kind="ExternalInput").ap()
    wo1 = nc.dram_tensor("wo1", [D, D], F32, kind="ExternalInput").ap()
    bo_t = nc.dram_tensor("bo_t", [128, 16], F32, kind="ExternalInput").ap()
    wguT = nc.dram_tensor("wguT", [D, 2 * I1], BF16, kind="ExternalInput").ap()
    wdT = nc.dram_tensor("wdT", [I1, D], BF16, kind="ExternalInput").ap()
    ones_bf_in = nc.dram_tensor("ones_bf", [128, 1], BF16, kind="ExternalInput").ap()
    ones_f32_in = nc.dram_tensor("ones_f32", [1, 128], F32, kind="ExternalInput").ap()
    x2_fm = nc.dram_tensor("x2_fm", [D, NT], F32, kind="ExternalOutput").ap()

    P16 = 16384.0
    NIC = I1 // 128

    with ExitStack() as stack:
        sb = lambda nm, shp, dt: stack.enter_context(nc.sbuf_tensor(nm, shp, dt))
        pst = lambda nm, shp, dt=F32: stack.enter_context(nc.psum_tensor(nm, shp, dt))
        x_s = sb("x_s", [128, NC * NT], F32)
        wb = [sb("wb0", [128, NC * 512], BF16), sb("wb1", [128, NC * 512], BF16)]
        bo_s = sb("bo_s", [128, 16], F32)
        ones_bf = sb("ones_bf_s", [128, 1], BF16)
        ones_f32 = sb("ones_f32_s", [1, 128], F32)
        g_s = sb("g_s", [128, 512], BF16)
        u_s = sb("u_s", [128, 512], BF16)
        sqb = sb("sqb", [128, 512], BF16)
        sqc = sb("sqc", [128, 512], BF16)
        y_s = sb("y_s", [128, 16 * 512], BF16)
        tmpf = sb("tmpf", [128, 512], F32)
        vrow = sb("vrow", [1, 512], F32)
        srow = sb("srow", [1, 512], F32)
        rrow = sb("rrow", [1, 512], F32)
        psA = pst("psA", [128, 512])
        psB = pst("psB", [128, 512])
        psG = pst("psG", [128, 512])
        psU = pst("psU", [128, 512])
        ps_row = pst("ps_row", [128, 512])
        ps_bc = pst("ps_bc", [128, 512])

        p = P()
        tok_x = p.add("pool", lambda e: e.dma_start(
            out=s3(x_s[:], NT), in_=d3(x_fm)), dma=True)
        cdeps = []
        for dst, src in ((bo_s, bo_t), (ones_bf, ones_bf_in), (ones_f32, ones_f32_in)):
            cdeps.append(p.add("sp", lambda e, d=dst, s=src: e.dma_start(out=d[:], in_=s),
                               dma=True))

        with nc.sbuf_tensor("ctx_s", [128, NC * NT], BF16) as ctx_s:
            tok_ctx = p.add("pool", lambda e: e.dma_start(
                out=s3(ctx_s[:], NT), in_=d3(ctx_fm)), dma=True)

            def ev_oproj(p_, deps, ps, pskey, orow, tg):
                oc = orow // 128
                sl = slice(oc * NT + tg * 512, oc * NT + (tg + 1) * 512)
                return p_.add("dve", lambda e, ps=ps, sl=sl, oc=oc: e.scalar_tensor_tensor(
                    out=x_s[:, sl], in0=ps[:], scalar=bo_s[:, oc:oc + 1], in1=x_s[:, sl],
                    op0=ALU.add, op1=ALU.add), deps=deps + [tok_x] + cdeps)

            oproj_toks = emit_proj(p, wo1, D, ctx_s, NT, [tok_ctx],
                                   wb, ["wb0", "wb1"], [psA, psB], ["psA", "psB"], ev_oproj)
        p.set("oproj", oproj_toks[-1])

        out_toks = []
        if stage == "oproj":
            st_x2 = p.add("sp", lambda e: e.dma_start(
                out=d3(x2_fm), in_=s3(x_s[:], NT)), deps=oproj_toks, dma=True)
            p.emit(nc, stack, [st_x2])
            return nc
        with nc.sbuf_tensor("xn1_s", [128, NC * NT], BF16) as xn1_s, \
             nc.sbuf_tensor("hf", [128, NIC * 512], BF16) as hf:
            tok_xn1 = p.add("pool", lambda e: e.dma_start(
                out=s3(xn1_s[:], NT), in_=d3(xn1_fm)), deps=p.use("oproj"), dma=True)
            for tb in range(2):
                fuse_toks = []
                nicg = 3 if stage == "gate3" else 11
                for icg in range(nicg):
                    t_wg = p.add("pool", lambda e, icg=icg: e.dma_start(
                        out=s3(wb[0][:], 512), in_=d3(wguT[:, ts(icg, 512)])),
                        deps=p.use("wb0", "oproj"), dma=True)
                    t_wu = p.add("pool", lambda e, icg=icg: e.dma_start(
                        out=s3(wb[1][:], 512), in_=d3(wguT[:, I1 + icg * 512: I1 + (icg + 1) * 512])),
                        deps=p.use("wb1", "oproj"), dma=True)
                    mmg = []
                    mmu = []
                    for j4 in range(4):
                        ic = icg * 4 + j4

                        def f_g(e, j4=j4, tb=tb):
                            for dc in range(NC):
                                ins = e.matmul(psG[:], wb[0][:, dc * 512 + j4 * 128: dc * 512 + (j4 + 1) * 128],
                                               xn1_s[:, dc * NT + tb * 512: dc * NT + (tb + 1) * 512],
                                               start=(dc == 0), stop=(dc == NC - 1))
                            return ins

                        def f_u(e, j4=j4, tb=tb):
                            for dc in range(NC):
                                ins = e.matmul(psU[:], wb[1][:, dc * 512 + j4 * 128: dc * 512 + (j4 + 1) * 128],
                                               xn1_s[:, dc * NT + tb * 512: dc * NT + (tb + 1) * 512],
                                               start=(dc == 0), stop=(dc == NC - 1))
                            return ins

                        t_g = p.add("pe", f_g, deps=[t_wg, tok_xn1] + p.use("psG"))
                        t_u = p.add("pe", f_u, deps=[t_wu, tok_xn1] + p.use("psU"))
                        mmg.append(t_g)
                        mmu.append(t_u)
                        t_sil = p.add("act", lambda e: e.activation(
                            out=g_s[:], in_=psG[:], func=AF.Silu, bias=0.0, scale=P16),
                            deps=[t_g] + p.use("gsbuf"))
                        t_ue = p.add("dve", lambda e: e.tensor_scalar(
                            out=u_s[:], in0=psU[:], scalar1=P16, scalar2=None, op0=ALU.mult),
                            deps=[t_u] + p.use("usbuf"))
                        p.set("psG", t_sil)
                        p.set("psU", t_ue)
                        t_f = p.add("dve", lambda e, ic=ic: e.tensor_tensor(
                            out=hf[:, ts(ic, 512)], in0=g_s[:], in1=u_s[:], op=ALU.mult),
                            deps=[t_sil, t_ue] + p.use("hfrd"))
                        p.set("gsbuf", t_f)
                        p.set("usbuf", t_f)
                        fuse_toks.append(t_f)
                    p.set("wb0", mmg[-1])
                    p.set("wb1", mmu[-1])
                if stage == "gate3":
                    st_x2 = p.add("sp", lambda e: e.dma_start(
                        out=d3(x2_fm), in_=s3(x_s[:], NT)), deps=fuse_toks, dma=True)
                    p.emit(nc, stack, [st_x2])
                    return nc
                dn_toks = []
                for oc in range(16):
                    wbi = oc % 2
                    t_wd = p.add("pool", lambda e, oc=oc, wbi=wbi: e.dma_start(
                        out=s3(wb[wbi][:, 0:NIC * 128], 128), in_=wdT[:, ts(oc, 128)].rearrange("(c p) o -> p c o", p=128)),
                        deps=p.use(f"wb{wbi}") + fuse_toks[-1:], dma=True)
                    ps = psG if wbi == 0 else psU
                    pskey = "psG" if wbi == 0 else "psU"

                    def f_d(e, ps=ps, wbi=wbi):
                        for ic in range(NIC):
                            ins = e.matmul(ps[:], wb[wbi][:, ts(ic, 128)], hf[:, ts(ic, 512)],
                                           start=(ic == 0), stop=(ic == NIC - 1))
                        return ins

                    t_d = p.add("pe", f_d, deps=[t_wd] + p.use(pskey) + fuse_toks)
                    p.set(f"wb{wbi}", t_d)
                    t_ye = p.add("dve", lambda e, ps=ps, oc=oc: e.tensor_scalar(
                        out=y_s[:, ts(oc, 512)], in0=ps[:], scalar1=P16, scalar2=None,
                        op0=ALU.mult), deps=[t_d])
                    p.set(pskey, t_ye)
                    dn_toks.append(t_ye)
                sq_toks = []
                sqbufs = [sqb, sqc]
                for oc in range(16):
                    t_sq = p.add("act", lambda e, oc=oc, b=sqbufs[oc % 2]: e.activation(
                        out=b[:, 0:512], in_=y_s[:, ts(oc, 512)], func=AF.Square,
                        bias=0.0, scale=1.0),
                        deps=[dn_toks[oc]] + p.use(f"sqb{oc % 2}"))
                    sq_toks.append(t_sq)

                def f_rm(e):
                    for oc in range(16):
                        ins = e.matmul(ps_row[0:1, :], ones_bf[:, 0:1],
                                       sqbufs[oc % 2][:, 0:512].bitcast(BF16) if False else sqbufs[oc % 2][:, 0:512],
                                       start=(oc == 0), stop=(oc == 15))
                    return ins

                t_rm = p.add("pe", f_rm, deps=sq_toks + p.use("psrow"))
                p.set("sqb0", t_rm)
                p.set("sqb1", t_rm)
                rm_toks = [t_rm]

                def f_vrow(e):
                    return e.tensor_scalar(out=vrow[:], in0=ps_row[0:1, :], scalar1=1.0 / D,
                                           scalar2=RMS_EPS, op0=ALU.mult, op1=ALU.add)

                t_vr = p.add("dve", f_vrow, deps=[rm_toks[-1]])
                p.set("psrow", t_vr)
                t_sr = p.add("act", lambda e: e.activation(out=srow[:], in_=vrow[:],
                                                           func=AF.Sqrt), deps=[t_vr])
                t_rc = p.add("dve", lambda e: e.reciprocal(out=rrow[:], in_=srow[:]),
                             deps=[t_sr])
                t_bc = p.add("pe", lambda e: e.matmul(ps_bc[:], ones_f32[0:1, :], rrow[:],
                                                      start=True, stop=True),
                             deps=[t_rc] + p.use("psbc"))

                def f_fin(e, tb=tb):
                    for oc in range(16):
                        sl = slice(oc * NT + tb * 512, oc * NT + (tb + 1) * 512)
                        e.tensor_tensor(out=tmpf[:], in0=y_s[:, ts(oc, 512)], in1=ps_bc[:],
                                        op=ALU.mult)
                        ins = e.tensor_tensor(out=x_s[:, sl], in0=x_s[:, sl], in1=tmpf[:],
                                              op=ALU.add)
                    return ins

                t_fin = p.add("dve", f_fin, deps=[t_bc] + dn_toks)
                p.set("psbc", t_fin)
                out_toks.append(t_fin)
                if stage == "tb0":
                    break

        st_x2 = p.add("sp", lambda e: e.dma_start(
            out=d3(x2_fm), in_=s3(x_s[:], NT)),
            deps=out_toks, dma=True)
        p.emit(nc, stack, [st_x2])
    return nc



def host_consts():
    slopes = (2.0 ** (-0.5 * np.arange(1, H + 1))).astype(np.float64)
    k = np.arange(S, dtype=np.float64)
    bias0 = slopes[:, None] * (k[None, :] - (S - 1))
    A0 = np.round(bias0)
    bias1 = slopes[:, None] * k[None, :] / math.sqrt(HD)
    A1 = np.round(bias1)
    mask = np.where(np.arange(128)[None, :] <= np.arange(128)[:, None], 0.0, NEG)
    return {
        "A0": A0.astype(np.float16), "w0": np.exp(bias0 - A0).astype(np.float32),
        "c0": (-bias0).astype(np.float32),
        "A1": A1.astype(np.float16), "w1": np.exp(bias1 - A1).astype(np.float32),
        "c1": (-bias1).astype(np.float32),
        "mask": mask.astype(np.float32),
        "ident": np.eye(128, dtype=BF),
        "ones_bf": np.ones((128, 1), dtype=BF),
        "ones_f32": np.ones((1, 128), dtype=np.float32),
        "ones_f16": np.ones((1, 128), dtype=np.float16),
    }


def attn_core_consts(C, layer, g):
    A = C["A0"] if layer == 0 else C["A1"]
    w = C["w0"] if layer == 0 else C["w1"]
    c = C["c0"] if layer == 0 else C["c1"]
    hs = slice(g * 8, g * 8 + 8)
    a_arr = np.ascontiguousarray(A[hs].reshape(1, 8 * S))
    w_arr = np.ascontiguousarray(
        w[hs].reshape(8, 16, 128).transpose(2, 0, 1).reshape(128, 8 * 16))
    c_arr = np.ascontiguousarray(
        c[hs].reshape(8, 16, 128).transpose(2, 0, 1).reshape(128, 8 * 16))
    return a_arr, w_arr, c_arr


def col_t(v, n):
    return np.ascontiguousarray(v.reshape(n, 128).T)


_cache = {}


def _get(name, builder):
    if name not in _cache:
        _cache[name] = builder()
    return _cache[name]


def kernel(**inp):
    if FUSED:
        return kernel_fused(inp)
    f32 = np.float32
    X = np.ascontiguousarray(np.asarray(inp["L0_attn_x"], dtype=f32).reshape(B * S, D))
    C = host_consts()

    wqkvT = np.ascontiguousarray(np.asarray(inp["L0_attn_Wqkv"], f32).T)
    b_qkv = np.asarray(inp["L0_attn_b_qkv"], f32)
    woutT = np.ascontiguousarray(np.asarray(inp["L0_attn_Wout"], f32).T)
    b_out = np.asarray(inp["L0_attn_b_out"], f32)
    wg8 = np.ascontiguousarray(np.asarray(inp["L0_ffn_w_gate"]))
    wu8 = np.ascontiguousarray(np.asarray(inp["L0_ffn_w_up"]))
    wd8 = np.ascontiguousarray(np.asarray(inp["L0_ffn_w_down"]))
    sg = np.asarray(inp["L0_ffn_s_gate"], f32)
    su = np.asarray(inp["L0_ffn_s_up"], f32)
    sd = np.asarray(inp["L0_ffn_s_down"], f32)
    wq1 = np.ascontiguousarray(np.asarray(inp["L1_attn_Wq"], f32))
    wk1 = np.ascontiguousarray(np.asarray(inp["L1_attn_Wk"], f32))
    wv1 = np.ascontiguousarray(np.asarray(inp["L1_attn_Wv"], f32))
    wo1 = np.ascontiguousarray(np.asarray(inp["L1_attn_Wo"], f32))
    bo1 = np.asarray(inp["L1_attn_bo"], f32)
    gup = np.ascontiguousarray(np.asarray(inp["L1_ffn_gate_up_packed"]))
    gsc = np.ascontiguousarray(np.asarray(inp["L1_ffn_gate_up_scales"]).T)
    dp = np.ascontiguousarray(np.asarray(inp["L1_ffn_down_packed"]))
    dsc = np.ascontiguousarray(np.asarray(inp["L1_ffn_down_scales"]).T)

    x_slices = [np.ascontiguousarray(X[ts(c, NT), :].T) for c in range(8)]

    nc1 = _get("t1", build_t1)
    bq_t = col_t(b_qkv, 48)
    maps1 = [{
        "x_fm": x_slices[c], "wqkvT": wqkvT, "bq_t": bq_t,
        "bv_row": np.ascontiguousarray(b_qkv[2 * D:].reshape(1, D)),
        "ones_bf": C["ones_bf"], "ones_f32": C["ones_f32"], "ones_f16": C["ones_f16"],
    } for c in range(8)]
    r1 = run_bass_kernel_spmd(nc1, maps1, core_ids=list(range(8))).results

    Q0 = [np.concatenate([r1[2 * b]["q0"], r1[2 * b + 1]["q0"]], axis=1) for b in range(B)]
    K0 = [np.concatenate([r1[2 * b]["k0"], r1[2 * b + 1]["k0"]], axis=1) for b in range(B)]
    V0 = [np.concatenate([r1[2 * b]["v0"], r1[2 * b + 1]["v0"]], axis=0) for b in range(B)]

    nca = _get("attn", build_attn)
    maps2 = []
    for c in range(8):
        b, g = c // 2, c % 2
        a_arr, w_arr, c_arr = attn_core_consts(C, 0, g)
        maps2.append({
            "q_in": np.ascontiguousarray(Q0[b][ts(g, 1024), :]),
            "k_in": np.ascontiguousarray(K0[b][ts(g, 1024), :]),
            "v_in": np.ascontiguousarray(V0[b][:, ts(g, 1024)]),
            "a_in": a_arr, "w_in": w_arr, "c_in": c_arr,
            "mask_in": C["mask"], "ident_in": C["ident"], "ones_f16": C["ones_f16"],
        })
    r2 = run_bass_kernel_spmd(nca, maps2, core_ids=list(range(8))).results
    CTX0 = [np.concatenate([r2[2 * b]["ctx_out"], r2[2 * b + 1]["ctx_out"]], axis=1)
            for b in range(B)]

    ncd = _get("td", build_td)
    mapsd = [{
        "gup": np.ascontiguousarray(gup[ts(c, 1408), :]),
        "gsc": np.ascontiguousarray(gsc[ts(c, 1408), :]),
        "dp": np.ascontiguousarray(dp[ts(c, 256), :]),
        "dsc": np.ascontiguousarray(dsc[ts(c, 256), :]),
    } for c in range(8)]
    rd = run_bass_kernel_spmd(ncd, mapsd, core_ids=list(range(8))).results
    wgu_dense = np.concatenate([rd[c]["wgu_d"] for c in range(8)], axis=0)
    wd_dense = np.concatenate([rd[c]["wd_d"] for c in range(8)], axis=0)
    wguT = np.ascontiguousarray(wgu_dense.T)
    wdT = np.ascontiguousarray(wd_dense.T)

    nc3 = _get("t3", build_t3)
    maps3 = []
    for c in range(8):
        b = c // 2
        ctxb = CTX0[b].T
        maps3.append({
            "x_fm": x_slices[c],
            "ctx_fm": np.ascontiguousarray(ctxb[:, ts(c % 2, 1024)]),
            "xn0_fm": r1[c]["xn0"],
            "woutT": woutT, "bout_t": col_t(b_out, 16),
            "wg8": wg8, "wu8": wu8, "wd8": wd8,
            "sg_t": col_t(sg, 64), "su_t": col_t(su, 64), "sd_t": col_t(sd, 16),
            "wq1": wq1, "wk1": wk1, "wv1": wv1,
            "ones_bf": C["ones_bf"], "ones_f32": C["ones_f32"],
        })
    r3 = run_bass_kernel_spmd(nc3, maps3, core_ids=list(range(8))).results

    Q1 = [np.concatenate([r3[2 * b]["q1"], r3[2 * b + 1]["q1"]], axis=1) for b in range(B)]
    K1 = [np.concatenate([r3[2 * b]["k1"], r3[2 * b + 1]["k1"]], axis=1) for b in range(B)]
    V1 = [np.concatenate([r3[2 * b]["v1"], r3[2 * b + 1]["v1"]], axis=0) for b in range(B)]

    maps4 = []
    for c in range(8):
        b, g = c // 2, c % 2
        a_arr, w_arr, c_arr = attn_core_consts(C, 1, g)
        maps4.append({
            "q_in": np.ascontiguousarray(Q1[b][ts(g, 1024), :]),
            "k_in": np.ascontiguousarray(K1[b][ts(g, 1024), :]),
            "v_in": np.ascontiguousarray(V1[b][:, ts(g, 1024)]),
            "a_in": a_arr, "w_in": w_arr, "c_in": c_arr,
            "mask_in": C["mask"], "ident_in": C["ident"], "ones_f16": C["ones_f16"],
        })
    r4 = run_bass_kernel_spmd(nca, maps4, core_ids=list(range(8))).results
    CTX1 = [np.concatenate([r4[2 * b]["ctx_out"], r4[2 * b + 1]["ctx_out"]], axis=1)
            for b in range(B)]

    nc5 = _get("t5", build_t5)
    maps5 = []
    for c in range(8):
        b = c // 2
        ctxb = CTX1[b].T
        maps5.append({
            "x_fm": r3[c]["x1_fm"],
            "ctx_fm": np.ascontiguousarray(ctxb[:, ts(c % 2, 1024)]),
            "xn1_fm": r3[c]["xn1"],
            "wo1": wo1, "bo_t": col_t(bo1, 16),
            "wguT": wguT, "wdT": wdT,
            "ones_bf": C["ones_bf"], "ones_f32": C["ones_f32"],
        })
    r5 = run_bass_kernel_spmd(nc5, maps5, core_ids=list(range(8))).results

    X2 = np.empty((B * S, D), dtype=f32)
    for c in range(8):
        X2[ts(c, NT), :] = r5[c]["x2_fm"].T
    return X2.reshape(B, S, D)





def emit_ln_stream(p, nc, st, x_dram, xn_s, pst0, ones_bf, ones_f32, bar, tag):
    lnst = st.enter_context(ExitStack())
    sb = lambda nm, shp, dt: st.enter_context(nc.sbuf_tensor(tag + nm, shp, dt))
    pst = lambda nm, shp, dt=F32: lnst.enter_context(nc.psum_tensor(tag + nm, shp, dt))
    xtg = [sb("ln_xtg0", [128, NC * 512], BF16), sb("ln_xtg1", [128, NC * 512], BF16)]
    bufs = mk_ln_bufs(sb, pst, with_xb=False)
    bufs["ones_bf"] = ones_bf
    bufs["ones_f32"] = ones_f32
    sq = bufs["ln_sq"]
    mrow, vrow, srow, rrow, msq = (bufs["ln_mrow"], bufs["ln_vrow"],
                                   bufs["ln_srow"], bufs["ln_rrow"], bufs["ln_msq"])
    ps_r0, ps_r1 = bufs["ps_row0"], bufs["ps_row1"]
    ps_bcm, ps_bcr = bufs["ps_bcm"], bufs["ps_bcr"]
    tmp = bufs["ln_tmp"]
    toks = []
    for tg in range(S // 512):
        xs = xtg[tg % 2]
        xb = xs
        t_x = p.add("pool", lambda e, tg=tg, xs=xs: e.dma_start(
            out=s3(xs[:], 512), in_=d3(x_dram[:, ts(tg, 512)])),
            deps=bar + p.use(tag + f"lnx{tg % 2}"), dma=True)

        def f_xb(e, xs=xs):
            for dc in range(NC):
                ins = e.activation(out=sq[:, ts(dc, 512)], in_=xs[:, ts(dc, 512)],
                                   func=AF.Square)
            return ins

        t_xb = p.add("act", f_xb, deps=[t_x] + p.use(tag + "ln_red"))

        def f_sum(e, xb=xb):
            for dc in range(NC):
                e.matmul(ps_r0[0:1, :], ones_bf[:, 0:1], xb[:, ts(dc, 512)],
                         start=(dc == 0), stop=(dc == NC - 1))
            for dc in range(NC):
                ins = e.matmul(ps_r1[0:1, :], ones_bf[:, 0:1], sq[:, ts(dc, 512)],
                               start=(dc == 0), stop=(dc == NC - 1))
            return ins

        t_red = p.add("pe", f_sum, deps=[t_xb] + p.use(tag + "ln_rows") + bar)
        p.set(tag + "ln_red", t_red)

        def f_rows(e):
            e.tensor_scalar(out=mrow[:], in0=ps_r0[0:1, :], scalar1=1.0 / D,
                            scalar2=None, op0=ALU.mult)
            e.tensor_tensor(out=msq[:], in0=mrow[:], in1=mrow[:], op=ALU.mult)
            e.tensor_scalar(out=vrow[:], in0=ps_r1[0:1, :], scalar1=1.0 / D,
                            scalar2=LN_EPS, op0=ALU.mult, op1=ALU.add)
            return e.tensor_tensor(out=vrow[:], in0=vrow[:], in1=msq[:], op=ALU.subtract)

        t_rows = p.add("dve", f_rows, deps=[t_red])
        p.set(tag + "ln_rows", t_rows)
        t_sqrt = p.add("act", lambda e: e.activation(out=srow[:], in_=vrow[:], func=AF.Sqrt),
                       deps=[t_rows])
        t_rcp = p.add("dve", lambda e: e.reciprocal(out=rrow[:], in_=srow[:]), deps=[t_sqrt])

        def f_bc(e):
            e.matmul(ps_bcm[:], ones_f32[0:1, :], mrow[:], start=True, stop=True)
            return e.matmul(ps_bcr[:], ones_f32[0:1, :], rrow[:], start=True, stop=True)

        t_bc = p.add("pe", f_bc, deps=[t_rcp, t_rows] + p.use(tag + "ln_xn"))

        def f_xn(e, tg=tg, xs=xs):
            for dc in range(NC):
                sl = slice(dc * S + tg * 512, dc * S + tg * 512 + 512)
                e.tensor_tensor(out=tmp[:], in0=xs[:, ts(dc, 512)], in1=ps_bcm[:],
                                op=ALU.subtract)
                ins = e.tensor_tensor(out=xn_s[:, sl], in0=tmp[:], in1=ps_bcr[:],
                                      op=ALU.mult)
            return ins

        t_xn = p.add("dve", f_xn, deps=[t_bc, t_x])
        p.set(tag + "ln_xn", t_xn)
        p.set(tag + f"lnx{tg % 2}", t_xn)
        toks.append(t_xn)
    lnst.close()
    return toks


def emit_attn_stage(p, nc, st, pst, q_d, k_d, v_d, ctx_d, a_d, w_s, c_s, mask_s,
                    ident, ones16, bar, tag, interleave_cb=None, skew=True):
    global ATTN_SKEW_MODE
    """Full causal attention DRAM->DRAM, ctx stored fm. 3-stage skewed
    software pipeline: scores/exp(i) || transposes(i-1) || ctx/store(i-2)."""
    sb = lambda nm, shp, dt: st.enter_context(nc.sbuf_tensor(tag + nm, shp, dt))
    NKC = S // 128
    qh = [sb("qh0", [128, S], BF16), sb("qh1", [128, S], BF16)]
    kh = [sb("kh0", [128, S], BF16), sb("kh1", [128, S], BF16)]
    vh = [sb("vh0", [128, NKC * 129], BF16), sb("vh1", [128, NKC * 129], BF16)]
    p_s = [sb("p_s0", [128, S], BF16), sb("p_s1", [128, S], BF16)]
    pt_s = [sb("pt_s0", [128, S], BF16), sb("pt_s1", [128, S], BF16),
            sb("pt_s2", [128, S], BF16)]
    a_hs = [sb("ah0", [1, S], F16), sb("ah1", [1, S], F16)]
    rcp = sb("rcp", [128, 1], F32)
    ctx_ev = [sb("ctxev0", [128, 128], BF16), sb("ctxev1", [128, 128], BF16)]
    ctxh = [sb("ctxh0", [128, S], BF16), sb("ctxh1", [128, S], BF16)]
    store_toks = []
    ps_sc = [pst(tag + "ps_sc0", [128, 512]), pst(tag + "ps_sc1", [128, 512])]
    ps_pt = [pst(tag + "ps_ptA", [128, 1024], BF16), pst(tag + "ps_ptB", [128, 1024], BF16)]
    ps_cx = [pst(tag + "ps_cx0", [128, 512]), pst(tag + "ps_cx1", [128, 512])]

    t_ones_v = [p.add("dve", lambda e, b=b: e.memset(vh[b][:], 1.0), deps=bar)
                for b in range(2)]
    csc = [0]
    gq = [0]

    def emit_sc(h, qt, bsel, t_load, gi):
        klen = (qt + 1) * 128
        nch = (klen + 511) // 512
        par = gi % 2
        t_exps = []
        for c in range(nch):
            w_ = min(512, klen - c * 512)
            pi = csc[0] % 2
            csc[0] += 1
            psc = ps_sc[pi]
            sckey = tag + f"sc{pi}"

            def f_sc(e, h=h, qt=qt, c=c, w_=w_, bsel=bsel, psc=psc):
                e.matmul(psc[:, 0:w_], qh[bsel][:, ts(qt, 128)],
                         kh[bsel][:, c * 512: c * 512 + w_],
                         start=True, stop=False)
                return e.matmul(psc[:, 0:w_], ones16[0:1, :],
                                a_hs[bsel][0:1, c * 512: c * 512 + w_],
                                start=False, stop=True)

            t_sc = p.add("pe", f_sc, deps=t_load + p.use(sckey) + bar)
            if qt // 4 == c:
                off = (qt % 4) * 128
                t_sc = p.add("dve", lambda e, psc=psc, off=off: e.tensor_tensor(
                    out=psc[:, off:off + 128], in0=psc[:, off:off + 128],
                    in1=mask_s[:], op=ALU.add), deps=[t_sc])
            t_exp = p.add("act", lambda e, h=h, qt=qt, c=c, w_=w_, psc=psc, par=par: e.activation(
                out=p_s[par][:, c * 512: c * 512 + w_], in_=psc[:, 0:w_], func=AF.Exp,
                bias=c_s[:, h * 16 + qt: h * 16 + qt + 1], scale=1.0),
                deps=[t_sc] + p.use(tag + f"pexp{par}"))
            p.set(sckey, t_exp)
            t_exps.append(t_exp)
        return t_exps

    def emit_tr(h, qt, gi, t_exps):
        par = gi % 2
        par3 = gi % 3
        nkc = qt + 1
        pspt = ps_pt[gi % 2]
        t_ptev_list = []
        for kc4 in range((nkc + 3) // 4):
            kc_lo = kc4 * 4
            kc_hi = min(kc_lo + 4, nkc)
            exps_dep = [t_exps[min(kc4, len(t_exps) - 1)]]

            def f_tr(e, kc_lo=kc_lo, kc_hi=kc_hi, par=par, pspt=pspt):
                for kc in range(kc_lo, kc_hi):
                    ins = e.transpose(pspt[:, ts(kc % 4, 128)], p_s[par][:, ts(kc, 128)],
                                      ident[:])
                return ins

            t_tr = p.add("pe", f_tr, deps=exps_dep + p.use(tag + f"ptps{gi % 2}"))

            def f_ptev(e, kc_lo=kc_lo, kc_hi=kc_hi, h=h, par3=par3, pspt=pspt):
                for kc in range(kc_lo, kc_hi):
                    ins = e.tensor_scalar(
                        out=pt_s[par3][:, ts(kc, 128)], in0=pspt[:, ts(kc % 4, 128)],
                        scalar1=w_s[:, h * 16 + kc: h * 16 + kc + 1],
                        scalar2=None, op0=ALU.mult)
                return ins

            t_ptev = p.add("dve", f_ptev, deps=[t_tr] + p.use(tag + f"ptw{par3}"))
            p.set(tag + f"ptps{gi % 2}", t_ptev)
            t_ptev_list.append(t_ptev)
        p.set(tag + f"pexp{par}", t_ptev_list[-1])
        return t_ptev_list

    def emit_back(h, qt, bsel, gi, t_ptev_list, t_load):
        par3 = gi % 3
        nkc = qt + 1
        pcx = ps_cx[gi % 2]
        cxkey = tag + f"cx{gi % 2}"

        def f_cx(e, nkc=nkc, bsel=bsel, pcx=pcx, par3=par3):
            for kc in range(nkc):
                ins = e.matmul(pcx[:, 0:129], pt_s[par3][:, ts(kc, 128)],
                               vh[bsel][:, kc * 129:(kc + 1) * 129],
                               start=(kc == 0), stop=(kc == nkc - 1))
            return ins

        t_cx = p.add("pe", f_cx, deps=t_ptev_list + t_load + p.use(cxkey))
        p.set(tag + f"ptw{par3}", t_cx)
        t_rcp = p.add("dve", lambda e, pcx=pcx: e.reciprocal(
            out=rcp[:], in_=pcx[:, 128:129]), deps=[t_cx] + p.use(tag + "rcp"))
        cev = ctx_ev[gi % 2]
        t_cev = p.add("act", lambda e, pcx=pcx, cev=cev: e.activation(
            out=cev[:], in_=pcx[:, 0:128], func=AF.Identity, bias=0.0,
            scale=rcp[:]), deps=[t_rcp] + p.use(tag + f"cev{gi % 2}"))
        p.set(tag + "rcp", t_cev)
        p.set(cxkey, t_cev)
        t_ctr = p.add("pe", lambda e, cev=cev, pspt=ps_pt[gi % 2]: e.transpose(
            pspt[:, 512:640], cev[:], ident[:]),
            deps=[t_cev] + p.use(tag + f"ctps{gi % 2}"))
        p.set(tag + f"cev{gi % 2}", t_ctr)
        t_ctev = p.add("dve", lambda e, h=h, qt=qt, pspt=ps_pt[gi % 2]: e.tensor_copy(
            out=ctxh[h % 2][:, ts(qt, 128)], in_=pspt[:, 512:640]),
            deps=[t_ctr] + p.use(tag + f"ctxh{h % 2}_w"))
        p.set(tag + f"ctps{gi % 2}", t_ctev)
        p.set(tag + f"ctxh{h % 2}_w", t_ctev)
        if qt == NKC - 1:
            t_st = p.add("sp", lambda e, h=h: e.dma_start(
                out=ctx_d[ts(h, 128), :], in_=ctxh[h % 2][:]),
                deps=[t_ctev], dma=True)
            p.set(tag + f"ctxh{h % 2}_w", t_st)
            store_toks.append(t_st)
            return t_st
        return t_ctev

    head_backs = {hh: [] for hh in range(H)}
    pend_tr = []
    pend_back = []

    def advance(drain=False):
        back_keep = 0 if (drain or ATTN_SKEW_MODE != 2) else 1
        while len(pend_back) > back_keep:
            hb, qb, bb, gb, tpl, tlb = pend_back.pop(0)
            head_backs[hb].append(emit_back(hb, qb, bb, gb, tpl, tlb))
        while pend_tr if drain else len(pend_tr) > 1:
            ht, qt_, bt_, gt, te, tlt = pend_tr.pop(0)
            pend_back.append((ht, qt_, bt_, gt, emit_tr(ht, qt_, gt, te), tlt))
        if drain or ATTN_SKEW_MODE != 2:
            while pend_back:
                hb, qb, bb, gb, tpl, tlb = pend_back.pop(0)
                head_backs[hb].append(emit_back(hb, qb, bb, gb, tpl, tlb))

    for h in range(H):
        bsel = h % 2
        dep_war = (head_backs.get(h - 2, [])[-2:] + [t_ones_v[bsel]] + bar)
        t_lq = p.add("pool", lambda e, h=h, bsel=bsel: e.dma_start(
            out=qh[bsel][:], in_=q_d[ts(h, 128), :]), deps=dep_war, dma=True)
        t_lk = p.add("pool", lambda e, h=h, bsel=bsel: e.dma_start(
            out=kh[bsel][:], in_=k_d[ts(h, 128), :]), deps=dep_war, dma=True)
        t_lv = p.add("pool", lambda e, h=h, bsel=bsel: e.dma_start(
            out=vh[bsel][:].rearrange("p (c x) -> p c x", x=129)[:, :, 0:128],
            in_=v_d[:, ts(h, 128)].rearrange("(c p) v -> p c v", p=128)),
            deps=dep_war, dma=True)
        t_la = p.add("sp", lambda e, h=h, bsel=bsel: e.dma_start(
            out=a_hs[bsel][:], in_=a_d[h:h + 1, :]), deps=dep_war, dma=True)
        t_load = [t_lq, t_lk, t_lv, t_la]

        for qt in range(16):
            gi = gq[0]
            gq[0] += 1
            te = emit_sc(h, qt, bsel, t_load, gi)
            pend_tr.append((h, qt, bsel, gi, te, t_load))
            if skew:
                advance()
            else:
                advance(drain=True)
        if interleave_cb is not None:
            interleave_cb(h)
    advance(drain=True)
    return store_toks


def emit_dq_make(p, nc, st, gup, gscT, dp, dscT, wgu_dd, wd_dd, ident, cdeps):
    sb = lambda nm, shp, dt: st.enter_context(nc.sbuf_tensor(nm, shp, dt))
    pk = sb("dq_pk", [128, 352], I32)
    sh = sb("dq_sh", [128, 704], I16)
    bits = sb("dq_bits", [128, 4 * 704], I16)
    vals = sb("dq_vals", [128, 2816], BF16)
    w_sc = sb("dq_wsc", [128, 2816], BF16)
    wT = vals
    sc_s = sb("dq_sc", [128, 22], F32)
    ps_dq = st.enter_context(nc.psum_tensor("ps_dq", [128, 1024], BF16))

    def chunk(src_pk, src_sc, dst_view, nwords, ngs, n_tr):
        nv = nwords * 2
        nn = nwords * 8
        t_lp = p.add("pool", lambda e: e.dma_start(out=pk[:, 0:nwords], in_=src_pk),
                     deps=p.use("dq_pk") + cdeps, dma=True)
        t_ls = p.add("pool", lambda e: e.dma_start(out=sc_s[:, 0:ngs], in_=src_sc),
                     deps=p.use("dq_sc") + cdeps, dma=True)
        v16 = pk[:, 0:nwords].bitcast(I16)
        vals4 = vals[:, 0:nn].rearrange("p (n four) -> p n four", four=4)

        def f_bits(e):
            for j in range(4):
                if j < 3:
                    e.tensor_scalar(out=sh[:, 0:nv], in0=v16, scalar1=12 - 4 * j,
                                    scalar2=None, op0=ALU.logical_shift_left)
                    srcb = sh[:, 0:nv]
                else:
                    srcb = v16
                ins = e.tensor_scalar(out=bits[:, j * nv:(j + 1) * nv], in0=srcb,
                                      scalar1=3, scalar2=-29184,
                                      op0=ALU.logical_shift_right, op1=ALU.bitwise_and)
            return ins

        t_sh = p.add("dve", f_bits, deps=[t_lp] + p.use("dq_bits"))
        p.set("dq_pk", t_sh)

        def f_conv(e):
            for j in range(4):
                ins = e.copy(out=vals4[:, :, j], in_=bits[:, j * nv:(j + 1) * nv].bitcast(F16))
            return ins

        t_b = p.add("act", f_conv, deps=[t_sh] + p.use("dq_vals"))
        p.set("dq_bits", t_b)

        def f_scale(e, ngs=ngs):
            for g in range(ngs):
                ins = e.activation(out=w_sc[:, ts(g, 128)], in_=vals[:, ts(g, 128)],
                                   func=AF.Identity, bias=0.0,
                                   scale=sc_s[:, g:g + 1])
            return ins

        t_s = p.add("act", f_scale, deps=[t_b, t_ls] + p.use("dq_wsc"))
        p.set("dq_vals", t_s)
        p.set("dq_sc", t_s)
        tr_toks = []
        for g8 in range((n_tr + 7) // 8):
            def f_tr(e, g8=g8, n_tr=n_tr):
                for j in range(8):
                    g = g8 * 8 + j
                    if g >= n_tr:
                        break
                    ins = e.transpose(ps_dq[:, ts(j, 128)], w_sc[:, ts(g, 128)], ident[:])
                return ins

            t_tr = p.add("pe", f_tr, deps=[t_s] + p.use("dqpt"))

            def f_tev(e, g8=g8, n_tr=n_tr):
                for j in range(8):
                    g = g8 * 8 + j
                    if g >= n_tr:
                        break
                    ins = e.tensor_copy(out=wT[:, ts(g, 128)], in_=ps_dq[:, ts(j, 128)])
                return ins

            t_tev = p.add("dve", f_tev, deps=[t_tr] + p.use("dq_wT"))
            p.set("dqpt", t_tev)
            tr_toks.append(t_tev)
        t_st = p.add("sp", lambda e: e.dma_start(out=dst_view, in_=wT[:, 0:n_tr * 128]),
                     deps=tr_toks, dma=True)
        p.set("dq_wT", t_st)
        p.set("dq_wsc", t_st)
        p.set("dq_vals", t_st)
        return t_st

    jobs = []
    for c in range(88):
        jobs.append((gup[ts(c, 128), :], gscT[ts(c, 128), :],
                     wgu_dd[c], 256, 16, 16))
    for c in range(16):
        for half in range(2):
            jobs.append((dp[ts(c, 128), ts(half, 352)],
                         dscT[ts(c, 128), ts(half, 22)],
                         wd_dd[c][:, half * 2816:(half + 1) * 2816], 352, 22, 22))
    state = {"i": 0, "toks": []}

    def emit_some(n):
        while n > 0 and state["i"] < len(jobs):
            state["toks"].append(chunk(*jobs[state["i"]]))
            state["i"] += 1
            n -= 1
        return state["toks"]

    return emit_some


def emit_block_mlp(p, nc, st, pst, layer, x_src, ctx_src, xn_src, x_dst,
                   wdrams, consts, bar, tag, interleave_cb=None):
    sb = lambda nm, shp, dt: st.enter_context(nc.sbuf_tensor(tag + nm, shp, dt))
    x_s = sb("x_s", [128, NC * 512], F32)
    ctx_s = sb("ctx_s", [128, NC * 512], BF16)
    xn_s = sb("xn_s", [128, NC * 512], BF16)
    g_sb = [sb("g_s0", [128, 512], BF16), sb("g_s1", [128, 512], BF16)]
    u_sb = [sb("u_s0", [128, 512], BF16), sb("u_s1", [128, 512], BF16)]
    wb = [sb("wb0", [128, NC * 512], BF16), sb("wb1", [128, NC * 512], BF16)]
    psA = pst(tag + "psA", [128, 512])
    psB = pst(tag + "psB", [128, 512])
    psGs = [pst(tag + "psG0", [128, 512]), pst(tag + "psG1", [128, 512])]
    psUs = [pst(tag + "psU0", [128, 512]), pst(tag + "psU1", [128, 512])]
    if layer == 0:
        NI = I0
        NIC_ = 64
        wg_d, wu_d, wd_d = wdrams["wg8"], wdrams["wu8"], wdrams["wd8"]
        wout_d = wdrams["woutT"]
        bo_s, sg_s, su_s, sd_s = consts["bo0"], consts["sg"], consts["su"], consts["sd"]
        gu_scale = None
        w8r = [sb(f"w8r{i}", [128, 2048], I8) for i in range(2)]
        w8i = [0]

        def ld8(src_ap, dst_bf, dst_sl, extra):
            sl = w8i[0] % 2
            w8i[0] += 1
            t_l = p.add("sp", lambda e: e.dma_start(out=w8r[sl][:], in_=src_ap),
                        deps=p.use(tag + f"w8r{sl}"), dma=True)
            t_c = p.add("act", lambda e: e.copy(out=dst_bf[:, dst_sl], in_=w8r[sl][:]),
                        deps=[t_l] + extra)
            p.set(tag + f"w8r{sl}", t_c)
            return t_c
    else:
        NI = I1
        NIC_ = 44
        wgu_d, wd_d = wdrams["wgu_dd"], wdrams["wd_dd"]
        wout_d = wdrams["wo1"]
        bo_s = consts["bo1"]
        P16 = 16384.0
        sqb = sb("sqb", [128, 512], BF16)
        sqc = sb("sqc", [128, 512], BF16)
        y_s = sb("y_s", [128, 16 * 512], BF16)
        tmpf = sb("tmpf", [128, 512], F32)
        vrow = sb("vrow", [1, 512], F32)
        srow = sb("srow", [1, 512], F32)
        rrow = sb("rrow", [1, 512], F32)
        ps_row = pst(tag + "ps_row", [128, 512])
        ps_bc = pst(tag + "ps_bc", [128, 512])
    hf = sb("hf", [128, NIC_ * 512], BF16)
    ones_bf, ones_f32 = consts["ones_bf"], consts["ones_f32"]

    out_toks = []
    for tb in range(4):
        csl = ts(tb, 512)
        t_x = p.add("pool", lambda e, csl=csl: e.dma_start(
            out=s3(x_s[:], 512), in_=d3(x_src[:, csl])),
            deps=p.use(tag + "x_s") + bar, dma=True)
        t_ctx = p.add("pool", lambda e, csl=csl: e.dma_start(
            out=s3(ctx_s[:], 512), in_=d3(ctx_src[:, csl])),
            deps=p.use(tag + "ctx_s") + bar, dma=True)
        t_xn = p.add("pool", lambda e, csl=csl: e.dma_start(
            out=s3(xn_s[:], 512), in_=d3(xn_src[:, csl])),
            deps=p.use(tag + "xn_s") + bar, dma=True)

        op_toks = []
        for og in range(4):
            wi = og % 2
            wdep = p.use(tag + f"wb{wi}") + bar
            t_w4 = [p.add("pool", lambda e, og=og, c4=c4, b=wb[wi]: e.dma_start(
                out=b[:, c4 * 2048:(c4 + 1) * 2048], in_=wout_d[4 * og + c4]),
                deps=wdep, dma=True) for c4 in range(4)]
            mms = []
            for oc4 in range(4):
                oc = og * 4 + oc4
                ps = psA if oc % 2 == 0 else psB
                pskey = tag + ("psA" if oc % 2 == 0 else "psB")

                def f_mm(e, oc4=oc4, ps=ps, b=wb[wi]):
                    for dc in range(NC):
                        ins = e.matmul(ps[:], b[:, oc4 * 2048 + dc * 128: oc4 * 2048 + (dc + 1) * 128],
                                       ctx_s[:, ts(dc, 512)],
                                       start=(dc == 0), stop=(dc == NC - 1))
                    return ins

                t_mm = p.add("pe", f_mm, deps=[t_w4[oc4], t_ctx] + p.use(pskey) + bar)
                mms.append(t_mm)
                t_ev = p.add("dve", lambda e, ps=ps, oc=oc: e.scalar_tensor_tensor(
                    out=x_s[:, ts(oc, 512)], in0=ps[:], scalar=bo_s[:, oc:oc + 1],
                    in1=x_s[:, ts(oc, 512)], op0=ALU.add, op1=ALU.add),
                    deps=[t_mm, t_x] + p.use(tag + "consts"))
                p.set(pskey, t_ev)
                op_toks.append(t_ev)
            p.set(tag + f"wb{wi}", mms[-1])

        fuse_toks = []
        ngrp_icg = NIC_ // 4
        for icg in range(ngrp_icg):
            if layer == 0:
                gdep = p.use(tag + "wb0") + bar
                udep = p.use(tag + "wb1") + bar
                t_wg = [ld8(wg_d[4 * icg + c4], wb[0],
                            slice(c4 * 2048, (c4 + 1) * 2048), gdep)
                        for c4 in range(4)]
                t_wu = [ld8(wu_d[4 * icg + c4], wb[1],
                            slice(c4 * 2048, (c4 + 1) * 2048), udep)
                        for c4 in range(4)]
            else:
                gdep = p.use(tag + "wb0") + bar
                udep = p.use(tag + "wb1") + bar
                t_wg = [p.add("pool", lambda e, icg=icg, c4=c4: e.dma_start(
                    out=wb[0][:, c4 * 2048:(c4 + 1) * 2048], in_=wgu_d[4 * icg + c4]),
                    deps=gdep, dma=True) for c4 in range(4)]
                t_wu = [p.add("pool", lambda e, icg=icg, c4=c4: e.dma_start(
                    out=wb[1][:, c4 * 2048:(c4 + 1) * 2048], in_=wgu_d[44 + 4 * icg + c4]),
                    deps=udep, dma=True) for c4 in range(4)]
            if interleave_cb is not None:
                interleave_cb(-1)
            mmg = []
            mmu = []
            for j4 in range(4):
                ic = icg * 4 + j4
                pi = ic % 2
                psG = psGs[pi]
                psU = psUs[pi]
                g_s = g_sb[pi]
                u_s = u_sb[pi]

                def wsl(j4, dc):
                    return slice(j4 * 2048 + dc * 128, j4 * 2048 + (dc + 1) * 128)

                def f_g(e, j4=j4, psG=psG):
                    for dc in range(NC):
                        ins = e.matmul(psG[:], wb[0][:, wsl(j4, dc)],
                                       xn_s[:, ts(dc, 512)],
                                       start=(dc == 0), stop=(dc == NC - 1))
                    return ins

                def f_u(e, j4=j4, psU=psU):
                    for dc in range(NC):
                        ins = e.matmul(psU[:], wb[1][:, wsl(j4, dc)],
                                       xn_s[:, ts(dc, 512)],
                                       start=(dc == 0), stop=(dc == NC - 1))
                    return ins

                t_g = p.add("pe", f_g, deps=[t_wg[j4], t_xn] + p.use(tag + f"psG{pi}") + bar)
                t_u = p.add("pe", f_u, deps=[t_wu[j4], t_xn] + p.use(tag + f"psU{pi}"))
                mmg.append(t_g)
                mmu.append(t_u)
                if layer == 0:
                    t_sil = p.add("act", lambda e, ic=ic, psG=psG, g_s=g_s: e.activation(
                        out=g_s[:], in_=psG[:], func=AF.Silu, bias=0.0,
                        scale=sg_s[:, ic:ic + 1]), deps=[t_g] + p.use(tag + f"gsbuf{pi}"))
                    t_ue = p.add("dve", lambda e, ic=ic, psU=psU, u_s=u_s: e.tensor_scalar(
                        out=u_s[:], in0=psU[:], scalar1=su_s[:, ic:ic + 1],
                        scalar2=None, op0=ALU.mult), deps=[t_u] + p.use(tag + f"usbuf{pi}"))
                else:
                    t_sil = p.add("act", lambda e, psG=psG, g_s=g_s: e.activation(
                        out=g_s[:], in_=psG[:], func=AF.Silu, bias=0.0, scale=P16),
                        deps=[t_g] + p.use(tag + f"gsbuf{pi}"))
                    t_ue = p.add("dve", lambda e, psU=psU, u_s=u_s: e.tensor_scalar(
                        out=u_s[:], in0=psU[:], scalar1=P16, scalar2=None,
                        op0=ALU.mult), deps=[t_u] + p.use(tag + f"usbuf{pi}"))
                p.set(tag + f"psG{pi}", t_sil)
                p.set(tag + f"psU{pi}", t_ue)
                t_f = p.add("dve", lambda e, ic=ic, g_s=g_s, u_s=u_s: e.tensor_tensor(
                    out=hf[:, ts(ic, 512)], in0=g_s[:], in1=u_s[:], op=ALU.mult),
                    deps=[t_sil, t_ue] + p.use(tag + "hfrd"))
                p.set(tag + f"gsbuf{pi}", t_f)
                p.set(tag + f"usbuf{pi}", t_f)
                fuse_toks.append(t_f)
            p.set(tag + "wb0", mmg[-1])
            p.set(tag + "wb1", mmu[-1])

        dn_toks = []
        icq = NIC_ // 4
        for oc in range(16):
            wbi = oc % 2
            ddep = p.use(tag + f"wb{wbi}") + fuse_toks[-1:]
            if layer == 0:
                t_wd4 = [ld8(wd_d[oc][:, c4 * 2048:(c4 + 1) * 2048],
                             wb[wbi], slice(c4 * 2048, (c4 + 1) * 2048), ddep)
                         for c4 in range(4)]
            else:
                t_wd4 = [p.add("pool", lambda e, oc=oc, wbi=wbi, c4=c4: e.dma_start(
                    out=wb[wbi][:, c4 * icq * 128:(c4 + 1) * icq * 128],
                    in_=wd_d[oc][:, c4 * icq * 128:(c4 + 1) * icq * 128]),
                    deps=ddep, dma=True) for c4 in range(4)]
            ps = psGs[0] if wbi == 0 else psUs[0]
            pskey = tag + ("psG0" if wbi == 0 else "psU0")

            t_d = None
            for c4 in range(4):
                def f_d(e, ps=ps, wbi=wbi, c4=c4):
                    for ic in range(c4 * icq, (c4 + 1) * icq):
                        ins = e.matmul(ps[:], wb[wbi][:, ts(ic, 128)], hf[:, ts(ic, 512)],
                                       start=(ic == 0), stop=(ic == NIC_ - 1))
                    return ins

                dsub = [t_wd4[c4]] + ([t_d] if t_d is not None
                                      else p.use(pskey) + fuse_toks)
                t_d = p.add("pe", f_d, deps=dsub)
            p.set(tag + f"wb{wbi}", t_d)
            if layer == 0:
                t_dev = p.add("dve", lambda e, oc=oc, ps=ps: e.scalar_tensor_tensor(
                    out=x_s[:, ts(oc, 512)], in0=ps[:], scalar=sd_s[:, oc:oc + 1],
                    in1=x_s[:, ts(oc, 512)], op0=ALU.mult, op1=ALU.add),
                    deps=[t_d] + op_toks[-1:])
            else:
                t_dev = p.add("dve", lambda e, oc=oc, ps=ps: e.tensor_scalar(
                    out=y_s[:, ts(oc, 512)], in0=ps[:], scalar1=P16, scalar2=None,
                    op0=ALU.mult), deps=[t_d] + p.use(tag + "y_rd"))
            p.set(pskey, t_dev)
            dn_toks.append(t_dev)
        p.set(tag + "hfrd", dn_toks[-1])

        if layer == 1:
            sq_toks = []
            sqbufs = [sqb, sqc]
            for oc in range(16):
                t_sq = p.add("act", lambda e, oc=oc, b=sqbufs[oc % 2]: e.activation(
                    out=b[:], in_=y_s[:, ts(oc, 512)], func=AF.Square, bias=0.0,
                    scale=1.0), deps=[dn_toks[oc]] + p.use(tag + f"sqb{oc % 2}"))
                sq_toks.append(t_sq)

            def f_rm(e):
                for oc in range(16):
                    ins = e.matmul(ps_row[0:1, :], ones_bf[:, 0:1],
                                   sqbufs[oc % 2][:], start=(oc == 0), stop=(oc == 15))
                return ins

            t_rm = p.add("pe", f_rm, deps=sq_toks + p.use(tag + "psrow"))
            p.set(tag + "sqb0", t_rm)
            p.set(tag + "sqb1", t_rm)
            t_vr = p.add("dve", lambda e: e.tensor_scalar(
                out=vrow[:], in0=ps_row[0:1, :], scalar1=1.0 / D, scalar2=RMS_EPS,
                op0=ALU.mult, op1=ALU.add), deps=[t_rm])
            p.set(tag + "psrow", t_vr)
            t_sr = p.add("act", lambda e: e.activation(out=srow[:], in_=vrow[:],
                                                       func=AF.Sqrt), deps=[t_vr])
            t_rc = p.add("dve", lambda e: e.reciprocal(out=rrow[:], in_=srow[:]),
                         deps=[t_sr])
            t_bc = p.add("pe", lambda e: e.matmul(ps_bc[:], ones_f32[0:1, :], rrow[:],
                                                  start=True, stop=True),
                         deps=[t_rc] + p.use(tag + "psbc"))

            def f_fin(e):
                for oc in range(16):
                    e.tensor_tensor(out=tmpf[:], in0=y_s[:, ts(oc, 512)], in1=ps_bc[:],
                                    op=ALU.mult)
                    ins = e.tensor_tensor(out=x_s[:, ts(oc, 512)], in0=x_s[:, ts(oc, 512)],
                                          in1=tmpf[:], op=ALU.add)
                return ins

            t_fin = p.add("dve", f_fin, deps=[t_bc] + dn_toks + op_toks)
            p.set(tag + "psbc", t_fin)
            p.set(tag + "y_rd", t_fin)
            last = [t_fin]
        else:
            last = dn_toks

        t_stx = p.add("sp", lambda e, csl=csl: e.dma_start(
            out=d3(x_dst[:, csl]), in_=s3(x_s[:], 512)), deps=last + op_toks, dma=True)
        p.set(tag + "x_s", t_stx)
        p.set(tag + "ctx_s", t_stx)
        p.set(tag + "xn_s", t_stx)
        out_toks.append(t_stx)
        if interleave_cb is not None:
            interleave_cb(tb)
    return out_toks


def build_fused(stop_after=None, no_dq=False):
    nc = bass.Bass("TRN2", target_bir_lowering=False)
    di = lambda nm, shp, dt: nc.dram_tensor(nm, shp, dt, kind="ExternalInput").ap()
    dint = lambda nm, shp, dt: nc.dram_tensor(nm, shp, dt).ap()
    x_fm = di("x_fm", [D, S], F32)
    wqkvT = di("wqkvT", [12, 128, NC * 512], F32)
    bq_t = di("bq_t", [128, 48], F32)
    bv_row = di("bv_row", [1, D], F32)
    woutT = di("woutT", [16, 128, 2048], F32)
    bout_t = di("bout_t", [128, 16], F32)
    wg8 = di("wg8", [64, 128, 2048], I8)
    wu8 = di("wu8", [64, 128, 2048], I8)
    wd8 = di("wd8", [16, 128, I0], I8)
    sg_t = di("sg_t", [128, 64], F32)
    su_t = di("su_t", [128, 64], F32)
    sd_t = di("sd_t", [128, 16], F32)
    wq1 = di("wq1", [4, 128, NC * 512], F32)
    wk1 = di("wk1", [4, 128, NC * 512], F32)
    wv1 = di("wv1", [4, 128, NC * 512], F32)
    wo1 = di("wo1", [16, 128, 2048], F32)
    bo_t = di("bo_t", [128, 16], F32)
    gup = di("gup", [11264, 256], I32)
    gscT = di("gscT", [11264, 16], F16)
    dp = di("dp", [D, 704], I32)
    dscT = di("dscT", [D, 44], F16)
    a0_in = di("a0", [H, S], F16)
    w0_in = di("w0", [128, H * 16], F32)
    c0_in = di("c0", [128, H * 16], F32)
    a1_in = di("a1", [H, S], F16)
    w1_in = di("w1", [128, H * 16], F32)
    c1_in = di("c1", [128, H * 16], F32)
    mask_in = di("mask_in", [128, 128], F32)
    ident_in = di("ident_in", [128, 128], BF16)
    ones_bf_in = di("ones_bf", [128, 1], BF16)
    ones_f32_in = di("ones_f32", [1, 128], F32)
    ones_f16_in = di("ones_f16", [1, 128], F16)
    x2_fm = nc.dram_tensor("x2_fm", [D, S], F32, kind="ExternalOutput").ap()

    q0_d = dint("q0_d", [D, S], BF16)
    k0_d = dint("k0_d", [D, S], BF16)
    v0_d = dint("v0_d", [S, D], BF16)
    xn0_d = dint("xn0_d", [D, S], BF16)
    ctx0_d = dint("ctx0_d", [D, S], BF16)
    x1_d = dint("x1_d", [D, S], F32)
    xn1_d = dint("xn1_d", [D, S], BF16)
    q1_d = dint("q1_d", [D, S], BF16)
    k1_d = dint("k1_d", [D, S], BF16)
    v1_d = dint("v1_d", [S, D], BF16)
    ctx1_d = dint("ctx1_d", [D, S], BF16)
    wgu_dd = dint("wgu_dd", [88, 128, 2048], BF16)
    wd_dd = dint("wd_dd", [16, 128, I1], BF16)

    p = P()
    with ExitStack() as stack:
        sb = lambda nm, shp, dt: stack.enter_context(nc.sbuf_tensor(nm, shp, dt))
        w0_s = sb("w0_s", [128, H * 16], F32)
        c0_s = sb("c0_s", [128, H * 16], F32)
        w1_s = sb("w1_s", [128, H * 16], F32)
        c1_s = sb("c1_s", [128, H * 16], F32)
        mask_s = sb("mask_s", [128, 128], F32)
        ident = sb("ident", [128, 128], BF16)
        ones_bf = sb("ones_bf_s", [128, 1], BF16)
        ones_f32 = sb("ones_f32_s", [1, 128], F32)
        ones16 = sb("ones16_s", [1, 128], F16)
        bt = sb("bt", [128, 48], F32)
        bscl = sb("bscl", [128, 16], F32)
        bv32 = sb("bv32", [1, D], F32)
        bv16 = sb("bv16", [1, D], F16)
        bo0_s = sb("bo0_s", [128, 16], F32)
        bo1_s = sb("bo1_s", [128, 16], F32)
        sg_s = sb("sg_s", [128, 64], F32)
        su_s = sb("su_s", [128, 64], F32)
        sd_s = sb("sd_s", [128, 16], F32)

        cdeps = []
        for dst, srcd in ((w0_s, w0_in), (c0_s, c0_in),
                          (w1_s, w1_in), (c1_s, c1_in),
                          (mask_s, mask_in), (ident, ident_in), (ones_bf, ones_bf_in),
                          (ones_f32, ones_f32_in), (ones16, ones_f16_in),
                          (bt, bq_t), (bv32, bv_row), (bo0_s, bout_t), (bo1_s, bo_t),
                          (sg_s, sg_t), (su_s, su_t), (sd_s, sd_t)):
            cdeps.append(p.add("sp", lambda e, d=dst, s=srcd: e.dma_start(out=d[:], in_=s),
                               dma=True))
        t_bscl = p.add("dve", lambda e: e.tensor_scalar(
            out=bscl[:], in0=bt[:, 0:16], scalar1=SCALE, scalar2=None, op0=ALU.mult),
            deps=cdeps)
        t_bv16 = p.add("dve", lambda e: e.tensor_copy(out=bv16[:], in_=bv32[:]), deps=cdeps)
        cdeps += [t_bscl, t_bv16]
        consts = {"ones_bf": ones_bf, "ones_f32": ones_f32,
                  "bo0": bo0_s, "bo1": bo1_s, "sg": sg_s, "su": su_s, "sd": sd_s}
        wdrams = {"wg8": wg8, "wu8": wu8, "wd8": wd8, "woutT": woutT, "wo1": wo1,
                  "wgu_dd": wgu_dd, "wd_dd": wd_dd}

        def stage_qkv(st, pst, xn_s, wqkv_mode, q_dst, k_dst, v_dst, bar):
            sb2 = lambda nm, shp, dt: st.enter_context(nc.sbuf_tensor(wqkv_mode + nm, shp, dt))
            wb = [sb2("wbA", [128, NC * 512], BF16), sb2("wbB", [128, NC * 512], BF16)]
            evp = [sb2("evA", [128, 512], BF16), sb2("evB", [128, 512], BF16)]
            evv = [sb2("evVA", [128, 512], BF16), sb2("evVB", [128, 512], BF16)]
            psA = pst(wqkv_mode + "psA", [128, 512])
            psB = pst(wqkv_mode + "psB", [128, 512])
            psC = pst(wqkv_mode + "psC", [128, 512])
            psVA = pst(wqkv_mode + "psVA", [128, 512])
            psVB = pst(wqkv_mode + "psVB", [128, 512])
            xt = [None]

            if wqkv_mode == "A":
                def route_qk(orow):
                    if orow < D:
                        def f(e, ps, ev, orow):
                            return e.activation(out=ev[:], in_=ps[:], func=AF.Identity,
                                                bias=bscl[:, orow // 128: orow // 128 + 1],
                                                scale=SCALE)
                        return f, q_dst
                    def f(e, ps, ev, orow):
                        return e.activation(out=ev[:], in_=ps[:], func=AF.Identity,
                                            bias=bt[:, orow // 128: orow // 128 + 1],
                                            scale=1.0)
                    return f, k_dst

                ev_qk = mk_store_evac(p, evp, [wqkv_mode + "evA", wqkv_mode + "evB"], route_qk)
                st_qk = emit_proj(p, wqkvT, 2 * D, xn_s, S, bar + cdeps, wb,
                                  [wqkv_mode + "wbA", wqkv_mode + "wbB"],
                                  [psA, psB, psC],
                                  [wqkv_mode + "psA", wqkv_mode + "psB", wqkv_mode + "psC"],
                                  ev_qk)
                ev_v = mk_store_evac_tm(p, evv, [wqkv_mode + "evVA", wqkv_mode + "evVB"], v_dst)
                st_v = emit_proj_tm(p, wqkvT, D, 2 * D, xn_s, S, bar + cdeps, wb,
                                    [wqkv_mode + "wbA", wqkv_mode + "wbB"],
                                    [psVA, psVB], [wqkv_mode + "psVA", wqkv_mode + "psVB"],
                                    ev_v, bias_rhs=bv16, ones16=ones16)
                return st_qk + st_v
            else:
                def route_q(orow):
                    def f(e, ps, ev, orow):
                        return e.activation(out=ev[:], in_=ps[:], func=AF.Copy, bias=0.0,
                                            scale=SCALE)
                    return f, q_dst

                def route_k(orow):
                    def f(e, ps, ev, orow):
                        return e.copy(out=ev[:], in_=ps[:])
                    return f, k_dst

                ev_q = mk_store_evac(p, evp, [wqkv_mode + "evA", wqkv_mode + "evB"], route_q)
                st_q = emit_proj(p, wq1, D, xn_s, S, bar + cdeps, wb,
                                 [wqkv_mode + "wbA", wqkv_mode + "wbB"],
                                 [psA, psB, psC],
                                 [wqkv_mode + "psA", wqkv_mode + "psB", wqkv_mode + "psC"],
                                 ev_q)
                ev_k = mk_store_evac(p, evp, [wqkv_mode + "evA", wqkv_mode + "evB"], route_k)
                st_k = emit_proj(p, wk1, D, xn_s, S, bar + cdeps, wb,
                                 [wqkv_mode + "wbA", wqkv_mode + "wbB"],
                                 [psA, psB, psC],
                                 [wqkv_mode + "psA", wqkv_mode + "psB", wqkv_mode + "psC"],
                                 ev_k)
                ev_v = mk_store_evac_tm(p, evv, [wqkv_mode + "evVA", wqkv_mode + "evVB"], v_dst)
                st_v = emit_proj_tm(p, wv1, D, 0, xn_s, S, bar + cdeps, wb,
                                    [wqkv_mode + "wbA", wqkv_mode + "wbB"],
                                    [psVA, psVB], [wqkv_mode + "psVA", wqkv_mode + "psVB"],
                                    ev_v)
                return st_q + st_k + st_v

        with ExitStack() as stDQ:
            dq_emit0 = emit_dq_make(p, nc, stDQ, gup, gscT, dp, dscT, wgu_dd, wd_dd,
                                   ident, cdeps)
            dq_emit = (lambda n: []) if no_dq else dq_emit0

            with ExitStack() as stA:
                pstA = lambda nm, shp, dt=F32: stA.enter_context(nc.psum_tensor(nm, shp, dt))
                xn_s = stA.enter_context(nc.sbuf_tensor("xn0_s", [128, NC * S], BF16))
                ln_toks = emit_ln_stream(p, nc, stA, x_fm, xn_s, pstA, ones_bf, ones_f32,
                                         cdeps, "A")
                st_xn0 = p.add("sp", lambda e: e.dma_start(
                    out=d3(xn0_d), in_=s3(xn_s[:], S)), deps=ln_toks, dma=True)
                qkv_toks = stage_qkv(stA, pstA, xn_s, "A", q0_d, k0_d, v0_d, ln_toks)
                barA = p.frontier()
                dq_emit(8)

            with ExitStack() as stB:
                pstB = lambda nm, shp, dt=F32: stB.enter_context(nc.psum_tensor(nm, shp, dt))
                b_toks = emit_attn_stage(p, nc, stB, pstB, q0_d, k0_d, v0_d, ctx0_d,
                                         a0_in, w0_s, c0_s, mask_s, ident, ones16, barA,
                                         "B", interleave_cb=lambda h: dq_emit(2),
                                         skew=ATTN_SKEW)
                barB = p.frontier()
                if stop_after == "B":
                    t_dbg = p.add("pool", lambda e: e.dma_start(
                        out=x2_fm, in_=ctx0_d), deps=barB, dma=True)
                    p.emit(nc, stack, [t_dbg] + p.frontier())
                    return nc

            with ExitStack() as stC:
                pstC = lambda nm, shp, dt=F32: stC.enter_context(nc.psum_tensor(nm, shp, dt))
                c_toks = emit_block_mlp(p, nc, stC, pstC, 0, x_fm, ctx0_d, xn0_d, x1_d,
                                        wdrams, consts, barB, "C",
                                        interleave_cb=lambda tb: dq_emit(2))
                dq_toks = dq_emit(200)
                barC = p.frontier()
                if stop_after == "C":
                    t_dbg = p.add("pool", lambda e: e.dma_start(
                        out=x2_fm, in_=x1_d), deps=barC, dma=True)
                    p.emit(nc, stack, [t_dbg] + p.frontier())
                    return nc

        with ExitStack() as stD:
            pstD = lambda nm, shp, dt=F32: stD.enter_context(nc.psum_tensor(nm, shp, dt))
            xn1_s = stD.enter_context(nc.sbuf_tensor("xn1_s", [128, NC * S], BF16))
            ln1_toks = emit_ln_stream(p, nc, stD, x1_d, xn1_s, pstD, ones_bf, ones_f32, barC, "D")
            st_xn1 = p.add("sp", lambda e: e.dma_start(
                out=d3(xn1_d), in_=s3(xn1_s[:], S)), deps=ln1_toks, dma=True)
            qkv1_toks = stage_qkv(stD, pstD, xn1_s, "D", q1_d, k1_d, v1_d, ln1_toks)
            barD = p.frontier()
            if stop_after == "D":
                t_dbg = p.add("pool", lambda e: e.dma_start(
                    out=x2_fm, in_=x1_d), deps=barD, dma=True)
                p.emit(nc, stack, [t_dbg] + p.frontier())
                return nc

        with ExitStack() as stE:
            pstE = lambda nm, shp, dt=F32: stE.enter_context(nc.psum_tensor(nm, shp, dt))
            e_toks = emit_attn_stage(p, nc, stE, pstE, q1_d, k1_d, v1_d, ctx1_d,
                                     a1_in, w1_s, c1_s, mask_s, ident, ones16, barD, "E",
                                     skew=ATTN_SKEW)
            barE = p.frontier()
            if stop_after == "E":
                t_dbg = p.add("pool", lambda e: e.dma_start(
                    out=x2_fm, in_=x1_d), deps=barE, dma=True)
                p.emit(nc, stack, [t_dbg] + p.frontier())
                return nc

        with ExitStack() as stF:
            pstF = lambda nm, shp, dt=F32: stF.enter_context(nc.psum_tensor(nm, shp, dt))
            f_toks = emit_block_mlp(p, nc, stF, pstF, 1, x1_d, ctx1_d, xn1_d, x2_fm,
                                    wdrams, consts, barE, "F")

            p.emit(nc, stack, f_toks + p.frontier())
    return nc


ATTN_SKEW = True
ATTN_SKEW_MODE = 1
FUSED = True


def kernel_fused(inp):
    f32 = np.float32
    C = host_consts()
    X = np.asarray(inp["L0_attn_x"], dtype=f32)
    ncF = _get("fused", build_fused)

    def at16(Aarr, warr, carr):
        a_arr = np.ascontiguousarray(Aarr.reshape(H, S))
        w_arr = np.ascontiguousarray(
            warr.reshape(H, 16, 128).transpose(2, 0, 1).reshape(128, H * 16))
        c_arr = np.ascontiguousarray(
            carr.reshape(H, 16, 128).transpose(2, 0, 1).reshape(128, H * 16))
        return a_arr, w_arr, c_arr

    a0, w0, c0 = at16(C["A0"], C["w0"], C["c0"])
    a1, w1, c1 = at16(C["A1"], C["w1"], C["c1"])

    def rg(w, gw=512):
        din, dout = w.shape
        r = w.reshape(din // 128, 128, dout // gw, gw).transpose(2, 1, 0, 3)
        return np.ascontiguousarray(r.reshape(dout // gw, 128, (din // 128) * gw))

    wqkvT = np.asarray(inp["L0_attn_Wqkv"], f32).T
    b_qkv = np.asarray(inp["L0_attn_b_qkv"], f32)
    common = {
        "wqkvT": rg(wqkvT), "bq_t": col_t(b_qkv, 48),
        "bv_row": np.ascontiguousarray(b_qkv[2 * D:].reshape(1, D)),
        "woutT": rg(np.asarray(inp["L0_attn_Wout"], f32).T, 128),
        "bout_t": col_t(np.asarray(inp["L0_attn_b_out"], f32), 16),
        "wg8": rg(np.asarray(inp["L0_ffn_w_gate"]), 128),
        "wu8": rg(np.asarray(inp["L0_ffn_w_up"]), 128),
        "wd8": rg(np.asarray(inp["L0_ffn_w_down"]), 128),
        "sg_t": col_t(np.asarray(inp["L0_ffn_s_gate"], f32), 64),
        "su_t": col_t(np.asarray(inp["L0_ffn_s_up"], f32), 64),
        "sd_t": col_t(np.asarray(inp["L0_ffn_s_down"], f32), 16),
        "wq1": rg(np.asarray(inp["L1_attn_Wq"], f32)),
        "wk1": rg(np.asarray(inp["L1_attn_Wk"], f32)),
        "wv1": rg(np.asarray(inp["L1_attn_Wv"], f32)),
        "wo1": rg(np.asarray(inp["L1_attn_Wo"], f32), 128),
        "bo_t": col_t(np.asarray(inp["L1_attn_bo"], f32), 16),
        "gup": np.ascontiguousarray(np.asarray(inp["L1_ffn_gate_up_packed"])),
        "gscT": np.ascontiguousarray(np.asarray(inp["L1_ffn_gate_up_scales"]).T),
        "dp": np.ascontiguousarray(np.asarray(inp["L1_ffn_down_packed"])),
        "dscT": np.ascontiguousarray(np.asarray(inp["L1_ffn_down_scales"]).T),
        "a0": a0, "w0": w0, "c0": c0, "a1": a1, "w1": w1, "c1": c1,
        "mask_in": C["mask"], "ident_in": C["ident"],
        "ones_bf": C["ones_bf"], "ones_f32": C["ones_f32"], "ones_f16": C["ones_f16"],
    }
    maps = []
    for c in range(8):
        b = c // 2
        m = dict(common)
        m["x_fm"] = np.ascontiguousarray(X[b].T)
        maps.append(m)
    res = run_bass_kernel_spmd(ncF, maps, core_ids=list(range(8))).results
    out = np.empty((B, S, D), dtype=f32)
    for b in range(B):
        out[b] = res[2 * b]["x2_fm"].T
    return out


if __name__ == "__main__":
    pass


# version 18 (final)
# speedup vs baseline: 1.2092x; 1.2092x over previous
import math
from contextlib import ExitStack

import numpy as np
import ml_dtypes

import concourse.bass as bass
import concourse.mybir as mybir
from concourse.bass_utils import run_bass_kernel_spmd

F32 = mybir.dt.float32
BF16 = mybir.dt.bfloat16
F16 = mybir.dt.float16
I8 = mybir.dt.int8
I16 = mybir.dt.int16
I32 = mybir.dt.int32
AF = mybir.ActivationFunctionType
ALU = mybir.AluOpType
BF = ml_dtypes.bfloat16

D = 2048
H = 16
HD = 128
B = 4
S = 2048
NT = 1024
NC = 16
SCALE = 1.0 / math.sqrt(HD)
LN_EPS = 1e-5
RMS_EPS = 1e-6
I0 = 8192
I1 = 5632
NEG = -1e9


def ts(i, n):
    return slice(i * n, (i + 1) * n)


def d3(dram_ap):
    return dram_ap.rearrange("(c p) t -> p c t", p=128)


def s3(sbuf_ap, w):
    return sbuf_ap.rearrange("p (c t) -> p c t", t=w)



class P:
    SELF_WAIT_ALL = False
    ENG = ("sp", "pe", "act", "dve", "pool")
    BLOCK_ATTR = {"sp": "sync", "pe": "tensor", "act": "scalar",
                  "dve": "vector", "pool": "gpsimd"}

    NDMA_SLOTS = 6

    def __init__(self):
        self.q = {e: [] for e in self.ENG}
        self.cnt = {e: 0 for e in self.ENG}
        self.track = {}
        self.dma_idx = {}
        self.slot_uses = {}
        self.slot_sems = set()

    def add(self, eng, fn, deps=(), dma=False, self_wait=False):
        waits = {}

        def _w(t):
            if t is None:
                return
            e2, v = t
            if e2 != eng or self_wait or P.SELF_WAIT_ALL:
                waits[e2] = max(waits.get(e2, 0), v)

        for t in deps:
            if isinstance(t, list):
                for u in t:
                    _w(u)
            else:
                _w(t)
        if dma:
            i = self.dma_idx.get(eng, 0)
            self.dma_idx[eng] = i + 1
            sname = f"{eng}d{i % self.NDMA_SLOTS}"
            self.slot_sems.add(sname)
            prev = self.slot_uses.get(sname, 0)
            waits[sname] = max(waits.get(sname, 0), 16 * prev)
            self.slot_uses[sname] = prev + 1
            self.q[eng].append((waits, fn, 16, sname))
            return (sname, 16 * (prev + 1))
        self.cnt[eng] += 1
        self.q[eng].append((waits, fn, 1, eng))
        return (eng, self.cnt[eng])

    def frontier(self):
        toks = [(e, self.cnt[e]) for e in self.ENG if self.cnt[e] > 0]
        toks += [(s, 16 * self.slot_uses[s]) for s in sorted(self.slot_sems)]
        return toks

    def use(self, *keys):
        out = []
        for k in keys:
            out.extend(self.track.get(k, []))
        return out

    def set(self, key, *toks):
        self.track[key] = [t for t in toks if t is not None]

    def emit(self, nc, stack, final_deps):
        sems = {e: stack.enter_context(nc.semaphore(f"sem_{e}")) for e in self.ENG}
        for sname in sorted(self.slot_sems):
            sems[sname] = stack.enter_context(nc.semaphore(f"sem_{sname}"))
        fw = {}
        for t in final_deps:
            if t is None:
                continue
            e2, v = t
            fw[e2] = max(fw.get(e2, 0), v)
        block = stack.enter_context(nc.Block())
        for e in self.ENG:
            ops = self.q[e]
            if not ops and e != "pool":
                continue

            def body(eng, e=e, ops=ops):
                waited = {}
                for waits, fn, amt, tgt in ops:
                    for sname, v in waits.items():
                        if waited.get(sname, 0) < v:
                            eng.wait_ge(sems[sname], v)
                            waited[sname] = v
                    ins = fn(eng)
                    ins.then_inc(sems[tgt], amt)
                if e == "pool":
                    for sname, v in fw.items():
                        eng.wait_ge(sems[sname], v)

            getattr(block, self.BLOCK_ATTR[e])(body)



def mk_ln_bufs(sb, pst, with_xb=True):
    return {
        **({"ln_xb": sb("ln_xb", [128, NC * 512], BF16)} if with_xb else {}),
        "ln_sq": sb("ln_sq", [128, NC * 512], BF16),
        "ln_mrow": sb("ln_mrow", [1, 512], F32),
        "ln_vrow": sb("ln_vrow", [1, 512], F32),
        "ln_srow": sb("ln_srow", [1, 512], F32),
        "ln_rrow": sb("ln_rrow", [1, 512], F32),
        "ln_msq": sb("ln_msq", [1, 512], F32),
        "ln_tmp": sb("ln_tmp", [128, 512], F32),
        "ps_row0": pst("ps_row0", [128, 512]),
        "ps_row1": pst("ps_row1", [128, 512]),
        "ps_bcm": pst("ps_bcm", [128, 512]),
        "ps_bcr": pst("ps_bcr", [128, 512]),
    }


def emit_ln(p, bufs, x_s, xn_s, ntok, deps_x):
    xb, sq = bufs["ln_xb"], bufs["ln_sq"]
    mrow, vrow, srow, rrow, msq = (bufs["ln_mrow"], bufs["ln_vrow"],
                                   bufs["ln_srow"], bufs["ln_rrow"], bufs["ln_msq"])
    ps_r0, ps_r1 = bufs["ps_row0"], bufs["ps_row1"]
    ps_bcm, ps_bcr = bufs["ps_bcm"], bufs["ps_bcr"]
    ones_bf, ones_f32 = bufs["ones_bf"], bufs["ones_f32"]
    tmp = bufs["ln_tmp"]
    ngrp = ntok // 512
    toks = []
    for tg in range(ngrp):
        c0 = tg * 512

        def f_xb(e, c0=c0):
            for dc in range(NC):
                e.copy(out=xb[:, ts(dc, 512)], in_=x_s[:, dc * ntok + c0: dc * ntok + c0 + 512])
            for dc in range(NC):
                ins = e.activation(out=sq[:, ts(dc, 512)],
                                   in_=x_s[:, dc * ntok + c0: dc * ntok + c0 + 512],
                                   func=AF.Square)
            return ins

        t_xb = p.add("act", f_xb, deps=deps_x + p.use(tag + "ln_red"))

        def f_sum(e):
            for dc in range(NC):
                e.matmul(ps_r0[0:1, :], ones_bf[:, 0:1], xb[:, ts(dc, 512)],
                         start=(dc == 0), stop=(dc == NC - 1))
            for dc in range(NC):
                ins = e.matmul(ps_r1[0:1, :], ones_bf[:, 0:1], sq[:, ts(dc, 512)],
                               start=(dc == 0), stop=(dc == NC - 1))
            return ins

        t_red = p.add("pe", f_sum, deps=[t_xb] + p.use(tag + "ln_rows"))
        p.set(tag + "ln_red", t_red)

        def f_rows(e):
            e.tensor_scalar(out=mrow[:], in0=ps_r0[0:1, :], scalar1=1.0 / D,
                            scalar2=None, op0=ALU.mult)
            e.tensor_tensor(out=msq[:], in0=mrow[:], in1=mrow[:], op=ALU.mult)
            e.tensor_scalar(out=vrow[:], in0=ps_r1[0:1, :], scalar1=1.0 / D,
                            scalar2=LN_EPS, op0=ALU.mult, op1=ALU.add)
            return e.tensor_tensor(out=vrow[:], in0=vrow[:], in1=msq[:], op=ALU.subtract)

        t_rows = p.add("dve", f_rows, deps=[t_red])
        p.set(tag + "ln_rows", t_rows)
        t_sqrt = p.add("act", lambda e: e.activation(out=srow[:], in_=vrow[:], func=AF.Sqrt),
                       deps=[t_rows])
        t_rcp = p.add("dve", lambda e: e.reciprocal(out=rrow[:], in_=srow[:]), deps=[t_sqrt])

        def f_bc(e):
            e.matmul(ps_bcm[:], ones_f32[0:1, :], mrow[:], start=True, stop=True)
            return e.matmul(ps_bcr[:], ones_f32[0:1, :], rrow[:], start=True, stop=True)

        t_bc = p.add("pe", f_bc, deps=[t_rcp, t_rows] + p.use(tag + "ln_xn"))

        def f_xn(e, c0=c0):
            for dc in range(NC):
                sl = slice(dc * ntok + c0, dc * ntok + c0 + 512)
                e.tensor_tensor(out=tmp[:], in0=x_s[:, sl], in1=ps_bcm[:], op=ALU.subtract)
                ins = e.tensor_tensor(out=xn_s[:, sl], in0=tmp[:], in1=ps_bcr[:], op=ALU.mult)
            return ins

        t_xn = p.add("dve", f_xn, deps=[t_bc] + deps_x)
        p.set(tag + "ln_xn", t_xn)
        toks.append(t_xn)
    return toks



def emit_proj(p, wdram, n_out, src_s, ntok, src_toks, wb, wkeys, psbufs, pskeys,
              evac, extra_deps_w=()):
    ngrp = ntok // 512
    nog = n_out // 512
    ev_toks = []
    t_w = [None] * nog
    for og in range(nog):
        wi = og % 2

        def f_w(e, og=og, b=wb[wi]):
            return e.dma_start(out=b[:], in_=wdram[og])

        t_w[og] = p.add("pool", f_w, deps=p.use(wkeys[wi]) + list(extra_deps_w), dma=True)
        mm_toks = []
        for tg in range(ngrp):
            for oc4 in range(4):
                pi = (og * ngrp * 4 + tg * 4 + oc4) % len(psbufs)
                ps = psbufs[pi]
                orow = og * 512 + oc4 * 128

                def f_mm(e, tg=tg, oc4=oc4, ps=ps, b=wb[wi]):
                    for dc in range(NC):
                        ins = e.matmul(ps[:], b[:, dc * 512 + oc4 * 128: dc * 512 + (oc4 + 1) * 128],
                                       src_s[:, dc * ntok + tg * 512: dc * ntok + (tg + 1) * 512],
                                       start=(dc == 0), stop=(dc == NC - 1))
                    return ins

                t_mm = p.add("pe", f_mm, deps=[t_w[og]] + p.use(pskeys[pi]) + src_toks)
                mm_toks.append(t_mm)
                t_ev = evac(p, [t_mm], ps, pskeys[pi], orow, tg)
                p.set(pskeys[pi], t_ev)
                ev_toks.append(t_ev)
        p.set(wkeys[wi], mm_toks[-1])
    return ev_toks


def emit_proj_tm(p, wdram, n_out, o_base, src_s, ntok, src_toks, wb, wkeys,
                 psbufs, pskeys, evac, bias_rhs=None, ones16=None):
    ntt = ntok // 128
    nog = n_out // 512
    ev_toks = []
    for og in range(nog):
        wi = og % 2

        def f_w(e, og=og, b=wb[wi]):
            return e.dma_start(out=b[:], in_=wdram[o_base // 512 + og])

        t_w = p.add("pool", f_w, deps=p.use(wkeys[wi]), dma=True)
        mm_toks = []
        for tt in range(ntt):
            pi = (og * ntt + tt) % len(psbufs)
            ps = psbufs[pi]

            def f_mm(e, tt=tt, ps=ps, b=wb[wi], og=og):
                for dc in range(NC):
                    ins = e.matmul(ps[:], src_s[:, dc * ntok + tt * 128: dc * ntok + (tt + 1) * 128],
                                   b[:, ts(dc, 512)],
                                   start=(dc == 0), stop=(bias_rhs is None and dc == NC - 1))
                if bias_rhs is not None:
                    ins = e.matmul(ps[:], ones16[0:1, :], bias_rhs[0:1, ts(og, 512)],
                                   start=False, stop=True)
                return ins

            t_mm = p.add("pe", f_mm, deps=[t_w] + p.use(pskeys[pi]) + src_toks)
            mm_toks.append(t_mm)
            t_ev = evac(p, [t_mm], ps, pskeys[pi], og, tt)
            p.set(pskeys[pi], t_ev)
            ev_toks.append(t_ev)
        p.set(wkeys[wi], mm_toks[-1])
    return ev_toks


def mk_store_evac(p, evbufs, evkeys, route):
    state = {"i": 0}

    def evac(p_, deps, ps, pskey, orow, tg):
        i = state["i"] % 2
        state["i"] += 1
        ev = evbufs[i]
        emitfn, dst = route(orow)
        t_ev = p_.add("act", lambda e, ps=ps, ev=ev, orow=orow: emitfn(e, ps, ev, orow),
                      deps=deps + p_.use(evkeys[i]))

        def f_st(e, ev=ev, orow=orow, tg=tg):
            return e.dma_start(out=dst[(orow % D):(orow % D) + 128, ts(tg, 512)], in_=ev[:])

        t_st = p_.add("sp", f_st, deps=[t_ev], dma=True)
        p_.set(evkeys[i], t_st)
        return t_st

    return evac


def mk_store_evac_tm(p, evbufs, evkeys, dst):
    state = {"i": 0}

    def evac(p_, deps, ps, pskey, og, tt):
        i = state["i"] % 2
        state["i"] += 1
        ev = evbufs[i]
        t_ev = p_.add("act", lambda e, ps=ps, ev=ev: e.copy(out=ev[:], in_=ps[:, 0:512]),
                      deps=deps + p_.use(evkeys[i]))
        t_st = p_.add("sp", lambda e, ev=ev, og=og, tt=tt: e.dma_start(
            out=dst[ts(tt, 128), ts(og, 512)], in_=ev[:]), deps=[t_ev], dma=True)
        p_.set(evkeys[i], t_st)
        return t_st

    return evac



def build_t1():
    nc = bass.Bass("TRN2", target_bir_lowering=False)
    x_fm = nc.dram_tensor("x_fm", [D, NT], F32, kind="ExternalInput").ap()
    wqkvT = nc.dram_tensor("wqkvT", [D, 3 * D], F32, kind="ExternalInput").ap()
    bq_t = nc.dram_tensor("bq_t", [128, 48], F32, kind="ExternalInput").ap()
    bv_row = nc.dram_tensor("bv_row", [1, D], F32, kind="ExternalInput").ap()
    ones_bf_in = nc.dram_tensor("ones_bf", [128, 1], BF16, kind="ExternalInput").ap()
    ones_f32_in = nc.dram_tensor("ones_f32", [1, 128], F32, kind="ExternalInput").ap()
    ones_f16_in = nc.dram_tensor("ones_f16", [1, 128], F16, kind="ExternalInput").ap()

    xn0 = nc.dram_tensor("xn0", [D, NT], BF16, kind="ExternalOutput").ap()
    q0 = nc.dram_tensor("q0", [D, NT], BF16, kind="ExternalOutput").ap()
    k0 = nc.dram_tensor("k0", [D, NT], BF16, kind="ExternalOutput").ap()
    v0 = nc.dram_tensor("v0", [NT, D], BF16, kind="ExternalOutput").ap()

    with ExitStack() as stack:
        sb = lambda nm, shp, dt: stack.enter_context(nc.sbuf_tensor(nm, shp, dt))
        pst = lambda nm, shp, dt=F32: stack.enter_context(nc.psum_tensor(nm, shp, dt))
        bufs = mk_ln_bufs(sb, pst)
        bufs["ones_bf"] = sb("ones_bf_s", [128, 1], BF16)
        bufs["ones_f32"] = sb("ones_f32_s", [1, 128], F32)
        x_s = sb("x_s", [128, NC * NT], F32)
        xn_s = sb("xn_s", [128, NC * NT], BF16)
        wb = [sb("wb0", [128, NC * 512], BF16), sb("wb1", [128, NC * 512], BF16)]
        evp = [sb("evA", [128, 512], BF16), sb("evB", [128, 512], BF16)]
        evv = [sb("evVA", [128, 512], BF16), sb("evVB", [128, 512], BF16)]
        bt = sb("bt", [128, 48], F32)
        bscl = sb("bscl", [128, 16], F32)
        bv32 = sb("bv32", [1, D], F32)
        bv16 = sb("bv16", [1, D], F16)
        ones16 = sb("ones16", [1, 128], F16)
        psA = pst("psA", [128, 512])
        psB = pst("psB", [128, 512])
        psVA = pst("psVA", [128, 512])
        psVB = pst("psVB", [128, 512])

        p = P()
        tok_x = p.add("pool", lambda e: e.dma_start(
            out=s3(x_s[:], NT), in_=d3(x_fm)), dma=True)
        cdeps = []
        for dst, src in ((bufs["ones_bf"], ones_bf_in), (bufs["ones_f32"], ones_f32_in),
                         (ones16, ones_f16_in), (bt, bq_t), (bv32, bv_row)):
            cdeps.append(p.add("sp", lambda e, d=dst, s=src: e.dma_start(out=d[:], in_=s),
                               dma=True))
        t_bscl = p.add("dve", lambda e: e.tensor_scalar(
            out=bscl[:], in0=bt[:, 0:16], scalar1=SCALE, scalar2=None, op0=ALU.mult),
            deps=cdeps)
        t_bv16 = p.add("dve", lambda e: e.tensor_copy(out=bv16[:], in_=bv32[:]), deps=cdeps)

        xn_toks = emit_ln(p, bufs, x_s, xn_s, NT, [tok_x] + cdeps)
        st_xn = p.add("sp", lambda e: e.dma_start(
            out=d3(xn0), in_=s3(xn_s[:], NT)),
            deps=xn_toks, dma=True)

        def route_qk(orow):
            if orow < D:
                def f(e, ps, ev, orow):
                    return e.activation(out=ev[:], in_=ps[:], func=AF.Identity,
                                        bias=bscl[:, orow // 128: orow // 128 + 1],
                                        scale=SCALE)
                return f, q0
            def f(e, ps, ev, orow):
                return e.activation(out=ev[:], in_=ps[:], func=AF.Identity,
                                    bias=bt[:, orow // 128: orow // 128 + 1], scale=1.0)
            return f, k0

        ev_qk = mk_store_evac(p, evp, ["evA", "evB"], route_qk)
        st_qk = emit_proj(p, wqkvT, 2 * D, xn_s, NT, xn_toks + [t_bscl],
                          wb, ["wb0", "wb1"], [psA, psB], ["psA", "psB"], ev_qk)

        ev_v = mk_store_evac_tm(p, evv, ["evVA", "evVB"], v0)
        st_v = emit_proj_tm(p, wqkvT, D, 2 * D, xn_s, NT, xn_toks + [t_bv16],
                            wb, ["wb0", "wb1"], [psVA, psVB], ["psVA", "psVB"],
                            ev_v, bias_rhs=bv16, ones16=ones16)

        p.emit(nc, stack, [st_xn] + st_qk + st_v)
    return nc



def build_attn():
    nc = bass.Bass("TRN2", target_bir_lowering=False)
    q_in = nc.dram_tensor("q_in", [8 * HD, S], BF16, kind="ExternalInput").ap()
    k_in = nc.dram_tensor("k_in", [8 * HD, S], BF16, kind="ExternalInput").ap()
    v_in = nc.dram_tensor("v_in", [S, 8 * HD], BF16, kind="ExternalInput").ap()
    a_in = nc.dram_tensor("a_in", [1, 8 * S], F16, kind="ExternalInput").ap()
    w_in = nc.dram_tensor("w_in", [128, 8 * 16], F32, kind="ExternalInput").ap()
    c_in = nc.dram_tensor("c_in", [128, 8 * 16], F32, kind="ExternalInput").ap()
    mask_in = nc.dram_tensor("mask_in", [128, 128], F32, kind="ExternalInput").ap()
    ident_in = nc.dram_tensor("ident_in", [128, 128], BF16, kind="ExternalInput").ap()
    ones_f16_in = nc.dram_tensor("ones_f16", [1, 128], F16, kind="ExternalInput").ap()
    ctx_out = nc.dram_tensor("ctx_out", [S, 8 * HD], BF16, kind="ExternalOutput").ap()

    NKC = S // 128

    with ExitStack() as stack:
        sb = lambda nm, shp, dt: stack.enter_context(nc.sbuf_tensor(nm, shp, dt))
        pst = lambda nm, shp, dt=F32: stack.enter_context(nc.psum_tensor(nm, shp, dt))
        qh = [sb("qh0", [128, S], BF16), sb("qh1", [128, S], BF16)]
        kh = [sb("kh0", [128, S], BF16), sb("kh1", [128, S], BF16)]
        vh = [sb("vh0", [128, NKC * 129], BF16), sb("vh1", [128, NKC * 129], BF16)]
        p_s = sb("p_s", [128, S], BF16)
        pt_s = sb("pt_s", [128, S], BF16)
        a_s = sb("a_s", [1, 8 * S], F16)
        w_s = sb("w_s", [128, 8 * 16], F32)
        c_s = sb("c_s", [128, 8 * 16], F32)
        mask_s = sb("mask_s", [128, 128], F32)
        ident = sb("ident", [128, 128], BF16)
        ones16 = sb("ones16", [1, 128], F16)
        rcp = sb("rcp", [128, 1], F32)
        ctx_ev = [sb("ctxev0", [128, 128], BF16), sb("ctxev1", [128, 128], BF16)]
        ps_sc = pst("ps_sc", [128, S])
        ps_pt = pst("ps_pt", [128, 1024], BF16)
        ps_cx = [pst("ps_cx0", [128, 512]), pst("ps_cx1", [128, 512])]

        p = P()
        tc = []
        for dst, src in ((a_s, a_in), (w_s, w_in), (c_s, c_in), (mask_s, mask_in),
                         (ident, ident_in), (ones16, ones_f16_in)):
            tc.append(p.add("sp", lambda e, d=dst, s=src: e.dma_start(out=d[:], in_=s),
                            dma=True))
        t_ones_v = [p.add("dve", lambda e, b=b: e.memset(vh[b][:], 1.0)) for b in range(2)]

        for h in range(8):
            bsel = h % 2
            dep_war = p.use(f"head{h - 2}") + [t_ones_v[bsel]]
            t_lq = p.add("pool", lambda e, h=h, bsel=bsel: e.dma_start(
                out=qh[bsel][:], in_=q_in[ts(h, 128), :]), deps=dep_war, dma=True)
            t_lk = p.add("pool", lambda e, h=h, bsel=bsel: e.dma_start(
                out=kh[bsel][:], in_=k_in[ts(h, 128), :]), deps=dep_war, dma=True)
            t_lv = p.add("pool", lambda e, h=h, bsel=bsel: e.dma_start(
                out=vh[bsel][:].rearrange("p (c x) -> p c x", x=129)[:, :, 0:128],
                in_=v_in[:, ts(h, 128)].rearrange("(c p) v -> p c v", p=128)),
                deps=dep_war, dma=True)
            t_load = [t_lq, t_lk, t_lv]

            last_ctx_st = None
            for qt in range(16):
                klen = (qt + 1) * 128
                nch = (klen + 511) // 512

                def f_sc(e, h=h, qt=qt, klen=klen, nch=nch, bsel=bsel):
                    for c in range(nch):
                        w = min(512, klen - c * 512)
                        e.matmul(ps_sc[:, c * 512: c * 512 + w],
                                 qh[bsel][:, ts(qt, 128)],
                                 kh[bsel][:, c * 512: c * 512 + w],
                                 start=True, stop=False)
                        ins = e.matmul(ps_sc[:, c * 512: c * 512 + w],
                                       ones16[0:1, :],
                                       a_s[0:1, h * S + c * 512: h * S + c * 512 + w],
                                       start=False, stop=True)
                    return ins

                t_sc = p.add("pe", f_sc, deps=t_load + tc + p.use("exp"))
                t_mask = p.add("dve", lambda e, qt=qt: e.tensor_tensor(
                    out=ps_sc[:, ts(qt, 128)], in0=ps_sc[:, ts(qt, 128)],
                    in1=mask_s[:], op=ALU.add), deps=[t_sc])

                t_exp = p.add("act", lambda e, h=h, qt=qt, klen=klen: e.activation(
                    out=p_s[:, 0:klen], in_=ps_sc[:, 0:klen], func=AF.Exp,
                    bias=c_s[:, h * 16 + qt: h * 16 + qt + 1], scale=1.0),
                    deps=[t_mask] + p.use("ptev"))
                p.set("exp", t_exp)

                nkc = qt + 1
                t_ptev_list = []
                for kc4 in range((nkc + 3) // 4):
                    def f_tr(e, kc4=kc4, nkc=nkc):
                        for j in range(4):
                            kc = kc4 * 4 + j
                            if kc >= nkc:
                                break
                            ins = e.transpose(ps_pt[:, ts(j, 128)], p_s[:, ts(kc, 128)],
                                              ident[:])
                        return ins

                    t_tr = p.add("pe", f_tr, deps=[t_exp] + p.use("pt_ps"))

                    def f_ptev(e, kc4=kc4, nkc=nkc, h=h):
                        for j in range(4):
                            kc = kc4 * 4 + j
                            if kc >= nkc:
                                break
                            ins = e.tensor_scalar(
                                out=pt_s[:, ts(kc, 128)], in0=ps_pt[:, ts(j, 128)],
                                scalar1=w_s[:, h * 16 + kc: h * 16 + kc + 1],
                                scalar2=None, op0=ALU.mult)
                        return ins

                    t_ptev = p.add("dve", f_ptev, deps=[t_tr])
                    p.set("pt_ps", t_ptev)
                    t_ptev_list.append(t_ptev)
                p.set("ptev", t_ptev_list[-1])

                pcx = ps_cx[qt % 2]
                cxkey = f"cx{qt % 2}"

                def f_cx(e, nkc=nkc, bsel=bsel, pcx=pcx):
                    for kc in range(nkc):
                        ins = e.matmul(pcx[:, 0:129], pt_s[:, ts(kc, 128)],
                                       vh[bsel][:, kc * 129:(kc + 1) * 129],
                                       start=(kc == 0), stop=(kc == nkc - 1))
                    return ins

                t_cx = p.add("pe", f_cx, deps=t_ptev_list + t_load + p.use(cxkey))
                t_rcp = p.add("dve", lambda e, pcx=pcx: e.reciprocal(
                    out=rcp[:], in_=pcx[:, 128:129]), deps=[t_cx])
                cev = ctx_ev[qt % 2]
                t_cev = p.add("dve", lambda e, pcx=pcx, cev=cev: e.tensor_scalar(
                    out=cev[:], in0=pcx[:, 0:128], scalar1=rcp[:], scalar2=None,
                    op0=ALU.mult), deps=[t_rcp] + p.use(f"cev{qt % 2}"), self_wait=True)
                p.set(cxkey, t_cev)
                last_ctx_st = p.add("sp", lambda e, cev=cev, qt=qt, h=h: e.dma_start(
                    out=ctx_out[ts(qt, 128), ts(h, 128)], in_=cev[:]), deps=[t_cev], dma=True)
                p.set(f"cev{qt % 2}", last_ctx_st)
            p.set(f"head{h}", last_ctx_st)

        p.emit(nc, stack, p.use("head6") + p.use("head7"))
    return nc



def build_t3():
    nc = bass.Bass("TRN2", target_bir_lowering=False)
    x_fm = nc.dram_tensor("x_fm", [D, NT], F32, kind="ExternalInput").ap()
    ctx_fm = nc.dram_tensor("ctx_fm", [D, NT], BF16, kind="ExternalInput").ap()
    xn0_fm = nc.dram_tensor("xn0_fm", [D, NT], BF16, kind="ExternalInput").ap()
    woutT = nc.dram_tensor("woutT", [D, D], F32, kind="ExternalInput").ap()
    bout_t = nc.dram_tensor("bout_t", [128, 16], F32, kind="ExternalInput").ap()
    wg8 = nc.dram_tensor("wg8", [D, I0], I8, kind="ExternalInput").ap()
    wu8 = nc.dram_tensor("wu8", [D, I0], I8, kind="ExternalInput").ap()
    wd8 = nc.dram_tensor("wd8", [I0, D], I8, kind="ExternalInput").ap()
    sg_t = nc.dram_tensor("sg_t", [128, 64], F32, kind="ExternalInput").ap()
    su_t = nc.dram_tensor("su_t", [128, 64], F32, kind="ExternalInput").ap()
    sd_t = nc.dram_tensor("sd_t", [128, 16], F32, kind="ExternalInput").ap()
    wq1 = nc.dram_tensor("wq1", [D, D], F32, kind="ExternalInput").ap()
    wk1 = nc.dram_tensor("wk1", [D, D], F32, kind="ExternalInput").ap()
    wv1 = nc.dram_tensor("wv1", [D, D], F32, kind="ExternalInput").ap()
    ones_bf_in = nc.dram_tensor("ones_bf", [128, 1], BF16, kind="ExternalInput").ap()
    ones_f32_in = nc.dram_tensor("ones_f32", [1, 128], F32, kind="ExternalInput").ap()

    x1_fm = nc.dram_tensor("x1_fm", [D, NT], F32, kind="ExternalOutput").ap()
    xn1 = nc.dram_tensor("xn1", [D, NT], BF16, kind="ExternalOutput").ap()
    q1 = nc.dram_tensor("q1", [D, NT], BF16, kind="ExternalOutput").ap()
    k1 = nc.dram_tensor("k1", [D, NT], BF16, kind="ExternalOutput").ap()
    v1 = nc.dram_tensor("v1", [NT, D], BF16, kind="ExternalOutput").ap()

    with ExitStack() as stack:
        sb = lambda nm, shp, dt: stack.enter_context(nc.sbuf_tensor(nm, shp, dt))
        pst = lambda nm, shp, dt=F32: stack.enter_context(nc.psum_tensor(nm, shp, dt))
        x_s = sb("x_s", [128, NC * NT], F32)
        wb = [sb("wb0", [128, NC * 512], BF16), sb("wb1", [128, NC * 512], BF16)]
        bo_s = sb("bo_s", [128, 16], F32)
        sg_s = sb("sg_s", [128, 64], F32)
        su_s = sb("su_s", [128, 64], F32)
        sd_s = sb("sd_s", [128, 16], F32)
        ones_bf = sb("ones_bf_s", [128, 1], BF16)
        ones_f32 = sb("ones_f32_s", [1, 128], F32)
        psA = pst("psA", [128, 512])
        psB = pst("psB", [128, 512])
        psG = pst("psG", [128, 512])
        psU = pst("psU", [128, 512])

        p = P()
        tok_x = p.add("pool", lambda e: e.dma_start(
            out=s3(x_s[:], NT), in_=d3(x_fm)), dma=True)
        cdeps = []
        for dst, src in ((bo_s, bout_t), (sg_s, sg_t), (su_s, su_t), (sd_s, sd_t),
                         (ones_bf, ones_bf_in), (ones_f32, ones_f32_in)):
            cdeps.append(p.add("sp", lambda e, d=dst, s=src: e.dma_start(out=d[:], in_=s),
                               dma=True))

        with nc.sbuf_tensor("ctx_s", [128, NC * NT], BF16) as ctx_s:
            tok_ctx = p.add("pool", lambda e: e.dma_start(
                out=s3(ctx_s[:], NT), in_=d3(ctx_fm)), dma=True)

            def ev_oproj(p_, deps, ps, pskey, orow, tg):
                oc = orow // 128
                sl = slice(oc * NT + tg * 512, oc * NT + (tg + 1) * 512)
                return p_.add("dve", lambda e, ps=ps, sl=sl, oc=oc: e.scalar_tensor_tensor(
                    out=x_s[:, sl], in0=ps[:], scalar=bo_s[:, oc:oc + 1], in1=x_s[:, sl],
                    op0=ALU.add, op1=ALU.add), deps=deps + [tok_x] + cdeps)

            oproj_toks = emit_proj(p, woutT, D, ctx_s, NT, [tok_ctx],
                                   wb, ["wb0", "wb1"], [psA, psB], ["psA", "psB"], ev_oproj)
        p.set("oproj", oproj_toks[-1])

        ffn_toks = []
        with nc.sbuf_tensor("xn0_s", [128, NC * NT], BF16) as xn0_s, \
             nc.sbuf_tensor("hf", [128, 64 * 512], BF16) as hf, \
             nc.sbuf_tensor("g_s", [128, 512], BF16) as g_s, \
             nc.sbuf_tensor("u_s", [128, 512], BF16) as u_s:
            tok_xn0 = p.add("pool", lambda e: e.dma_start(
                out=s3(xn0_s[:], NT), in_=d3(xn0_fm)), deps=p.use("oproj"), dma=True)
            for tb in range(2):
                fuse_toks = []
                for icg in range(16):
                    t_wg = p.add("pool", lambda e, icg=icg: e.dma_start(
                        out=s3(wb[0][:], 512), in_=d3(wg8[:, ts(icg, 512)])),
                        deps=p.use("wb0", "oproj"), dma=True)
                    t_wu = p.add("pool", lambda e, icg=icg: e.dma_start(
                        out=s3(wb[1][:], 512), in_=d3(wu8[:, ts(icg, 512)])),
                        deps=p.use("wb1", "oproj"), dma=True)
                    mmg = []
                    mmu = []
                    for j4 in range(4):
                        ic = icg * 4 + j4

                        def f_g(e, j4=j4, tb=tb):
                            for dc in range(NC):
                                ins = e.matmul(psG[:], wb[0][:, dc * 512 + j4 * 128: dc * 512 + (j4 + 1) * 128],
                                               xn0_s[:, dc * NT + tb * 512: dc * NT + (tb + 1) * 512],
                                               start=(dc == 0), stop=(dc == NC - 1))
                            return ins

                        def f_u(e, j4=j4, tb=tb):
                            for dc in range(NC):
                                ins = e.matmul(psU[:], wb[1][:, dc * 512 + j4 * 128: dc * 512 + (j4 + 1) * 128],
                                               xn0_s[:, dc * NT + tb * 512: dc * NT + (tb + 1) * 512],
                                               start=(dc == 0), stop=(dc == NC - 1))
                            return ins

                        t_g = p.add("pe", f_g, deps=[t_wg, tok_xn0] + p.use("psG"))
                        t_u = p.add("pe", f_u, deps=[t_wu, tok_xn0] + p.use("psU"))
                        mmg.append(t_g)
                        mmu.append(t_u)
                        t_sil = p.add("act", lambda e, ic=ic: e.activation(
                            out=g_s[:], in_=psG[:], func=AF.Silu, bias=0.0,
                            scale=sg_s[:, ic:ic + 1]), deps=[t_g] + p.use("gsbuf"))
                        t_ue = p.add("dve", lambda e, ic=ic: e.tensor_scalar(
                            out=u_s[:], in0=psU[:], scalar1=su_s[:, ic:ic + 1],
                            scalar2=None, op0=ALU.mult), deps=[t_u] + p.use("usbuf"))
                        p.set("psG", t_sil)
                        p.set("psU", t_ue)
                        t_f = p.add("dve", lambda e, ic=ic: e.tensor_tensor(
                            out=hf[:, ts(ic, 512)], in0=g_s[:], in1=u_s[:], op=ALU.mult),
                            deps=[t_sil, t_ue] + p.use("hfrd"))
                        p.set("gsbuf", t_f)
                        p.set("usbuf", t_f)
                        fuse_toks.append(t_f)
                    p.set("wb0", mmg[-1])
                    p.set("wb1", mmu[-1])
                dn_toks = []
                for oc in range(16):
                    wbi = oc % 2
                    t_wd = p.add("pool", lambda e, oc=oc, wbi=wbi: e.dma_start(
                        out=s3(wb[wbi][:, 0:64 * 128], 128), in_=wd8[:, ts(oc, 128)].rearrange("(c p) o -> p c o", p=128)),
                        deps=p.use(f"wb{wbi}") + fuse_toks[-1:], dma=True)
                    ps = psG if wbi == 0 else psU
                    pskey = "psG" if wbi == 0 else "psU"

                    def f_d(e, ps=ps, wbi=wbi):
                        for ic in range(64):
                            ins = e.matmul(ps[:], wb[wbi][:, ts(ic, 128)], hf[:, ts(ic, 512)],
                                           start=(ic == 0), stop=(ic == 63))
                        return ins

                    t_d = p.add("pe", f_d, deps=[t_wd] + p.use(pskey) + fuse_toks)
                    p.set(f"wb{wbi}", t_d)

                    def f_dev(e, oc=oc, ps=ps, tb=tb):
                        sl = slice(oc * NT + tb * 512, oc * NT + (tb + 1) * 512)
                        return e.scalar_tensor_tensor(out=x_s[:, sl], in0=ps[:],
                                                      scalar=sd_s[:, oc:oc + 1],
                                                      in1=x_s[:, sl], op0=ALU.mult,
                                                      op1=ALU.add)

                    t_dev = p.add("dve", f_dev, deps=[t_d])
                    p.set(pskey, t_dev)
                    dn_toks.append(t_dev)
                    ffn_toks.append(t_dev)
                p.set("hfrd", dn_toks[-1])

        st_x1 = p.add("sp", lambda e: e.dma_start(
            out=d3(x1_fm), in_=s3(x_s[:], NT)),
            deps=ffn_toks, dma=True)

        with ExitStack() as st2:
            sb2 = lambda nm, shp, dt: st2.enter_context(nc.sbuf_tensor(nm, shp, dt))
            bufs = mk_ln_bufs(sb2, pst)
            bufs["ones_bf"] = ones_bf
            bufs["ones_f32"] = ones_f32
            xn1_s = sb2("xn1_s", [128, NC * NT], BF16)
            evp = [sb2("evA", [128, 512], BF16), sb2("evB", [128, 512], BF16)]
            evv = [sb2("evVA", [128, 512], BF16), sb2("evVB", [128, 512], BF16)]
            ones16 = sb2("ones16_s", [1, 128], F16)

            xn_toks = emit_ln(p, bufs, x_s, xn1_s, NT, ffn_toks + cdeps)
            st_xn1 = p.add("sp", lambda e: e.dma_start(
                out=d3(xn1), in_=s3(xn1_s[:], NT)),
                deps=xn_toks, dma=True)

            def route_q(orow):
                def f(e, ps, ev, orow):
                    return e.activation(out=ev[:], in_=ps[:], func=AF.Copy, bias=0.0,
                                        scale=SCALE)
                return f, q1

            def route_k(orow):
                def f(e, ps, ev, orow):
                    return e.copy(out=ev[:], in_=ps[:])
                return f, k1

            ev_q = mk_store_evac(p, evp, ["evA", "evB"], route_q)
            st_q1 = emit_proj(p, wq1, D, xn1_s, NT, xn_toks, wb, ["wb0", "wb1"],
                              [psA, psB], ["psA", "psB"], ev_q)
            ev_k = mk_store_evac(p, evp, ["evA", "evB"], route_k)
            st_k1 = emit_proj(p, wk1, D, xn1_s, NT, xn_toks, wb, ["wb0", "wb1"],
                              [psA, psB], ["psA", "psB"], ev_k)
            ev_v = mk_store_evac_tm(p, evv, ["evVA", "evVB"], v1)
            st_v1 = emit_proj_tm(p, wv1, D, 0, xn1_s, NT, xn_toks, wb, ["wb0", "wb1"],
                                 [psG, psU], ["psG", "psU"], ev_v)

            p.emit(nc, stack, [st_x1, st_xn1] + st_q1 + st_k1 + st_v1)
    return nc



def build_td():
    nc = bass.Bass("TRN2", target_bir_lowering=False)
    NO = 11264 // 8
    NH = 2048 // 8
    gup = nc.dram_tensor("gup", [NO, 256], I32, kind="ExternalInput").ap()
    gsc = nc.dram_tensor("gsc", [NO, 16], F16, kind="ExternalInput").ap()
    dp = nc.dram_tensor("dp", [NH, 704], I32, kind="ExternalInput").ap()
    dsc = nc.dram_tensor("dsc", [NH, 44], F16, kind="ExternalInput").ap()
    wgu_d = nc.dram_tensor("wgu_d", [NO, D], BF16, kind="ExternalOutput").ap()
    wd_d = nc.dram_tensor("wd_d", [NH, I1], BF16, kind="ExternalOutput").ap()

    with ExitStack() as stack:
        sb = lambda nm, shp, dt: stack.enter_context(nc.sbuf_tensor(nm, shp, dt))
        pk = sb("pk", [128, 704], I32)
        sh = sb("sh", [128, 1408], I16)
        bits = sb("bits", [128, 1408], I16)
        vals = sb("vals", [128, I1], F32)
        w_s = sb("w_s", [128, I1], BF16)
        sc_s = sb("sc_s", [128, 44], F32)

        p = P()
        t_prev = None

        def chunk(src_pk, src_sc, dst, nwords, ngs, t_prev):
            nv = nwords * 2
            nn = nwords * 8
            t_lp = p.add("pool", lambda e: e.dma_start(out=pk[:, 0:nwords], in_=src_pk),
                         deps=[t_prev], dma=True)
            t_ls = p.add("pool", lambda e: e.dma_start(out=sc_s[:, 0:ngs], in_=src_sc),
                         deps=[t_prev], dma=True)
            v16 = pk[:, 0:nwords].bitcast(I16)
            vals4 = vals[:, 0:nn].rearrange("p (n four) -> p n four", four=4)

            def f_deq(e):
                for j in range(4):
                    if j < 3:
                        e.tensor_scalar(out=sh[:, 0:nv], in0=v16, scalar1=12 - 4 * j,
                                        scalar2=None, op0=ALU.logical_shift_left)
                        srcb = sh[:, 0:nv]
                    else:
                        srcb = v16
                    e.tensor_scalar(out=bits[:, 0:nv], in0=srcb, scalar1=3, scalar2=-29184,
                                    op0=ALU.logical_shift_right, op1=ALU.bitwise_and)
                    e.tensor_copy(out=vals4[:, :, j], in_=bits[:, 0:nv].bitcast(F16))
                for g in range(ngs):
                    ins = e.tensor_scalar(out=w_s[:, ts(g, 128)], in0=vals[:, ts(g, 128)],
                                          scalar1=sc_s[:, g:g + 1], scalar2=None,
                                          op0=ALU.mult)
                return ins

            t_dq = p.add("dve", f_deq, deps=[t_lp, t_ls])
            return p.add("sp", lambda e: e.dma_start(out=dst, in_=w_s[:, 0:nn]),
                         deps=[t_dq], dma=True)

        for c in range(11):
            t_prev = chunk(gup[ts(c, 128), :], gsc[ts(c, 128), :], wgu_d[ts(c, 128), :],
                           256, 16, t_prev)
        for c in range(2):
            t_prev = chunk(dp[ts(c, 128), :], dsc[ts(c, 128), :], wd_d[ts(c, 128), :],
                           704, 44, t_prev)

        p.emit(nc, stack, [t_prev])
    return nc



def build_t5(stage="all"):
    nc = bass.Bass("TRN2", target_bir_lowering=False)
    x_fm = nc.dram_tensor("x_fm", [D, NT], F32, kind="ExternalInput").ap()
    ctx_fm = nc.dram_tensor("ctx_fm", [D, NT], BF16, kind="ExternalInput").ap()
    xn1_fm = nc.dram_tensor("xn1_fm", [D, NT], BF16, kind="ExternalInput").ap()
    wo1 = nc.dram_tensor("wo1", [D, D], F32, kind="ExternalInput").ap()
    bo_t = nc.dram_tensor("bo_t", [128, 16], F32, kind="ExternalInput").ap()
    wguT = nc.dram_tensor("wguT", [D, 2 * I1], BF16, kind="ExternalInput").ap()
    wdT = nc.dram_tensor("wdT", [I1, D], BF16, kind="ExternalInput").ap()
    ones_bf_in = nc.dram_tensor("ones_bf", [128, 1], BF16, kind="ExternalInput").ap()
    ones_f32_in = nc.dram_tensor("ones_f32", [1, 128], F32, kind="ExternalInput").ap()
    x2_fm = nc.dram_tensor("x2_fm", [D, NT], F32, kind="ExternalOutput").ap()

    P16 = 16384.0
    NIC = I1 // 128

    with ExitStack() as stack:
        sb = lambda nm, shp, dt: stack.enter_context(nc.sbuf_tensor(nm, shp, dt))
        pst = lambda nm, shp, dt=F32: stack.enter_context(nc.psum_tensor(nm, shp, dt))
        x_s = sb("x_s", [128, NC * NT], F32)
        wb = [sb("wb0", [128, NC * 512], BF16), sb("wb1", [128, NC * 512], BF16)]
        bo_s = sb("bo_s", [128, 16], F32)
        ones_bf = sb("ones_bf_s", [128, 1], BF16)
        ones_f32 = sb("ones_f32_s", [1, 128], F32)
        g_s = sb("g_s", [128, 512], BF16)
        u_s = sb("u_s", [128, 512], BF16)
        sqb = sb("sqb", [128, 512], BF16)
        sqc = sb("sqc", [128, 512], BF16)
        y_s = sb("y_s", [128, 16 * 512], BF16)
        tmpf = sb("tmpf", [128, 512], F32)
        vrow = sb("vrow", [1, 512], F32)
        srow = sb("srow", [1, 512], F32)
        rrow = sb("rrow", [1, 512], F32)
        psA = pst("psA", [128, 512])
        psB = pst("psB", [128, 512])
        psG = pst("psG", [128, 512])
        psU = pst("psU", [128, 512])
        ps_row = pst("ps_row", [128, 512])
        ps_bc = pst("ps_bc", [128, 512])

        p = P()
        tok_x = p.add("pool", lambda e: e.dma_start(
            out=s3(x_s[:], NT), in_=d3(x_fm)), dma=True)
        cdeps = []
        for dst, src in ((bo_s, bo_t), (ones_bf, ones_bf_in), (ones_f32, ones_f32_in)):
            cdeps.append(p.add("sp", lambda e, d=dst, s=src: e.dma_start(out=d[:], in_=s),
                               dma=True))

        with nc.sbuf_tensor("ctx_s", [128, NC * NT], BF16) as ctx_s:
            tok_ctx = p.add("pool", lambda e: e.dma_start(
                out=s3(ctx_s[:], NT), in_=d3(ctx_fm)), dma=True)

            def ev_oproj(p_, deps, ps, pskey, orow, tg):
                oc = orow // 128
                sl = slice(oc * NT + tg * 512, oc * NT + (tg + 1) * 512)
                return p_.add("dve", lambda e, ps=ps, sl=sl, oc=oc: e.scalar_tensor_tensor(
                    out=x_s[:, sl], in0=ps[:], scalar=bo_s[:, oc:oc + 1], in1=x_s[:, sl],
                    op0=ALU.add, op1=ALU.add), deps=deps + [tok_x] + cdeps)

            oproj_toks = emit_proj(p, wo1, D, ctx_s, NT, [tok_ctx],
                                   wb, ["wb0", "wb1"], [psA, psB], ["psA", "psB"], ev_oproj)
        p.set("oproj", oproj_toks[-1])

        out_toks = []
        if stage == "oproj":
            st_x2 = p.add("sp", lambda e: e.dma_start(
                out=d3(x2_fm), in_=s3(x_s[:], NT)), deps=oproj_toks, dma=True)
            p.emit(nc, stack, [st_x2])
            return nc
        with nc.sbuf_tensor("xn1_s", [128, NC * NT], BF16) as xn1_s, \
             nc.sbuf_tensor("hf", [128, NIC * 512], BF16) as hf:
            tok_xn1 = p.add("pool", lambda e: e.dma_start(
                out=s3(xn1_s[:], NT), in_=d3(xn1_fm)), deps=p.use("oproj"), dma=True)
            for tb in range(2):
                fuse_toks = []
                nicg = 3 if stage == "gate3" else 11
                for icg in range(nicg):
                    t_wg = p.add("pool", lambda e, icg=icg: e.dma_start(
                        out=s3(wb[0][:], 512), in_=d3(wguT[:, ts(icg, 512)])),
                        deps=p.use("wb0", "oproj"), dma=True)
                    t_wu = p.add("pool", lambda e, icg=icg: e.dma_start(
                        out=s3(wb[1][:], 512), in_=d3(wguT[:, I1 + icg * 512: I1 + (icg + 1) * 512])),
                        deps=p.use("wb1", "oproj"), dma=True)
                    mmg = []
                    mmu = []
                    for j4 in range(4):
                        ic = icg * 4 + j4

                        def f_g(e, j4=j4, tb=tb):
                            for dc in range(NC):
                                ins = e.matmul(psG[:], wb[0][:, dc * 512 + j4 * 128: dc * 512 + (j4 + 1) * 128],
                                               xn1_s[:, dc * NT + tb * 512: dc * NT + (tb + 1) * 512],
                                               start=(dc == 0), stop=(dc == NC - 1))
                            return ins

                        def f_u(e, j4=j4, tb=tb):
                            for dc in range(NC):
                                ins = e.matmul(psU[:], wb[1][:, dc * 512 + j4 * 128: dc * 512 + (j4 + 1) * 128],
                                               xn1_s[:, dc * NT + tb * 512: dc * NT + (tb + 1) * 512],
                                               start=(dc == 0), stop=(dc == NC - 1))
                            return ins

                        t_g = p.add("pe", f_g, deps=[t_wg, tok_xn1] + p.use("psG"))
                        t_u = p.add("pe", f_u, deps=[t_wu, tok_xn1] + p.use("psU"))
                        mmg.append(t_g)
                        mmu.append(t_u)
                        t_sil = p.add("act", lambda e: e.activation(
                            out=g_s[:], in_=psG[:], func=AF.Silu, bias=0.0, scale=P16),
                            deps=[t_g] + p.use("gsbuf"))
                        t_ue = p.add("dve", lambda e: e.tensor_scalar(
                            out=u_s[:], in0=psU[:], scalar1=P16, scalar2=None, op0=ALU.mult),
                            deps=[t_u] + p.use("usbuf"))
                        p.set("psG", t_sil)
                        p.set("psU", t_ue)
                        t_f = p.add("dve", lambda e, ic=ic: e.tensor_tensor(
                            out=hf[:, ts(ic, 512)], in0=g_s[:], in1=u_s[:], op=ALU.mult),
                            deps=[t_sil, t_ue] + p.use("hfrd"))
                        p.set("gsbuf", t_f)
                        p.set("usbuf", t_f)
                        fuse_toks.append(t_f)
                    p.set("wb0", mmg[-1])
                    p.set("wb1", mmu[-1])
                if stage == "gate3":
                    st_x2 = p.add("sp", lambda e: e.dma_start(
                        out=d3(x2_fm), in_=s3(x_s[:], NT)), deps=fuse_toks, dma=True)
                    p.emit(nc, stack, [st_x2])
                    return nc
                dn_toks = []
                for oc in range(16):
                    wbi = oc % 2
                    t_wd = p.add("pool", lambda e, oc=oc, wbi=wbi: e.dma_start(
                        out=s3(wb[wbi][:, 0:NIC * 128], 128), in_=wdT[:, ts(oc, 128)].rearrange("(c p) o -> p c o", p=128)),
                        deps=p.use(f"wb{wbi}") + fuse_toks[-1:], dma=True)
                    ps = psG if wbi == 0 else psU
                    pskey = "psG" if wbi == 0 else "psU"

                    def f_d(e, ps=ps, wbi=wbi):
                        for ic in range(NIC):
                            ins = e.matmul(ps[:], wb[wbi][:, ts(ic, 128)], hf[:, ts(ic, 512)],
                                           start=(ic == 0), stop=(ic == NIC - 1))
                        return ins

                    t_d = p.add("pe", f_d, deps=[t_wd] + p.use(pskey) + fuse_toks)
                    p.set(f"wb{wbi}", t_d)
                    t_ye = p.add("dve", lambda e, ps=ps, oc=oc: e.tensor_scalar(
                        out=y_s[:, ts(oc, 512)], in0=ps[:], scalar1=P16, scalar2=None,
                        op0=ALU.mult), deps=[t_d])
                    p.set(pskey, t_ye)
                    dn_toks.append(t_ye)
                sq_toks = []
                sqbufs = [sqb, sqc]
                for oc in range(16):
                    t_sq = p.add("act", lambda e, oc=oc, b=sqbufs[oc % 2]: e.activation(
                        out=b[:, 0:512], in_=y_s[:, ts(oc, 512)], func=AF.Square,
                        bias=0.0, scale=1.0),
                        deps=[dn_toks[oc]] + p.use(f"sqb{oc % 2}"))
                    sq_toks.append(t_sq)

                def f_rm(e):
                    for oc in range(16):
                        ins = e.matmul(ps_row[0:1, :], ones_bf[:, 0:1],
                                       sqbufs[oc % 2][:, 0:512].bitcast(BF16) if False else sqbufs[oc % 2][:, 0:512],
                                       start=(oc == 0), stop=(oc == 15))
                    return ins

                t_rm = p.add("pe", f_rm, deps=sq_toks + p.use("psrow"))
                p.set("sqb0", t_rm)
                p.set("sqb1", t_rm)
                rm_toks = [t_rm]

                def f_vrow(e):
                    return e.tensor_scalar(out=vrow[:], in0=ps_row[0:1, :], scalar1=1.0 / D,
                                           scalar2=RMS_EPS, op0=ALU.mult, op1=ALU.add)

                t_vr = p.add("dve", f_vrow, deps=[rm_toks[-1]])
                p.set("psrow", t_vr)
                t_sr = p.add("act", lambda e: e.activation(out=srow[:], in_=vrow[:],
                                                           func=AF.Sqrt), deps=[t_vr])
                t_rc = p.add("dve", lambda e: e.reciprocal(out=rrow[:], in_=srow[:]),
                             deps=[t_sr])
                t_bc = p.add("pe", lambda e: e.matmul(ps_bc[:], ones_f32[0:1, :], rrow[:],
                                                      start=True, stop=True),
                             deps=[t_rc] + p.use("psbc"))

                def f_fin(e, tb=tb):
                    for oc in range(16):
                        sl = slice(oc * NT + tb * 512, oc * NT + (tb + 1) * 512)
                        e.tensor_tensor(out=tmpf[:], in0=y_s[:, ts(oc, 512)], in1=ps_bc[:],
                                        op=ALU.mult)
                        ins = e.tensor_tensor(out=x_s[:, sl], in0=x_s[:, sl], in1=tmpf[:],
                                              op=ALU.add)
                    return ins

                t_fin = p.add("dve", f_fin, deps=[t_bc] + dn_toks)
                p.set("psbc", t_fin)
                out_toks.append(t_fin)
                if stage == "tb0":
                    break

        st_x2 = p.add("sp", lambda e: e.dma_start(
            out=d3(x2_fm), in_=s3(x_s[:], NT)),
            deps=out_toks, dma=True)
        p.emit(nc, stack, [st_x2])
    return nc



def host_consts():
    slopes = (2.0 ** (-0.5 * np.arange(1, H + 1))).astype(np.float64)
    k = np.arange(S, dtype=np.float64)
    bias0 = slopes[:, None] * (k[None, :] - (S - 1))
    A0 = np.round(bias0)
    bias1 = slopes[:, None] * k[None, :] / math.sqrt(HD)
    A1 = np.round(bias1)
    mask = np.where(np.arange(128)[None, :] <= np.arange(128)[:, None], 0.0, NEG)
    return {
        "A0": A0.astype(np.float16), "w0": np.exp(bias0 - A0).astype(np.float32),
        "c0": (-bias0).astype(np.float32),
        "A1": A1.astype(np.float16), "w1": np.exp(bias1 - A1).astype(np.float32),
        "c1": (-bias1).astype(np.float32),
        "mask": mask.astype(np.float32),
        "ident": np.eye(128, dtype=BF),
        "ones_bf": np.ones((128, 1), dtype=BF),
        "ones_f32": np.ones((1, 128), dtype=np.float32),
        "ones_f16": np.ones((1, 128), dtype=np.float16),
    }


def attn_core_consts(C, layer, g):
    A = C["A0"] if layer == 0 else C["A1"]
    w = C["w0"] if layer == 0 else C["w1"]
    c = C["c0"] if layer == 0 else C["c1"]
    hs = slice(g * 8, g * 8 + 8)
    a_arr = np.ascontiguousarray(A[hs].reshape(1, 8 * S))
    w_arr = np.ascontiguousarray(
        w[hs].reshape(8, 16, 128).transpose(2, 0, 1).reshape(128, 8 * 16))
    c_arr = np.ascontiguousarray(
        c[hs].reshape(8, 16, 128).transpose(2, 0, 1).reshape(128, 8 * 16))
    return a_arr, w_arr, c_arr


def col_t(v, n):
    return np.ascontiguousarray(v.reshape(n, 128).T)


_cache = {}


def _get(name, builder):
    if name not in _cache:
        _cache[name] = builder()
    return _cache[name]


def kernel(**inp):
    if FUSED:
        return kernel_fused(inp)
    f32 = np.float32
    X = np.ascontiguousarray(np.asarray(inp["L0_attn_x"], dtype=f32).reshape(B * S, D))
    C = host_consts()

    wqkvT = np.ascontiguousarray(np.asarray(inp["L0_attn_Wqkv"], f32).T)
    b_qkv = np.asarray(inp["L0_attn_b_qkv"], f32)
    woutT = np.ascontiguousarray(np.asarray(inp["L0_attn_Wout"], f32).T)
    b_out = np.asarray(inp["L0_attn_b_out"], f32)
    wg8 = np.ascontiguousarray(np.asarray(inp["L0_ffn_w_gate"]))
    wu8 = np.ascontiguousarray(np.asarray(inp["L0_ffn_w_up"]))
    wd8 = np.ascontiguousarray(np.asarray(inp["L0_ffn_w_down"]))
    sg = np.asarray(inp["L0_ffn_s_gate"], f32)
    su = np.asarray(inp["L0_ffn_s_up"], f32)
    sd = np.asarray(inp["L0_ffn_s_down"], f32)
    wq1 = np.ascontiguousarray(np.asarray(inp["L1_attn_Wq"], f32))
    wk1 = np.ascontiguousarray(np.asarray(inp["L1_attn_Wk"], f32))
    wv1 = np.ascontiguousarray(np.asarray(inp["L1_attn_Wv"], f32))
    wo1 = np.ascontiguousarray(np.asarray(inp["L1_attn_Wo"], f32))
    bo1 = np.asarray(inp["L1_attn_bo"], f32)
    gup = np.ascontiguousarray(np.asarray(inp["L1_ffn_gate_up_packed"]))
    gsc = np.ascontiguousarray(np.asarray(inp["L1_ffn_gate_up_scales"]).T)
    dp = np.ascontiguousarray(np.asarray(inp["L1_ffn_down_packed"]))
    dsc = np.ascontiguousarray(np.asarray(inp["L1_ffn_down_scales"]).T)

    x_slices = [np.ascontiguousarray(X[ts(c, NT), :].T) for c in range(8)]

    nc1 = _get("t1", build_t1)
    bq_t = col_t(b_qkv, 48)
    maps1 = [{
        "x_fm": x_slices[c], "wqkvT": wqkvT, "bq_t": bq_t,
        "bv_row": np.ascontiguousarray(b_qkv[2 * D:].reshape(1, D)),
        "ones_bf": C["ones_bf"], "ones_f32": C["ones_f32"], "ones_f16": C["ones_f16"],
    } for c in range(8)]
    r1 = run_bass_kernel_spmd(nc1, maps1, core_ids=list(range(8))).results

    Q0 = [np.concatenate([r1[2 * b]["q0"], r1[2 * b + 1]["q0"]], axis=1) for b in range(B)]
    K0 = [np.concatenate([r1[2 * b]["k0"], r1[2 * b + 1]["k0"]], axis=1) for b in range(B)]
    V0 = [np.concatenate([r1[2 * b]["v0"], r1[2 * b + 1]["v0"]], axis=0) for b in range(B)]

    nca = _get("attn", build_attn)
    maps2 = []
    for c in range(8):
        b, g = c // 2, c % 2
        a_arr, w_arr, c_arr = attn_core_consts(C, 0, g)
        maps2.append({
            "q_in": np.ascontiguousarray(Q0[b][ts(g, 1024), :]),
            "k_in": np.ascontiguousarray(K0[b][ts(g, 1024), :]),
            "v_in": np.ascontiguousarray(V0[b][:, ts(g, 1024)]),
            "a_in": a_arr, "w_in": w_arr, "c_in": c_arr,
            "mask_in": C["mask"], "ident_in": C["ident"], "ones_f16": C["ones_f16"],
        })
    r2 = run_bass_kernel_spmd(nca, maps2, core_ids=list(range(8))).results
    CTX0 = [np.concatenate([r2[2 * b]["ctx_out"], r2[2 * b + 1]["ctx_out"]], axis=1)
            for b in range(B)]

    ncd = _get("td", build_td)
    mapsd = [{
        "gup": np.ascontiguousarray(gup[ts(c, 1408), :]),
        "gsc": np.ascontiguousarray(gsc[ts(c, 1408), :]),
        "dp": np.ascontiguousarray(dp[ts(c, 256), :]),
        "dsc": np.ascontiguousarray(dsc[ts(c, 256), :]),
    } for c in range(8)]
    rd = run_bass_kernel_spmd(ncd, mapsd, core_ids=list(range(8))).results
    wgu_dense = np.concatenate([rd[c]["wgu_d"] for c in range(8)], axis=0)
    wd_dense = np.concatenate([rd[c]["wd_d"] for c in range(8)], axis=0)
    wguT = np.ascontiguousarray(wgu_dense.T)
    wdT = np.ascontiguousarray(wd_dense.T)

    nc3 = _get("t3", build_t3)
    maps3 = []
    for c in range(8):
        b = c // 2
        ctxb = CTX0[b].T
        maps3.append({
            "x_fm": x_slices[c],
            "ctx_fm": np.ascontiguousarray(ctxb[:, ts(c % 2, 1024)]),
            "xn0_fm": r1[c]["xn0"],
            "woutT": woutT, "bout_t": col_t(b_out, 16),
            "wg8": wg8, "wu8": wu8, "wd8": wd8,
            "sg_t": col_t(sg, 64), "su_t": col_t(su, 64), "sd_t": col_t(sd, 16),
            "wq1": wq1, "wk1": wk1, "wv1": wv1,
            "ones_bf": C["ones_bf"], "ones_f32": C["ones_f32"],
        })
    r3 = run_bass_kernel_spmd(nc3, maps3, core_ids=list(range(8))).results

    Q1 = [np.concatenate([r3[2 * b]["q1"], r3[2 * b + 1]["q1"]], axis=1) for b in range(B)]
    K1 = [np.concatenate([r3[2 * b]["k1"], r3[2 * b + 1]["k1"]], axis=1) for b in range(B)]
    V1 = [np.concatenate([r3[2 * b]["v1"], r3[2 * b + 1]["v1"]], axis=0) for b in range(B)]

    maps4 = []
    for c in range(8):
        b, g = c // 2, c % 2
        a_arr, w_arr, c_arr = attn_core_consts(C, 1, g)
        maps4.append({
            "q_in": np.ascontiguousarray(Q1[b][ts(g, 1024), :]),
            "k_in": np.ascontiguousarray(K1[b][ts(g, 1024), :]),
            "v_in": np.ascontiguousarray(V1[b][:, ts(g, 1024)]),
            "a_in": a_arr, "w_in": w_arr, "c_in": c_arr,
            "mask_in": C["mask"], "ident_in": C["ident"], "ones_f16": C["ones_f16"],
        })
    r4 = run_bass_kernel_spmd(nca, maps4, core_ids=list(range(8))).results
    CTX1 = [np.concatenate([r4[2 * b]["ctx_out"], r4[2 * b + 1]["ctx_out"]], axis=1)
            for b in range(B)]

    nc5 = _get("t5", build_t5)
    maps5 = []
    for c in range(8):
        b = c // 2
        ctxb = CTX1[b].T
        maps5.append({
            "x_fm": r3[c]["x1_fm"],
            "ctx_fm": np.ascontiguousarray(ctxb[:, ts(c % 2, 1024)]),
            "xn1_fm": r3[c]["xn1"],
            "wo1": wo1, "bo_t": col_t(bo1, 16),
            "wguT": wguT, "wdT": wdT,
            "ones_bf": C["ones_bf"], "ones_f32": C["ones_f32"],
        })
    r5 = run_bass_kernel_spmd(nc5, maps5, core_ids=list(range(8))).results

    X2 = np.empty((B * S, D), dtype=f32)
    for c in range(8):
        X2[ts(c, NT), :] = r5[c]["x2_fm"].T
    return X2.reshape(B, S, D)





def emit_ln_stream(p, nc, st, x_dram, xn_s, pst0, ones_bf, ones_f32, bar, tag):
    lnst = st.enter_context(ExitStack())
    sb = lambda nm, shp, dt: st.enter_context(nc.sbuf_tensor(tag + nm, shp, dt))
    pst = lambda nm, shp, dt=F32: lnst.enter_context(nc.psum_tensor(tag + nm, shp, dt))
    xtg = [sb("ln_xtg0", [128, NC * 512], BF16), sb("ln_xtg1", [128, NC * 512], BF16)]
    bufs = mk_ln_bufs(sb, pst, with_xb=False)
    bufs["ones_bf"] = ones_bf
    bufs["ones_f32"] = ones_f32
    sq = bufs["ln_sq"]
    mrow, vrow, srow, rrow, msq = (bufs["ln_mrow"], bufs["ln_vrow"],
                                   bufs["ln_srow"], bufs["ln_rrow"], bufs["ln_msq"])
    ps_r0, ps_r1 = bufs["ps_row0"], bufs["ps_row1"]
    ps_bcm, ps_bcr = bufs["ps_bcm"], bufs["ps_bcr"]
    tmp = bufs["ln_tmp"]
    toks = []
    for tg in range(S // 512):
        xs = xtg[tg % 2]
        xb = xs
        t_x = p.add("pool", lambda e, tg=tg, xs=xs: e.dma_start(
            out=s3(xs[:], 512), in_=d3(x_dram[:, ts(tg, 512)])),
            deps=bar + p.use(tag + f"lnx{tg % 2}"), dma=True)

        def f_xb(e, xs=xs):
            for dc in range(NC):
                ins = e.activation(out=sq[:, ts(dc, 512)], in_=xs[:, ts(dc, 512)],
                                   func=AF.Square)
            return ins

        t_xb = p.add("act", f_xb, deps=[t_x] + p.use(tag + "ln_red"))

        def f_sum(e, xb=xb):
            for dc in range(NC):
                e.matmul(ps_r0[0:1, :], ones_bf[:, 0:1], xb[:, ts(dc, 512)],
                         start=(dc == 0), stop=(dc == NC - 1))
            for dc in range(NC):
                ins = e.matmul(ps_r1[0:1, :], ones_bf[:, 0:1], sq[:, ts(dc, 512)],
                               start=(dc == 0), stop=(dc == NC - 1))
            return ins

        t_red = p.add("pe", f_sum, deps=[t_xb] + p.use(tag + "ln_rows") + bar)
        p.set(tag + "ln_red", t_red)

        def f_rows(e):
            e.tensor_scalar(out=mrow[:], in0=ps_r0[0:1, :], scalar1=1.0 / D,
                            scalar2=None, op0=ALU.mult)
            e.tensor_tensor(out=msq[:], in0=mrow[:], in1=mrow[:], op=ALU.mult)
            e.tensor_scalar(out=vrow[:], in0=ps_r1[0:1, :], scalar1=1.0 / D,
                            scalar2=LN_EPS, op0=ALU.mult, op1=ALU.add)
            return e.tensor_tensor(out=vrow[:], in0=vrow[:], in1=msq[:], op=ALU.subtract)

        t_rows = p.add("dve", f_rows, deps=[t_red])
        p.set(tag + "ln_rows", t_rows)
        t_sqrt = p.add("act", lambda e: e.activation(out=srow[:], in_=vrow[:], func=AF.Sqrt),
                       deps=[t_rows])
        t_rcp = p.add("dve", lambda e: e.reciprocal(out=rrow[:], in_=srow[:]), deps=[t_sqrt])

        def f_bc(e):
            e.matmul(ps_bcm[:], ones_f32[0:1, :], mrow[:], start=True, stop=True)
            return e.matmul(ps_bcr[:], ones_f32[0:1, :], rrow[:], start=True, stop=True)

        t_bc = p.add("pe", f_bc, deps=[t_rcp, t_rows] + p.use(tag + "ln_xn"))

        def f_xn(e, tg=tg, xs=xs):
            for dc in range(NC):
                sl = slice(dc * S + tg * 512, dc * S + tg * 512 + 512)
                e.tensor_tensor(out=tmp[:], in0=xs[:, ts(dc, 512)], in1=ps_bcm[:],
                                op=ALU.subtract)
                ins = e.tensor_tensor(out=xn_s[:, sl], in0=tmp[:], in1=ps_bcr[:],
                                      op=ALU.mult)
            return ins

        t_xn = p.add("dve", f_xn, deps=[t_bc, t_x])
        p.set(tag + "ln_xn", t_xn)
        p.set(tag + f"lnx{tg % 2}", t_xn)
        toks.append(t_xn)
    lnst.close()
    return toks


def emit_attn_stage(p, nc, st, pst, q_d, k_d, v_d, ctx_d, a_d, w_s, c_s, mask_s,
                    ident, ones16, bar, tag, interleave_cb=None, skew=True):
    global ATTN_SKEW_MODE
    """Full causal attention DRAM->DRAM, ctx stored fm. 3-stage skewed
    software pipeline: scores/exp(i) || transposes(i-1) || ctx/store(i-2)."""
    sb = lambda nm, shp, dt: st.enter_context(nc.sbuf_tensor(tag + nm, shp, dt))
    NKC = S // 128
    qh = [sb("qh0", [128, S], BF16), sb("qh1", [128, S], BF16)]
    kh = [sb("kh0", [128, S], BF16), sb("kh1", [128, S], BF16)]
    vh = [sb("vh0", [128, NKC * 129], BF16), sb("vh1", [128, NKC * 129], BF16)]
    p_s = [sb("p_s0", [128, S], BF16), sb("p_s1", [128, S], BF16)]
    pt_s = [sb("pt_s0", [128, S], BF16), sb("pt_s1", [128, S], BF16),
            sb("pt_s2", [128, S], BF16)]
    a_hs = [sb("ah0", [1, S], F16), sb("ah1", [1, S], F16)]
    rcp = sb("rcp", [128, 1], F32)
    ctx_ev = [sb("ctxev0", [128, 128], BF16), sb("ctxev1", [128, 128], BF16)]
    ctxh = [sb("ctxh0", [128, S], BF16), sb("ctxh1", [128, S], BF16)]
    store_toks = []
    ps_sc = [pst(tag + "ps_sc0", [128, 512]), pst(tag + "ps_sc1", [128, 512])]
    ps_pt = [pst(tag + "ps_ptA", [128, 1024], BF16), pst(tag + "ps_ptB", [128, 1024], BF16)]
    ps_cx = [pst(tag + "ps_cx0", [128, 512]), pst(tag + "ps_cx1", [128, 512])]

    t_ones_v = [p.add("dve", lambda e, b=b: e.memset(vh[b][:], 1.0), deps=bar)
                for b in range(2)]
    csc = [0]
    gq = [0]

    def emit_sc(h, qt, bsel, t_load, gi):
        klen = (qt + 1) * 128
        nch = (klen + 511) // 512
        par = gi % 2
        t_exps = []
        for c in range(nch):
            w_ = min(512, klen - c * 512)
            pi = csc[0] % 2
            csc[0] += 1
            psc = ps_sc[pi]
            sckey = tag + f"sc{pi}"

            def f_sc(e, h=h, qt=qt, c=c, w_=w_, bsel=bsel, psc=psc):
                e.matmul(psc[:, 0:w_], qh[bsel][:, ts(qt, 128)],
                         kh[bsel][:, c * 512: c * 512 + w_],
                         start=True, stop=False)
                return e.matmul(psc[:, 0:w_], ones16[0:1, :],
                                a_hs[bsel][0:1, c * 512: c * 512 + w_],
                                start=False, stop=True)

            t_sc = p.add("pe", f_sc, deps=t_load + p.use(sckey) + bar)
            if qt // 4 == c:
                off = (qt % 4) * 128
                t_sc = p.add("dve", lambda e, psc=psc, off=off: e.tensor_tensor(
                    out=psc[:, off:off + 128], in0=psc[:, off:off + 128],
                    in1=mask_s[:], op=ALU.add), deps=[t_sc])
            t_exp = p.add("act", lambda e, h=h, qt=qt, c=c, w_=w_, psc=psc, par=par: e.activation(
                out=p_s[par][:, c * 512: c * 512 + w_], in_=psc[:, 0:w_], func=AF.Exp,
                bias=c_s[:, h * 16 + qt: h * 16 + qt + 1], scale=1.0),
                deps=[t_sc] + p.use(tag + f"pexp{par}"))
            p.set(sckey, t_exp)
            t_exps.append(t_exp)
        return t_exps

    def emit_tr(h, qt, gi, t_exps):
        par = gi % 2
        par3 = gi % 3
        nkc = qt + 1
        pspt = ps_pt[gi % 2]
        t_ptev_list = []
        for kc4 in range((nkc + 3) // 4):
            kc_lo = kc4 * 4
            kc_hi = min(kc_lo + 4, nkc)
            exps_dep = [t_exps[min(kc4, len(t_exps) - 1)]]

            def f_tr(e, kc_lo=kc_lo, kc_hi=kc_hi, par=par, pspt=pspt):
                for kc in range(kc_lo, kc_hi):
                    ins = e.transpose(pspt[:, ts(kc % 4, 128)], p_s[par][:, ts(kc, 128)],
                                      ident[:])
                return ins

            t_tr = p.add("pe", f_tr, deps=exps_dep + p.use(tag + f"ptps{gi % 2}"))

            def f_ptev(e, kc_lo=kc_lo, kc_hi=kc_hi, h=h, par3=par3, pspt=pspt):
                for kc in range(kc_lo, kc_hi):
                    ins = e.tensor_scalar(
                        out=pt_s[par3][:, ts(kc, 128)], in0=pspt[:, ts(kc % 4, 128)],
                        scalar1=w_s[:, h * 16 + kc: h * 16 + kc + 1],
                        scalar2=None, op0=ALU.mult)
                return ins

            t_ptev = p.add("dve", f_ptev, deps=[t_tr] + p.use(tag + f"ptw{par3}"))
            p.set(tag + f"ptps{gi % 2}", t_ptev)
            t_ptev_list.append(t_ptev)
        p.set(tag + f"pexp{par}", t_ptev_list[-1])
        return t_ptev_list

    def emit_back(h, qt, bsel, gi, t_ptev_list, t_load):
        par3 = gi % 3
        nkc = qt + 1
        pcx = ps_cx[gi % 2]
        cxkey = tag + f"cx{gi % 2}"

        def f_cx(e, nkc=nkc, bsel=bsel, pcx=pcx, par3=par3):
            for kc in range(nkc):
                ins = e.matmul(pcx[:, 0:129], pt_s[par3][:, ts(kc, 128)],
                               vh[bsel][:, kc * 129:(kc + 1) * 129],
                               start=(kc == 0), stop=(kc == nkc - 1))
            return ins

        t_cx = p.add("pe", f_cx, deps=t_ptev_list + t_load + p.use(cxkey))
        p.set(tag + f"ptw{par3}", t_cx)
        t_rcp = p.add("dve", lambda e, pcx=pcx: e.reciprocal(
            out=rcp[:], in_=pcx[:, 128:129]), deps=[t_cx] + p.use(tag + "rcp"))
        cev = ctx_ev[gi % 2]
        t_cev = p.add("act", lambda e, pcx=pcx, cev=cev: e.activation(
            out=cev[:], in_=pcx[:, 0:128], func=AF.Identity, bias=0.0,
            scale=rcp[:]), deps=[t_rcp] + p.use(tag + f"cev{gi % 2}"))
        p.set(tag + "rcp", t_cev)
        p.set(cxkey, t_cev)
        t_ctr = p.add("pe", lambda e, cev=cev, pspt=ps_pt[gi % 2]: e.transpose(
            pspt[:, 512:640], cev[:], ident[:]),
            deps=[t_cev] + p.use(tag + f"ctps{gi % 2}"))
        p.set(tag + f"cev{gi % 2}", t_ctr)
        t_ctev = p.add("dve", lambda e, h=h, qt=qt, pspt=ps_pt[gi % 2]: e.tensor_copy(
            out=ctxh[h % 2][:, ts(qt, 128)], in_=pspt[:, 512:640]),
            deps=[t_ctr] + p.use(tag + f"ctxh{h % 2}_w"))
        p.set(tag + f"ctps{gi % 2}", t_ctev)
        p.set(tag + f"ctxh{h % 2}_w", t_ctev)
        if qt == NKC - 1:
            t_st = p.add("sp", lambda e, h=h: e.dma_start(
                out=ctx_d[ts(h, 128), :], in_=ctxh[h % 2][:]),
                deps=[t_ctev], dma=True)
            p.set(tag + f"ctxh{h % 2}_w", t_st)
            store_toks.append(t_st)
            return t_st
        return t_ctev

    head_backs = {hh: [] for hh in range(H)}
    pend_tr = []
    pend_back = []

    def advance(drain=False):
        back_keep = 0 if (drain or ATTN_SKEW_MODE != 2) else 1
        while len(pend_back) > back_keep:
            hb, qb, bb, gb, tpl, tlb = pend_back.pop(0)
            head_backs[hb].append(emit_back(hb, qb, bb, gb, tpl, tlb))
        while pend_tr if drain else len(pend_tr) > 1:
            ht, qt_, bt_, gt, te, tlt = pend_tr.pop(0)
            pend_back.append((ht, qt_, bt_, gt, emit_tr(ht, qt_, gt, te), tlt))
        if drain or ATTN_SKEW_MODE != 2:
            while pend_back:
                hb, qb, bb, gb, tpl, tlb = pend_back.pop(0)
                head_backs[hb].append(emit_back(hb, qb, bb, gb, tpl, tlb))

    for h in range(H):
        bsel = h % 2
        dep_war = (head_backs.get(h - 2, [])[-2:] + [t_ones_v[bsel]] + bar)
        t_lq = p.add("pool", lambda e, h=h, bsel=bsel: e.dma_start(
            out=qh[bsel][:], in_=q_d[ts(h, 128), :]), deps=dep_war, dma=True)
        t_lk = p.add("pool", lambda e, h=h, bsel=bsel: e.dma_start(
            out=kh[bsel][:], in_=k_d[ts(h, 128), :]), deps=dep_war, dma=True)
        t_lv = p.add("pool", lambda e, h=h, bsel=bsel: e.dma_start(
            out=vh[bsel][:].rearrange("p (c x) -> p c x", x=129)[:, :, 0:128],
            in_=v_d[:, ts(h, 128)].rearrange("(c p) v -> p c v", p=128)),
            deps=dep_war, dma=True)
        t_la = p.add("sp", lambda e, h=h, bsel=bsel: e.dma_start(
            out=a_hs[bsel][:], in_=a_d[h:h + 1, :]), deps=dep_war, dma=True)
        t_load = [t_lq, t_lk, t_lv, t_la]

        for qt in range(16):
            gi = gq[0]
            gq[0] += 1
            te = emit_sc(h, qt, bsel, t_load, gi)
            pend_tr.append((h, qt, bsel, gi, te, t_load))
            if skew:
                advance()
            else:
                advance(drain=True)
        if interleave_cb is not None:
            interleave_cb(h)
    advance(drain=True)
    return store_toks


def emit_dq_make(p, nc, st, gup, gscT, dp, dscT, wgu_dd, wd_dd, ident, cdeps):
    sb = lambda nm, shp, dt: st.enter_context(nc.sbuf_tensor(nm, shp, dt))
    pk = sb("dq_pk", [128, 352], I32)
    sh = sb("dq_sh", [128, 704], I16)
    bits = sb("dq_bits", [128, 4 * 704], I16)
    vals = sb("dq_vals", [128, 2816], BF16)
    w_sc = sb("dq_wsc", [128, 2816], BF16)
    wT = vals
    sc_s = sb("dq_sc", [128, 22], F32)
    ps_dq = st.enter_context(nc.psum_tensor("ps_dq", [128, 1024], BF16))

    def chunk(src_pk, src_sc, dst_view, nwords, ngs, n_tr):
        nv = nwords * 2
        nn = nwords * 8
        t_lp = p.add("pool", lambda e: e.dma_start(out=pk[:, 0:nwords], in_=src_pk),
                     deps=p.use("dq_pk") + cdeps, dma=True)
        t_ls = p.add("pool", lambda e: e.dma_start(out=sc_s[:, 0:ngs], in_=src_sc),
                     deps=p.use("dq_sc") + cdeps, dma=True)
        v16 = pk[:, 0:nwords].bitcast(I16)
        vals4 = vals[:, 0:nn].rearrange("p (n four) -> p n four", four=4)

        def f_bits(e):
            for j in range(4):
                if j < 3:
                    e.tensor_scalar(out=sh[:, 0:nv], in0=v16, scalar1=12 - 4 * j,
                                    scalar2=None, op0=ALU.logical_shift_left)
                    srcb = sh[:, 0:nv]
                else:
                    srcb = v16
                ins = e.tensor_scalar(out=bits[:, j * nv:(j + 1) * nv], in0=srcb,
                                      scalar1=3, scalar2=-29184,
                                      op0=ALU.logical_shift_right, op1=ALU.bitwise_and)
            return ins

        t_sh = p.add("dve", f_bits, deps=[t_lp] + p.use("dq_bits"))
        p.set("dq_pk", t_sh)

        def f_conv(e):
            for j in range(4):
                ins = e.copy(out=vals4[:, :, j], in_=bits[:, j * nv:(j + 1) * nv].bitcast(F16))
            return ins

        t_b = p.add("act", f_conv, deps=[t_sh] + p.use("dq_vals"))
        p.set("dq_bits", t_b)

        def f_scale(e, ngs=ngs):
            for g in range(ngs):
                ins = e.activation(out=w_sc[:, ts(g, 128)], in_=vals[:, ts(g, 128)],
                                   func=AF.Identity, bias=0.0,
                                   scale=sc_s[:, g:g + 1])
            return ins

        t_s = p.add("act", f_scale, deps=[t_b, t_ls] + p.use("dq_wsc"))
        p.set("dq_vals", t_s)
        p.set("dq_sc", t_s)
        tr_toks = []
        for g8 in range((n_tr + 7) // 8):
            def f_tr(e, g8=g8, n_tr=n_tr):
                for j in range(8):
                    g = g8 * 8 + j
                    if g >= n_tr:
                        break
                    ins = e.transpose(ps_dq[:, ts(j, 128)], w_sc[:, ts(g, 128)], ident[:])
                return ins

            t_tr = p.add("pe", f_tr, deps=[t_s] + p.use("dqpt"))

            def f_tev(e, g8=g8, n_tr=n_tr):
                for j in range(8):
                    g = g8 * 8 + j
                    if g >= n_tr:
                        break
                    ins = e.tensor_copy(out=wT[:, ts(g, 128)], in_=ps_dq[:, ts(j, 128)])
                return ins

            t_tev = p.add("dve", f_tev, deps=[t_tr] + p.use("dq_wT"))
            p.set("dqpt", t_tev)
            tr_toks.append(t_tev)
        t_st = p.add("sp", lambda e: e.dma_start(out=dst_view, in_=wT[:, 0:n_tr * 128]),
                     deps=tr_toks, dma=True)
        p.set("dq_wT", t_st)
        p.set("dq_wsc", t_st)
        p.set("dq_vals", t_st)
        return t_st

    jobs = []
    for c in range(88):
        jobs.append((gup[ts(c, 128), :], gscT[ts(c, 128), :],
                     wgu_dd[c], 256, 16, 16))
    for c in range(16):
        for half in range(2):
            jobs.append((dp[ts(c, 128), ts(half, 352)],
                         dscT[ts(c, 128), ts(half, 22)],
                         wd_dd[c][:, half * 2816:(half + 1) * 2816], 352, 22, 22))
    state = {"i": 0, "toks": []}

    def emit_some(n):
        while n > 0 and state["i"] < len(jobs):
            state["toks"].append(chunk(*jobs[state["i"]]))
            state["i"] += 1
            n -= 1
        return state["toks"]

    return emit_some


def emit_block_mlp(p, nc, st, pst, layer, x_src, ctx_src, xn_src, x_dst,
                   wdrams, consts, bar, tag, interleave_cb=None):
    sb = lambda nm, shp, dt: st.enter_context(nc.sbuf_tensor(tag + nm, shp, dt))
    x_s = sb("x_s", [128, NC * 512], F32)
    ctx_s = sb("ctx_s", [128, NC * 512], BF16)
    xn_s = sb("xn_s", [128, NC * 512], BF16)
    g_sb = [sb("g_s0", [128, 512], BF16), sb("g_s1", [128, 512], BF16)]
    u_sb = [sb("u_s0", [128, 512], BF16), sb("u_s1", [128, 512], BF16)]
    wb = [sb("wb0", [128, NC * 512], BF16), sb("wb1", [128, NC * 512], BF16)]
    psA = pst(tag + "psA", [128, 512])
    psB = pst(tag + "psB", [128, 512])
    psGs = [pst(tag + "psG0", [128, 512]), pst(tag + "psG1", [128, 512])]
    psUs = [pst(tag + "psU0", [128, 512]), pst(tag + "psU1", [128, 512])]
    if layer == 0:
        NI = I0
        NIC_ = 64
        wg_d, wu_d, wd_d = wdrams["wg8"], wdrams["wu8"], wdrams["wd8"]
        wout_d = wdrams["woutT"]
        bo_s, sg_s, su_s, sd_s = consts["bo0"], consts["sg"], consts["su"], consts["sd"]
        gu_scale = None
        w8r = [sb(f"w8r{i}", [128, 2048], I8) for i in range(2)]
        w8i = [0]

        def ld8(src_ap, dst_bf, dst_sl, extra):
            sl = w8i[0] % 2
            w8i[0] += 1
            t_l = p.add("sp", lambda e: e.dma_start(out=w8r[sl][:], in_=src_ap),
                        deps=p.use(tag + f"w8r{sl}"), dma=True)
            t_c = p.add("act", lambda e: e.copy(out=dst_bf[:, dst_sl], in_=w8r[sl][:]),
                        deps=[t_l] + extra)
            p.set(tag + f"w8r{sl}", t_c)
            return t_c
    else:
        NI = I1
        NIC_ = 44
        wgu_d, wd_d = wdrams["wgu_dd"], wdrams["wd_dd"]
        wout_d = wdrams["wo1"]
        bo_s = consts["bo1"]
        P16 = 16384.0
        sqb = sb("sqb", [128, 512], BF16)
        sqc = sb("sqc", [128, 512], BF16)
        y_s = sb("y_s", [128, 16 * 512], BF16)
        tmpf = sb("tmpf", [128, 512], F32)
        vrow = sb("vrow", [1, 512], F32)
        srow = sb("srow", [1, 512], F32)
        rrow = sb("rrow", [1, 512], F32)
        ps_row = pst(tag + "ps_row", [128, 512])
        ps_bc = pst(tag + "ps_bc", [128, 512])
    hf = sb("hf", [128, NIC_ * 512], BF16)
    ones_bf, ones_f32 = consts["ones_bf"], consts["ones_f32"]

    out_toks = []
    for tb in range(4):
        csl = ts(tb, 512)
        t_x = p.add("pool", lambda e, csl=csl: e.dma_start(
            out=s3(x_s[:], 512), in_=d3(x_src[:, csl])),
            deps=p.use(tag + "x_s") + bar, dma=True)
        t_ctx = p.add("pool", lambda e, csl=csl: e.dma_start(
            out=s3(ctx_s[:], 512), in_=d3(ctx_src[:, csl])),
            deps=p.use(tag + "ctx_s") + bar, dma=True)
        t_xn = p.add("pool", lambda e, csl=csl: e.dma_start(
            out=s3(xn_s[:], 512), in_=d3(xn_src[:, csl])),
            deps=p.use(tag + "xn_s") + bar, dma=True)

        op_toks = []
        for og in range(4):
            wi = og % 2
            wdep = p.use(tag + f"wb{wi}") + bar
            t_w4 = [p.add("pool", lambda e, og=og, c4=c4, b=wb[wi]: e.dma_start(
                out=b[:, c4 * 2048:(c4 + 1) * 2048], in_=wout_d[4 * og + c4]),
                deps=wdep, dma=True) for c4 in range(4)]
            mms = []
            for oc4 in range(4):
                oc = og * 4 + oc4
                ps = psA if oc % 2 == 0 else psB
                pskey = tag + ("psA" if oc % 2 == 0 else "psB")

                def f_mm(e, oc4=oc4, ps=ps, b=wb[wi]):
                    for dc in range(NC):
                        ins = e.matmul(ps[:], b[:, oc4 * 2048 + dc * 128: oc4 * 2048 + (dc + 1) * 128],
                                       ctx_s[:, ts(dc, 512)],
                                       start=(dc == 0), stop=(dc == NC - 1))
                    return ins

                t_mm = p.add("pe", f_mm, deps=[t_w4[oc4], t_ctx] + p.use(pskey) + bar)
                mms.append(t_mm)
                t_ev = p.add("dve", lambda e, ps=ps, oc=oc: e.scalar_tensor_tensor(
                    out=x_s[:, ts(oc, 512)], in0=ps[:], scalar=bo_s[:, oc:oc + 1],
                    in1=x_s[:, ts(oc, 512)], op0=ALU.add, op1=ALU.add),
                    deps=[t_mm, t_x] + p.use(tag + "consts"))
                p.set(pskey, t_ev)
                op_toks.append(t_ev)
            p.set(tag + f"wb{wi}", mms[-1])

        fuse_toks = []
        ngrp_icg = NIC_ // 4
        for icg in range(ngrp_icg):
            if layer == 0:
                gdep = p.use(tag + "wb0") + bar
                udep = p.use(tag + "wb1") + bar
                t_wg = [p.add("pool", lambda e, icg=icg, c4=c4: e.dma_start(
                    out=wb[0][:, c4 * 2048:(c4 + 1) * 2048], in_=wg_d[4 * icg + c4]),
                    deps=gdep, dma=True) for c4 in range(4)]
                t_wu = [ld8(wu_d[4 * icg + c4], wb[1],
                            slice(c4 * 2048, (c4 + 1) * 2048), udep)
                        for c4 in range(4)]
            else:
                gdep = p.use(tag + "wb0") + bar
                udep = p.use(tag + "wb1") + bar
                t_wg = [p.add("pool", lambda e, icg=icg, c4=c4: e.dma_start(
                    out=wb[0][:, c4 * 2048:(c4 + 1) * 2048], in_=wgu_d[4 * icg + c4]),
                    deps=gdep, dma=True) for c4 in range(4)]
                t_wu = [p.add("pool", lambda e, icg=icg, c4=c4: e.dma_start(
                    out=wb[1][:, c4 * 2048:(c4 + 1) * 2048], in_=wgu_d[44 + 4 * icg + c4]),
                    deps=udep, dma=True) for c4 in range(4)]
            if interleave_cb is not None:
                interleave_cb(-1)
            mmg = []
            mmu = []
            for j4 in range(4):
                ic = icg * 4 + j4
                pi = ic % 2
                psG = psGs[pi]
                psU = psUs[pi]
                g_s = g_sb[pi]
                u_s = u_sb[pi]

                def wsl(j4, dc):
                    return slice(j4 * 2048 + dc * 128, j4 * 2048 + (dc + 1) * 128)

                def f_g(e, j4=j4, psG=psG):
                    for dc in range(NC):
                        ins = e.matmul(psG[:], wb[0][:, wsl(j4, dc)],
                                       xn_s[:, ts(dc, 512)],
                                       start=(dc == 0), stop=(dc == NC - 1))
                    return ins

                def f_u(e, j4=j4, psU=psU):
                    for dc in range(NC):
                        ins = e.matmul(psU[:], wb[1][:, wsl(j4, dc)],
                                       xn_s[:, ts(dc, 512)],
                                       start=(dc == 0), stop=(dc == NC - 1))
                    return ins

                t_g = p.add("pe", f_g, deps=[t_wg[j4], t_xn] + p.use(tag + f"psG{pi}") + bar)
                t_u = p.add("pe", f_u, deps=[t_wu[j4], t_xn] + p.use(tag + f"psU{pi}"))
                mmg.append(t_g)
                mmu.append(t_u)
                if layer == 0:
                    t_sil = p.add("act", lambda e, ic=ic, psG=psG, g_s=g_s: e.activation(
                        out=g_s[:], in_=psG[:], func=AF.Silu, bias=0.0,
                        scale=sg_s[:, ic:ic + 1]), deps=[t_g] + p.use(tag + f"gsbuf{pi}"))
                    t_ue = p.add("dve", lambda e, ic=ic, psU=psU, u_s=u_s: e.tensor_scalar(
                        out=u_s[:], in0=psU[:], scalar1=su_s[:, ic:ic + 1],
                        scalar2=None, op0=ALU.mult), deps=[t_u] + p.use(tag + f"usbuf{pi}"))
                else:
                    t_sil = p.add("act", lambda e, psG=psG, g_s=g_s: e.activation(
                        out=g_s[:], in_=psG[:], func=AF.Silu, bias=0.0, scale=P16),
                        deps=[t_g] + p.use(tag + f"gsbuf{pi}"))
                    t_ue = p.add("dve", lambda e, psU=psU, u_s=u_s: e.tensor_scalar(
                        out=u_s[:], in0=psU[:], scalar1=P16, scalar2=None,
                        op0=ALU.mult), deps=[t_u] + p.use(tag + f"usbuf{pi}"))
                p.set(tag + f"psG{pi}", t_sil)
                p.set(tag + f"psU{pi}", t_ue)
                t_f = p.add("dve", lambda e, ic=ic, g_s=g_s, u_s=u_s: e.tensor_tensor(
                    out=hf[:, ts(ic, 512)], in0=g_s[:], in1=u_s[:], op=ALU.mult),
                    deps=[t_sil, t_ue] + p.use(tag + "hfrd"))
                p.set(tag + f"gsbuf{pi}", t_f)
                p.set(tag + f"usbuf{pi}", t_f)
                fuse_toks.append(t_f)
            p.set(tag + "wb0", mmg[-1])
            p.set(tag + "wb1", mmu[-1])

        dn_toks = []
        icq = NIC_ // 4
        for oc in range(16):
            wbi = oc % 2
            ddep = p.use(tag + f"wb{wbi}") + fuse_toks[-1:]
            if layer == 0:
                t_wd4 = [ld8(wd_d[oc][:, c4 * 2048:(c4 + 1) * 2048],
                             wb[wbi], slice(c4 * 2048, (c4 + 1) * 2048), ddep)
                         if c4 % 2 else
                         p.add("pool", lambda e, oc=oc, wbi=wbi, c4=c4: e.dma_start(
                             out=wb[wbi][:, c4 * 2048:(c4 + 1) * 2048],
                             in_=wd_d[oc][:, c4 * 2048:(c4 + 1) * 2048]),
                             deps=ddep, dma=True)
                         for c4 in range(4)]
            else:
                t_wd4 = [p.add("pool", lambda e, oc=oc, wbi=wbi, c4=c4: e.dma_start(
                    out=wb[wbi][:, c4 * icq * 128:(c4 + 1) * icq * 128],
                    in_=wd_d[oc][:, c4 * icq * 128:(c4 + 1) * icq * 128]),
                    deps=ddep, dma=True) for c4 in range(4)]
            ps = psGs[0] if wbi == 0 else psUs[0]
            pskey = tag + ("psG0" if wbi == 0 else "psU0")

            t_d = None
            for c4 in range(4):
                def f_d(e, ps=ps, wbi=wbi, c4=c4):
                    for ic in range(c4 * icq, (c4 + 1) * icq):
                        ins = e.matmul(ps[:], wb[wbi][:, ts(ic, 128)], hf[:, ts(ic, 512)],
                                       start=(ic == 0), stop=(ic == NIC_ - 1))
                    return ins

                dsub = [t_wd4[c4]] + ([t_d] if t_d is not None
                                      else p.use(pskey) + fuse_toks)
                t_d = p.add("pe", f_d, deps=dsub)
            p.set(tag + f"wb{wbi}", t_d)
            if layer == 0:
                t_dev = p.add("dve", lambda e, oc=oc, ps=ps: e.scalar_tensor_tensor(
                    out=x_s[:, ts(oc, 512)], in0=ps[:], scalar=sd_s[:, oc:oc + 1],
                    in1=x_s[:, ts(oc, 512)], op0=ALU.mult, op1=ALU.add),
                    deps=[t_d] + op_toks[-1:])
            else:
                t_dev = p.add("dve", lambda e, oc=oc, ps=ps: e.tensor_scalar(
                    out=y_s[:, ts(oc, 512)], in0=ps[:], scalar1=P16, scalar2=None,
                    op0=ALU.mult), deps=[t_d] + p.use(tag + "y_rd"))
            p.set(pskey, t_dev)
            dn_toks.append(t_dev)
        p.set(tag + "hfrd", dn_toks[-1])

        if layer == 1:
            sq_toks = []
            sqbufs = [sqb, sqc]
            for oc in range(16):
                t_sq = p.add("act", lambda e, oc=oc, b=sqbufs[oc % 2]: e.activation(
                    out=b[:], in_=y_s[:, ts(oc, 512)], func=AF.Square, bias=0.0,
                    scale=1.0), deps=[dn_toks[oc]] + p.use(tag + f"sqb{oc % 2}"))
                sq_toks.append(t_sq)

            def f_rm(e):
                for oc in range(16):
                    ins = e.matmul(ps_row[0:1, :], ones_bf[:, 0:1],
                                   sqbufs[oc % 2][:], start=(oc == 0), stop=(oc == 15))
                return ins

            t_rm = p.add("pe", f_rm, deps=sq_toks + p.use(tag + "psrow"))
            p.set(tag + "sqb0", t_rm)
            p.set(tag + "sqb1", t_rm)
            t_vr = p.add("dve", lambda e: e.tensor_scalar(
                out=vrow[:], in0=ps_row[0:1, :], scalar1=1.0 / D, scalar2=RMS_EPS,
                op0=ALU.mult, op1=ALU.add), deps=[t_rm])
            p.set(tag + "psrow", t_vr)
            t_sr = p.add("act", lambda e: e.activation(out=srow[:], in_=vrow[:],
                                                       func=AF.Sqrt), deps=[t_vr])
            t_rc = p.add("dve", lambda e: e.reciprocal(out=rrow[:], in_=srow[:]),
                         deps=[t_sr])
            t_bc = p.add("pe", lambda e: e.matmul(ps_bc[:], ones_f32[0:1, :], rrow[:],
                                                  start=True, stop=True),
                         deps=[t_rc] + p.use(tag + "psbc"))

            def f_fin(e):
                for oc in range(16):
                    e.tensor_tensor(out=tmpf[:], in0=y_s[:, ts(oc, 512)], in1=ps_bc[:],
                                    op=ALU.mult)
                    ins = e.tensor_tensor(out=x_s[:, ts(oc, 512)], in0=x_s[:, ts(oc, 512)],
                                          in1=tmpf[:], op=ALU.add)
                return ins

            t_fin = p.add("dve", f_fin, deps=[t_bc] + dn_toks + op_toks)
            p.set(tag + "psbc", t_fin)
            p.set(tag + "y_rd", t_fin)
            last = [t_fin]
        else:
            last = dn_toks

        t_stx = p.add("sp", lambda e, csl=csl: e.dma_start(
            out=d3(x_dst[:, csl]), in_=s3(x_s[:], 512)), deps=last + op_toks, dma=True)
        p.set(tag + "x_s", t_stx)
        p.set(tag + "ctx_s", t_stx)
        p.set(tag + "xn_s", t_stx)
        out_toks.append(t_stx)
        if interleave_cb is not None:
            interleave_cb(tb)
    return out_toks


def build_fused(stop_after=None, no_dq=False):
    nc = bass.Bass("TRN2", target_bir_lowering=False)
    di = lambda nm, shp, dt: nc.dram_tensor(nm, shp, dt, kind="ExternalInput").ap()
    dint = lambda nm, shp, dt: nc.dram_tensor(nm, shp, dt).ap()
    x_fm = di("x_fm", [D, S], F32)
    wqkvT = di("wqkvT", [12, 128, NC * 512], F32)
    bq_t = di("bq_t", [128, 48], F32)
    bv_row = di("bv_row", [1, D], F32)
    woutT = di("woutT", [16, 128, 2048], F32)
    bout_t = di("bout_t", [128, 16], F32)
    wg8 = di("wg8", [64, 128, 2048], I8)
    wu8 = di("wu8", [64, 128, 2048], I8)
    wd8 = di("wd8", [16, 128, I0], I8)
    sg_t = di("sg_t", [128, 64], F32)
    su_t = di("su_t", [128, 64], F32)
    sd_t = di("sd_t", [128, 16], F32)
    wq1 = di("wq1", [4, 128, NC * 512], F32)
    wk1 = di("wk1", [4, 128, NC * 512], F32)
    wv1 = di("wv1", [4, 128, NC * 512], F32)
    wo1 = di("wo1", [16, 128, 2048], F32)
    bo_t = di("bo_t", [128, 16], F32)
    gup = di("gup", [11264, 256], I32)
    gscT = di("gscT", [11264, 16], F16)
    dp = di("dp", [D, 704], I32)
    dscT = di("dscT", [D, 44], F16)
    a0_in = di("a0", [H, S], F16)
    w0_in = di("w0", [128, H * 16], F32)
    c0_in = di("c0", [128, H * 16], F32)
    a1_in = di("a1", [H, S], F16)
    w1_in = di("w1", [128, H * 16], F32)
    c1_in = di("c1", [128, H * 16], F32)
    mask_in = di("mask_in", [128, 128], F32)
    ident_in = di("ident_in", [128, 128], BF16)
    ones_bf_in = di("ones_bf", [128, 1], BF16)
    ones_f32_in = di("ones_f32", [1, 128], F32)
    ones_f16_in = di("ones_f16", [1, 128], F16)
    x2_fm = nc.dram_tensor("x2_fm", [D, S], F32, kind="ExternalOutput").ap()

    q0_d = dint("q0_d", [D, S], BF16)
    k0_d = dint("k0_d", [D, S], BF16)
    v0_d = dint("v0_d", [S, D], BF16)
    xn0_d = dint("xn0_d", [D, S], BF16)
    ctx0_d = dint("ctx0_d", [D, S], BF16)
    x1_d = dint("x1_d", [D, S], F32)
    xn1_d = dint("xn1_d", [D, S], BF16)
    q1_d = dint("q1_d", [D, S], BF16)
    k1_d = dint("k1_d", [D, S], BF16)
    v1_d = dint("v1_d", [S, D], BF16)
    ctx1_d = dint("ctx1_d", [D, S], BF16)
    wgu_dd = dint("wgu_dd", [88, 128, 2048], BF16)
    wd_dd = dint("wd_dd", [16, 128, I1], BF16)

    p = P()
    with ExitStack() as stack:
        sb = lambda nm, shp, dt: stack.enter_context(nc.sbuf_tensor(nm, shp, dt))
        w0_s = sb("w0_s", [128, H * 16], F32)
        c0_s = sb("c0_s", [128, H * 16], F32)
        w1_s = sb("w1_s", [128, H * 16], F32)
        c1_s = sb("c1_s", [128, H * 16], F32)
        mask_s = sb("mask_s", [128, 128], F32)
        ident = sb("ident", [128, 128], BF16)
        ones_bf = sb("ones_bf_s", [128, 1], BF16)
        ones_f32 = sb("ones_f32_s", [1, 128], F32)
        ones16 = sb("ones16_s", [1, 128], F16)
        bt = sb("bt", [128, 48], F32)
        bscl = sb("bscl", [128, 16], F32)
        bv32 = sb("bv32", [1, D], F32)
        bv16 = sb("bv16", [1, D], F16)
        bo0_s = sb("bo0_s", [128, 16], F32)
        bo1_s = sb("bo1_s", [128, 16], F32)
        sg_s = sb("sg_s", [128, 64], F32)
        su_s = sb("su_s", [128, 64], F32)
        sd_s = sb("sd_s", [128, 16], F32)

        cdeps = []
        for dst, srcd in ((w0_s, w0_in), (c0_s, c0_in),
                          (w1_s, w1_in), (c1_s, c1_in),
                          (mask_s, mask_in), (ident, ident_in), (ones_bf, ones_bf_in),
                          (ones_f32, ones_f32_in), (ones16, ones_f16_in),
                          (bt, bq_t), (bv32, bv_row), (bo0_s, bout_t), (bo1_s, bo_t),
                          (sg_s, sg_t), (su_s, su_t), (sd_s, sd_t)):
            cdeps.append(p.add("sp", lambda e, d=dst, s=srcd: e.dma_start(out=d[:], in_=s),
                               dma=True))
        t_bscl = p.add("dve", lambda e: e.tensor_scalar(
            out=bscl[:], in0=bt[:, 0:16], scalar1=SCALE, scalar2=None, op0=ALU.mult),
            deps=cdeps)
        t_bv16 = p.add("dve", lambda e: e.tensor_copy(out=bv16[:], in_=bv32[:]), deps=cdeps)
        cdeps += [t_bscl, t_bv16]
        consts = {"ones_bf": ones_bf, "ones_f32": ones_f32,
                  "bo0": bo0_s, "bo1": bo1_s, "sg": sg_s, "su": su_s, "sd": sd_s}
        wdrams = {"wg8": wg8, "wu8": wu8, "wd8": wd8, "woutT": woutT, "wo1": wo1,
                  "wgu_dd": wgu_dd, "wd_dd": wd_dd}

        def stage_qkv(st, pst, xn_s, wqkv_mode, q_dst, k_dst, v_dst, bar):
            sb2 = lambda nm, shp, dt: st.enter_context(nc.sbuf_tensor(wqkv_mode + nm, shp, dt))
            wb = [sb2("wbA", [128, NC * 512], BF16), sb2("wbB", [128, NC * 512], BF16)]
            evp = [sb2("evA", [128, 512], BF16), sb2("evB", [128, 512], BF16)]
            evv = [sb2("evVA", [128, 512], BF16), sb2("evVB", [128, 512], BF16)]
            psA = pst(wqkv_mode + "psA", [128, 512])
            psB = pst(wqkv_mode + "psB", [128, 512])
            psC = pst(wqkv_mode + "psC", [128, 512])
            psVA = pst(wqkv_mode + "psVA", [128, 512])
            psVB = pst(wqkv_mode + "psVB", [128, 512])
            xt = [None]

            if wqkv_mode == "A":
                def route_qk(orow):
                    if orow < D:
                        def f(e, ps, ev, orow):
                            return e.activation(out=ev[:], in_=ps[:], func=AF.Identity,
                                                bias=bscl[:, orow // 128: orow // 128 + 1],
                                                scale=SCALE)
                        return f, q_dst
                    def f(e, ps, ev, orow):
                        return e.activation(out=ev[:], in_=ps[:], func=AF.Identity,
                                            bias=bt[:, orow // 128: orow // 128 + 1],
                                            scale=1.0)
                    return f, k_dst

                ev_qk = mk_store_evac(p, evp, [wqkv_mode + "evA", wqkv_mode + "evB"], route_qk)
                st_qk = emit_proj(p, wqkvT, 2 * D, xn_s, S, bar + cdeps, wb,
                                  [wqkv_mode + "wbA", wqkv_mode + "wbB"],
                                  [psA, psB, psC],
                                  [wqkv_mode + "psA", wqkv_mode + "psB", wqkv_mode + "psC"],
                                  ev_qk)
                ev_v = mk_store_evac_tm(p, evv, [wqkv_mode + "evVA", wqkv_mode + "evVB"], v_dst)
                st_v = emit_proj_tm(p, wqkvT, D, 2 * D, xn_s, S, bar + cdeps, wb,
                                    [wqkv_mode + "wbA", wqkv_mode + "wbB"],
                                    [psVA, psVB], [wqkv_mode + "psVA", wqkv_mode + "psVB"],
                                    ev_v, bias_rhs=bv16, ones16=ones16)
                return st_qk + st_v
            else:
                def route_q(orow):
                    def f(e, ps, ev, orow):
                        return e.activation(out=ev[:], in_=ps[:], func=AF.Copy, bias=0.0,
                                            scale=SCALE)
                    return f, q_dst

                def route_k(orow):
                    def f(e, ps, ev, orow):
                        return e.copy(out=ev[:], in_=ps[:])
                    return f, k_dst

                ev_q = mk_store_evac(p, evp, [wqkv_mode + "evA", wqkv_mode + "evB"], route_q)
                st_q = emit_proj(p, wq1, D, xn_s, S, bar + cdeps, wb,
                                 [wqkv_mode + "wbA", wqkv_mode + "wbB"],
                                 [psA, psB, psC],
                                 [wqkv_mode + "psA", wqkv_mode + "psB", wqkv_mode + "psC"],
                                 ev_q)
                ev_k = mk_store_evac(p, evp, [wqkv_mode + "evA", wqkv_mode + "evB"], route_k)
                st_k = emit_proj(p, wk1, D, xn_s, S, bar + cdeps, wb,
                                 [wqkv_mode + "wbA", wqkv_mode + "wbB"],
                                 [psA, psB, psC],
                                 [wqkv_mode + "psA", wqkv_mode + "psB", wqkv_mode + "psC"],
                                 ev_k)
                ev_v = mk_store_evac_tm(p, evv, [wqkv_mode + "evVA", wqkv_mode + "evVB"], v_dst)
                st_v = emit_proj_tm(p, wv1, D, 0, xn_s, S, bar + cdeps, wb,
                                    [wqkv_mode + "wbA", wqkv_mode + "wbB"],
                                    [psVA, psVB], [wqkv_mode + "psVA", wqkv_mode + "psVB"],
                                    ev_v)
                return st_q + st_k + st_v

        with ExitStack() as stDQ:
            dq_emit0 = emit_dq_make(p, nc, stDQ, gup, gscT, dp, dscT, wgu_dd, wd_dd,
                                   ident, cdeps)
            dq_emit = (lambda n: []) if no_dq else dq_emit0

            with ExitStack() as stA:
                pstA = lambda nm, shp, dt=F32: stA.enter_context(nc.psum_tensor(nm, shp, dt))
                xn_s = stA.enter_context(nc.sbuf_tensor("xn0_s", [128, NC * S], BF16))
                ln_toks = emit_ln_stream(p, nc, stA, x_fm, xn_s, pstA, ones_bf, ones_f32,
                                         cdeps, "A")
                st_xn0 = p.add("sp", lambda e: e.dma_start(
                    out=d3(xn0_d), in_=s3(xn_s[:], S)), deps=ln_toks, dma=True)
                qkv_toks = stage_qkv(stA, pstA, xn_s, "A", q0_d, k0_d, v0_d, ln_toks)
                barA = p.frontier()
                dq_emit(8)

            with ExitStack() as stB:
                pstB = lambda nm, shp, dt=F32: stB.enter_context(nc.psum_tensor(nm, shp, dt))
                b_toks = emit_attn_stage(p, nc, stB, pstB, q0_d, k0_d, v0_d, ctx0_d,
                                         a0_in, w0_s, c0_s, mask_s, ident, ones16, barA,
                                         "B", interleave_cb=lambda h: dq_emit(2),
                                         skew=ATTN_SKEW)
                barB = p.frontier()
                if stop_after == "B":
                    t_dbg = p.add("pool", lambda e: e.dma_start(
                        out=x2_fm, in_=ctx0_d), deps=barB, dma=True)
                    p.emit(nc, stack, [t_dbg] + p.frontier())
                    return nc

            with ExitStack() as stC:
                pstC = lambda nm, shp, dt=F32: stC.enter_context(nc.psum_tensor(nm, shp, dt))
                c_toks = emit_block_mlp(p, nc, stC, pstC, 0, x_fm, ctx0_d, xn0_d, x1_d,
                                        wdrams, consts, barB, "C",
                                        interleave_cb=lambda tb: dq_emit(2))
                dq_toks = dq_emit(200)
                barC = p.frontier()
                if stop_after == "C":
                    t_dbg = p.add("pool", lambda e: e.dma_start(
                        out=x2_fm, in_=x1_d), deps=barC, dma=True)
                    p.emit(nc, stack, [t_dbg] + p.frontier())
                    return nc

        with ExitStack() as stD:
            pstD = lambda nm, shp, dt=F32: stD.enter_context(nc.psum_tensor(nm, shp, dt))
            xn1_s = stD.enter_context(nc.sbuf_tensor("xn1_s", [128, NC * S], BF16))
            ln1_toks = emit_ln_stream(p, nc, stD, x1_d, xn1_s, pstD, ones_bf, ones_f32, barC, "D")
            st_xn1 = p.add("sp", lambda e: e.dma_start(
                out=d3(xn1_d), in_=s3(xn1_s[:], S)), deps=ln1_toks, dma=True)
            qkv1_toks = stage_qkv(stD, pstD, xn1_s, "D", q1_d, k1_d, v1_d, ln1_toks)
            barD = p.frontier()
            if stop_after == "D":
                t_dbg = p.add("pool", lambda e: e.dma_start(
                    out=x2_fm, in_=x1_d), deps=barD, dma=True)
                p.emit(nc, stack, [t_dbg] + p.frontier())
                return nc

        with ExitStack() as stE:
            pstE = lambda nm, shp, dt=F32: stE.enter_context(nc.psum_tensor(nm, shp, dt))
            e_toks = emit_attn_stage(p, nc, stE, pstE, q1_d, k1_d, v1_d, ctx1_d,
                                     a1_in, w1_s, c1_s, mask_s, ident, ones16, barD, "E",
                                     skew=ATTN_SKEW)
            barE = p.frontier()
            if stop_after == "E":
                t_dbg = p.add("pool", lambda e: e.dma_start(
                    out=x2_fm, in_=x1_d), deps=barE, dma=True)
                p.emit(nc, stack, [t_dbg] + p.frontier())
                return nc

        with ExitStack() as stF:
            pstF = lambda nm, shp, dt=F32: stF.enter_context(nc.psum_tensor(nm, shp, dt))
            f_toks = emit_block_mlp(p, nc, stF, pstF, 1, x1_d, ctx1_d, xn1_d, x2_fm,
                                    wdrams, consts, barE, "F")

            p.emit(nc, stack, f_toks + p.frontier())
    return nc


ATTN_SKEW = True
ATTN_SKEW_MODE = 1
FUSED = True


def kernel_fused(inp):
    f32 = np.float32
    C = host_consts()
    X = np.asarray(inp["L0_attn_x"], dtype=f32)
    ncF = _get("fused", build_fused)

    def at16(Aarr, warr, carr):
        a_arr = np.ascontiguousarray(Aarr.reshape(H, S))
        w_arr = np.ascontiguousarray(
            warr.reshape(H, 16, 128).transpose(2, 0, 1).reshape(128, H * 16))
        c_arr = np.ascontiguousarray(
            carr.reshape(H, 16, 128).transpose(2, 0, 1).reshape(128, H * 16))
        return a_arr, w_arr, c_arr

    a0, w0, c0 = at16(C["A0"], C["w0"], C["c0"])
    a1, w1, c1 = at16(C["A1"], C["w1"], C["c1"])

    def rg(w, gw=512):
        din, dout = w.shape
        r = w.reshape(din // 128, 128, dout // gw, gw).transpose(2, 1, 0, 3)
        return np.ascontiguousarray(r.reshape(dout // gw, 128, (din // 128) * gw))

    wqkvT = np.asarray(inp["L0_attn_Wqkv"], f32).T
    b_qkv = np.asarray(inp["L0_attn_b_qkv"], f32)
    common = {
        "wqkvT": rg(wqkvT), "bq_t": col_t(b_qkv, 48),
        "bv_row": np.ascontiguousarray(b_qkv[2 * D:].reshape(1, D)),
        "woutT": rg(np.asarray(inp["L0_attn_Wout"], f32).T, 128),
        "bout_t": col_t(np.asarray(inp["L0_attn_b_out"], f32), 16),
        "wg8": rg(np.asarray(inp["L0_ffn_w_gate"]), 128),
        "wu8": rg(np.asarray(inp["L0_ffn_w_up"]), 128),
        "wd8": rg(np.asarray(inp["L0_ffn_w_down"]), 128),
        "sg_t": col_t(np.asarray(inp["L0_ffn_s_gate"], f32), 64),
        "su_t": col_t(np.asarray(inp["L0_ffn_s_up"], f32), 64),
        "sd_t": col_t(np.asarray(inp["L0_ffn_s_down"], f32), 16),
        "wq1": rg(np.asarray(inp["L1_attn_Wq"], f32)),
        "wk1": rg(np.asarray(inp["L1_attn_Wk"], f32)),
        "wv1": rg(np.asarray(inp["L1_attn_Wv"], f32)),
        "wo1": rg(np.asarray(inp["L1_attn_Wo"], f32), 128),
        "bo_t": col_t(np.asarray(inp["L1_attn_bo"], f32), 16),
        "gup": np.ascontiguousarray(np.asarray(inp["L1_ffn_gate_up_packed"])),
        "gscT": np.ascontiguousarray(np.asarray(inp["L1_ffn_gate_up_scales"]).T),
        "dp": np.ascontiguousarray(np.asarray(inp["L1_ffn_down_packed"])),
        "dscT": np.ascontiguousarray(np.asarray(inp["L1_ffn_down_scales"]).T),
        "a0": a0, "w0": w0, "c0": c0, "a1": a1, "w1": w1, "c1": c1,
        "mask_in": C["mask"], "ident_in": C["ident"],
        "ones_bf": C["ones_bf"], "ones_f32": C["ones_f32"], "ones_f16": C["ones_f16"],
    }
    maps = []
    for c in range(8):
        b = c // 2
        m = dict(common)
        m["x_fm"] = np.ascontiguousarray(X[b].T)
        maps.append(m)
    res = run_bass_kernel_spmd(ncF, maps, core_ids=list(range(8))).results
    out = np.empty((B, S, D), dtype=f32)
    for b in range(B):
        out[b] = res[2 * b]["x2_fm"].T
    return out


if __name__ == "__main__":
    pass
